# version 04
# baseline (speedup 1.0000x reference)
import contextlib
import numpy as np
import concourse.bass as bass
import concourse.mybir as mybir
import concourse.bass_utils as _bass_utils
from concourse.bass_utils import run_bass_kernel_spmd

_orig_run_command = _bass_utils.run_command


def _patched_run_command(cmd, *a, **kw):
    cmd = [("--enable-ldw-opt=true" if c == "--enable-ldw-opt=false" else c)
           for c in cmd]
    return _orig_run_command(cmd, *a, **kw)


_bass_utils.run_command = _patched_run_command

F32 = mybir.dt.float32
BF16 = mybir.dt.bfloat16
I32 = mybir.dt.int32
U32 = mybir.dt.uint32
AF = mybir.ActivationFunctionType
ALU = mybir.AluOpType
AX = mybir.AxisListType

B_CORE = 2
T = 2048
D = 1024
NTOK = B_CORE * T
NCORES = 8
DC = D // 128

NEGBIG = -1.0e30

L0FFN_GROUPS = [[0, 1, 2, 3, 4, 5, 6, 7]]
L0FFN_THRESH = 45875.2


class Prog:
    ENGINES = ("tensor", "vector", "scalar", "gpsimd", "sync")
    SELF_SERIAL = {"vector": "dve", "scalar": "act", "gpsimd": "pool"}

    def __init__(self, nc, stk):
        self.nc = nc
        self.stk = stk
        self.ops = {e: [] for e in self.ENGINES}
        self._sems = {}
        self.nblocks = 0

    def sem(self, name):
        if name not in self._sems:
            h = self.stk.enter_context(self.nc.semaphore(name))
            self._sems[name] = [h, 0]
        return self._sems[name]

    def count(self, name):
        return self.sem(name)[1]

    def emit(self, engine, fn, wait=(), dma=None, pe_tok=False):
        waits = [w for w in wait if w is not None]
        tok = None
        selfwait = None
        if dma is not None:
            s = self.sem(dma)
            s[1] += 16
            tok = (dma, s[1])
            semname, inc = dma, 16
        elif engine in self.SELF_SERIAL:
            semname = self.SELF_SERIAL[engine]
            s = self.sem(semname)
            if s[1] > 0:
                selfwait = (semname, s[1])
            s[1] += 1
            tok = (semname, s[1])
            inc = 1
        elif engine == "tensor" and pe_tok:
            s = self.sem("pe")
            s[1] += 1
            tok = ("pe", s[1])
            semname, inc = "pe", 1
        else:
            semname, inc = None, 0

        def closure(eng, fn=fn, waits=waits, selfwait=selfwait,
                    semname=semname, inc=inc):
            for (wn, wt) in waits:
                eng.wait_ge(self._sems[wn][0], wt)
            if selfwait is not None:
                eng.wait_ge(self._sems[selfwait[0]][0], selfwait[1])
            instr = fn(eng)
            if semname is not None and instr is not None:
                instr.then_inc(self._sems[semname][0], inc)

        self.ops[engine].append(closure)
        return tok

    def wait_only(self, engine, wait):
        waits = [w for w in wait if w is not None]

        def closure(eng, waits=waits):
            for (wn, wt) in waits:
                eng.wait_ge(self._sems[wn][0], wt)

        self.ops[engine].append(closure)

    def setup_consts(self, values):
        t = self.stk.enter_context(
            self.nc.sbuf_tensor("constpool", [128, max(1, len(values))], F32))
        for i, v in enumerate(values):
            self.emit("gpsimd", lambda g, i=i, v=v: g.memset(t[:, i:i + 1], v))
            self.nc.const_aps.register(F32, v, t[:, i:i + 1])

    def barrier(self):
        nonempty = [e for e in self.ENGINES if self.ops[e]]
        if not nonempty:
            return
        self.nblocks += 1
        with self.nc.Block(f"blk{self.nblocks}") as blk:
            for e in self.ENGINES:
                oplist = self.ops[e]
                if not oplist:
                    continue

                def body(eng, oplist=oplist):
                    for fnc in oplist:
                        fnc(eng)

                getattr(blk, e)(body)
        self.ops = {e: [] for e in self.ENGINES}


def emit_identity(P, ident, idraw):
    t = P.emit("gpsimd", lambda g: g.iota(idraw[:, :], [[1, 128]], base=0,
                                          channel_multiplier=-1))
    P.emit("vector", lambda v: v.tensor_scalar(ident[:, :], idraw[:, :], 0, None,
                                               ALU.is_equal), wait=[t])


def emit_ln_tile(P, xin, row0, xt, ldtok, sb, xn_out, eps=1e-5, out_dtype_bf=True,
                 gain=None, bias_v=None, out_wait=()):
    stats, mv, rstd = sb["stats"], sb["mv"], sb["rstd"]
    P.emit("vector", lambda v: v.bn_stats(stats[:, 0:6], xt[:, 0:512]), wait=[ldtok])
    P.emit("vector", lambda v: v.bn_stats(stats[:, 6:12], xt[:, 512:1024]))
    ta = P.emit("vector", lambda v: v.bn_aggr(mv[:, :], stats[:, :]))
    ts = P.emit("scalar", lambda s: s.activation(rstd[:, :], mv[:, 1:2], AF.Ln,
                                                 bias=eps), wait=[ta])
    tsx = P.emit("scalar", lambda s: s.activation(rstd[:, :], rstd[:, :], AF.Exp,
                                                  scale=-0.5))
    P.wait_only("vector", [tsx])
    if gain is None:
        tk = P.emit("vector", lambda v: v.tensor_scalar(
            xn_out, xt[:, :], mv[:, 0:1], rstd[:, :], ALU.subtract, ALU.mult),
            wait=list(out_wait))
        return tk
    tmp = sb["lntmp"]
    P.emit("vector", lambda v: v.tensor_scalar(
        tmp[:, :], xt[:, :], mv[:, 0:1], rstd[:, :], ALU.subtract, ALU.mult))
    P.emit("vector", lambda v: v.tensor_mul(tmp[:, :], tmp[:, :], gain))
    tk = P.emit("vector", lambda v: v.tensor_add(xn_out, tmp[:, :], bias_v),
                wait=list(out_wait))
    return tk


def emit_ln_to_fm(P, xin, xn_fm, b, sb, ps, tiles=16, eps=1e-5, tok0=None):
    if tok0 is None:
        tok0 = b * T
    trtok = {0: None, 1: None}
    cptok = {0: None, 1: None}
    prev_txn = {0: None, 1: None}
    for i in range(tiles):
        slot = i % 2
        xt = sb[f"xt{slot}"]
        row0 = tok0 + i * 128
        ld = P.emit("sync", lambda s, xt=xt, row0=row0: s.dma_start(
            xt[:, :], xin[row0:row0 + 128, :]), dma=f"dma_x{slot}",
            wait=[prev_txn[slot]])
        txn = emit_ln_tile(P, xin, row0, xt, ld, sb, sb["xntm"][:, :], eps=eps)
        prev_txn[slot] = txn
        for c in range(DC):
            pslot = c % 2
            ptr = ps[f"tr{pslot}"]
            ttr = P.emit("tensor", lambda t, c=c, ptr=ptr: t.transpose(
                ptr[:, :], sb["xntm"][:, c * 128:(c + 1) * 128], sb["ident"][:, :]),
                pe_tok=True, wait=[txn, cptok[pslot]])
            col0 = i * 128
            eng = "scalar" if pslot == 0 else "vector"
            cptok[pslot] = P.emit(eng, lambda e, c=c, ptr=ptr, col0=col0, eng=eng: (
                e.activation(xn_fm[:, c, col0:col0 + 128], ptr[:, :], AF.Identity)
                if eng == "scalar" else
                e.tensor_copy(xn_fm[:, c, col0:col0 + 128], ptr[:, :])),
                wait=[ttr])
    return [cptok[0], cptok[1]]


def load_weight_chunks(P, dst_ap, src_ap, name):
    src = src_ap.rearrange("(c p) o -> p c o", p=128)
    return P.emit("gpsimd", lambda g: g.dma_start(dst_ap, src), dma=name)


def emit_l2_attn(P, xin, xout, W, name="l2a"):
    nc = P.nc
    with contextlib.ExitStack() as pstk:
        sb_ = lambda n, s, d: pstk.enter_context(nc.sbuf_tensor(f"{name}_{n}", s, d))
        ps_ = lambda n, s, d: pstk.enter_context(nc.psum_tensor(f"{name}_{n}", s, d))
        xr = [sb_("xr0", [128, D], F32), sb_("xr1", [128, D], F32)]
        sb = {
            "xt0": xr[0], "xt1": xr[1],
            "stats": sb_("stats", [128, 12], F32), "mv": sb_("mv", [128, 2], F32),
            "rstd": sb_("rstd", [128, 1], F32), "xntm": sb_("xntm", [128, D], BF16),
            "ident": sb_("ident", [128, 128], BF16),
        }
        idraw = sb_("idraw", [128, 128], I32)
        ps = {"tr0": ps_("tr0", [128, 1024], BF16).ap()[:, 0:128],
              "tr1": ps_("tr1", [128, 1024], BF16).ap()[:, 0:128]}
        r3 = lambda t, c: t.ap().rearrange("p (c t) -> p c t", c=c)
        xn_fm = r3(sb_("xnfm", [128, DC * T], BF16), DC)
        wq = r3(sb_("wq", [128, DC * D], BF16), DC)
        wk = r3(sb_("wk", [128, DC * D], BF16), DC)
        wv = r3(sb_("wv", [128, DC * D], BF16), DC)
        qf = r3(sb_("qf", [128, DC * T], BF16), DC)
        kf = r3(sb_("kf", [128, DC * T], BF16), DC)
        vt = r3(sb_("vt", [128, 16 * D], BF16), 16)
        ssb = sb_("ssb", [128, T], F32)
        psb = sb_("psb", [128, T], BF16)
        ptt = r3(sb_("ptt", [128, 16 * 128], BF16), 16)
        rmax = sb_("rmax", [128, 1], F32)
        nbias = sb_("nbias", [128, 1], F32)
        rsum = sb_("rsum", [128, 1], F32)
        rrec = sb_("rrec", [128, 1], F32)
        pmm = [ps_("pmm0", [128, 512], F32), ps_("pmm1", [128, 512], F32)]
        pvv = [ps_("pvv0", [128, 512], F32), ps_("pvv1", [128, 512], F32)]

        emit_identity(P, sb["ident"], idraw)
        wtoks = [load_weight_chunks(P, w, W[n], "dma_w")
                 for w, n in ((wq, "l2_WqT"), (wk, "l2_WkT"), (wv, "l2_WvT"))]
        P.wait_only("gpsimd", [wtoks[-1]])
        P.barrier()

        for b in range(B_CORE):
            emit_ln_to_fm(P, xin, xn_fm, b, sb, ps)
            P.barrier()

            evtok = {0: None, 1: None}
            jobs = []
            for oc in range(DC):
                for tg in range(4):
                    jobs.append(("q", oc, tg))
                    jobs.append(("k", oc, tg))
            for tc in range(16):
                for half in range(2):
                    jobs.append(("v", tc, half))
            for jn, (kind, a1, a2) in enumerate(jobs):
                slot = jn % 2
                pm = pmm[slot]
                wl = [evtok[slot]]
                for dc in range(DC):
                    if kind == "q" or kind == "k":
                        wsb = wq if kind == "q" else wk
                        fn = lambda t, pm=pm, wsb=wsb, oc=a1, tg=a2, dc=dc: t.matmul(
                            pm[:, :], wsb[:, dc, oc * 128:(oc + 1) * 128],
                            xn_fm[:, dc, tg * 512:(tg + 1) * 512],
                            start=(dc == 0), stop=(dc == DC - 1))
                    else:
                        fn = lambda t, pm=pm, tc=a1, half=a2, dc=dc: t.matmul(
                            pm[:, :], xn_fm[:, dc, tc * 128:(tc + 1) * 128],
                            wv[:, dc, half * 512:(half + 1) * 512],
                            start=(dc == 0), stop=(dc == DC - 1))
                    tk = P.emit("tensor", fn, pe_tok=(dc == DC - 1),
                                wait=wl if dc == 0 else [])
                if kind == "q":
                    dstap = qf[:, a1, a2 * 512:(a2 + 1) * 512]
                    evtok[slot] = P.emit("scalar", lambda s, dstap=dstap, pm=pm:
                                         s.activation(dstap, pm[:, :], AF.Identity,
                                                      scale=1.0 / 32.0), wait=[tk])
                elif kind == "k":
                    dstap = kf[:, a1, a2 * 512:(a2 + 1) * 512]
                    evtok[slot] = P.emit("vector", lambda v, dstap=dstap, pm=pm:
                                         v.tensor_copy(dstap, pm[:, :]), wait=[tk])
                else:
                    dstap = vt[:, a1, a2 * 512:(a2 + 1) * 512]
                    eng = "scalar" if slot == 0 else "vector"
                    evtok[slot] = P.emit(eng, lambda e, dstap=dstap, pm=pm, eng=eng: (
                        e.activation(dstap, pm[:, :], AF.Identity) if eng == "scalar"
                        else e.tensor_copy(dstap, pm[:, :])), wait=[tk])
            P.barrier()

            ev = {0: None, 1: None}
            pvtok = {0: None, 1: None}
            addtok_hist = {0: None, 1: None}
            sttok = {0: None, 1: None}
            trtok = {0: None, 1: None}
            last_ttr = None
            prev_pv = [None, None]
            for qi in range(16):
                for kg in range(4):
                    slot = kg % 2
                    pm = pmm[slot]
                    wl = [ev[slot]]
                    for dc in range(DC):
                        tk = P.emit("tensor", lambda t, pm=pm, qi=qi, kg=kg, dc=dc:
                                    t.matmul(pm[:, :],
                                             qf[:, dc, qi * 128:(qi + 1) * 128],
                                             kf[:, dc, kg * 512:(kg + 1) * 512],
                                             start=(dc == 0), stop=(dc == DC - 1)),
                                    pe_tok=(dc == DC - 1), wait=wl if dc == 0 else [])
                    ev[slot] = P.emit("vector", lambda v, pm=pm, kg=kg:
                                      v.tensor_copy(ssb[:, kg * 512:(kg + 1) * 512],
                                                    pm[:, :]), wait=[tk])
                P.emit("vector", lambda v: v.reduce_max(rmax[:, :], ssb[:, :],
                                                        axis=AX.X),
                       wait=[ev[0], ev[1]])
                tnb = P.emit("vector", lambda v: v.tensor_scalar(
                    nbias[:, :], rmax[:, :], -1.0, None, ALU.mult))
                tex = P.emit("scalar", lambda s: s.activation(
                    ssb[:, :], ssb[:, :], AF.Exp, bias=nbias[:, :],
                    accum_out=rsum[:, :]), wait=[tnb])
                P.emit("vector", lambda v: v.reciprocal(rrec[:, :], rsum[:, :]),
                       wait=[tex])
                tp = P.emit("vector", lambda v: v.tensor_scalar(
                    psb[:, :], ssb[:, :], rrec[:, :], None, ALU.mult),
                    wait=[last_ttr])
                for kc in range(16):
                    pslot = kc % 2
                    ptr = ps[f"tr{pslot}"]
                    ttr = P.emit("tensor", lambda t, kc=kc, ptr=ptr: t.transpose(
                        ptr[:, :], psb[:, kc * 128:(kc + 1) * 128],
                        sb["ident"][:, :]), pe_tok=True, wait=[tp, trtok[pslot]])
                    eng = "scalar" if pslot == 0 else "vector"
                    trtok[pslot] = P.emit(eng, lambda e, kc=kc, ptr=ptr, eng=eng: (
                        e.activation(ptt[:, kc, :], ptr[:, :], AF.Identity)
                        if eng == "scalar" else e.tensor_copy(ptt[:, kc, :],
                                                              ptr[:, :])),
                        wait=[ttr, prev_pv[0], prev_pv[1]])
                last_ttr = ttr
                qslot = qi % 2
                row0 = b * T + qi * 128
                ldx = P.emit("sync", lambda s, xr_=xr[qslot], row0=row0: s.dma_start(
                    xr_[:, :], xin[row0:row0 + 128, :]), dma=f"dma_xr{qslot}",
                    wait=[sttok[qslot]])
                lastpv = {}
                for half in range(2):
                    pv = pvv[half]
                    wl = [trtok[0], trtok[1], pvtok[half]]
                    for kc in range(16):
                        tk = P.emit("tensor", lambda t, pv=pv, kc=kc, half=half:
                                    t.matmul(pv[:, :], ptt[:, kc, :],
                                             vt[:, kc, half * 512:(half + 1) * 512],
                                             start=(kc == 0), stop=(kc == 15)),
                                    pe_tok=(kc == 15), wait=wl if kc == 0 else [])
                    lastpv[half] = tk
                prev_pv = [lastpv[0], lastpv[1]]
                at = {}
                for half in range(2):
                    at[half] = P.emit("vector", lambda v, pv=pvv[half], half=half,
                                      x_=xr[qslot]: v.tensor_add(
                        x_[:, half * 512:(half + 1) * 512], pv[:, :],
                        x_[:, half * 512:(half + 1) * 512]),
                        wait=[lastpv[half], ldx])
                    pvtok[half] = at[half]
                addtok_hist[qslot] = at[1]
                sttok[qslot] = P.emit("gpsimd", lambda g, o=xr[qslot], row0=row0:
                                      g.dma_start(xout[row0:row0 + 128, :], o[:, :]),
                                      dma=f"dma_o{qslot}", wait=[at[0], at[1]])
            P.wait_only("gpsimd", [sttok[0], sttok[1]])
            P.barrier()


def emit_l0_attn(P, xin, xout, W, name="l0a"):
    nc = P.nc
    with contextlib.ExitStack() as pstk:
        sb_ = lambda n, s, d: pstk.enter_context(nc.sbuf_tensor(f"{name}_{n}", s, d))
        ps_ = lambda n, s, d: pstk.enter_context(nc.psum_tensor(f"{name}_{n}", s, d))
        r3 = lambda t, c: t.ap().rearrange("p (c t) -> p c t", c=c)
        xr = [sb_("xr0", [128, D], F32), sb_("xr1", [128, D], F32)]
        sb = {
            "xt0": xr[0], "xt1": xr[1],
            "stats": sb_("stats", [128, 12], F32), "mv": sb_("mv", [128, 2], F32),
            "rstd": sb_("rstd", [128, 1], F32), "xntm": sb_("xntm", [128, D], BF16),
            "ident": sb_("ident", [128, 128], BF16),
        }
        idraw = sb_("idraw", [128, 128], I32)
        ps = {"tr0": ps_("tr0", [128, 1024], BF16).ap()[:, 0:128],
              "tr1": ps_("tr1", [128, 1024], BF16).ap()[:, 0:128]}
        pmm = [ps_("pmm0", [128, 512], F32), ps_("pmm1", [128, 512], F32)]
        pvv = [ps_("pvv0", [128, 512], F32), ps_("pvv1", [128, 512], F32)]
        xn_fm = r3(sb_("xnfm", [128, DC * T], BF16), DC)
        wqkv = r3(sb_("wqkv", [128, DC * 1536], BF16), DC)
        wqkr = r3(sb_("wqkr", [128, DC * 512], BF16), DC)
        wvr = r3(sb_("wvr", [128, DC * 512], BF16), DC)
        wproj = r3(sb_("wproj", [128, DC * D], BF16), DC)
        qf4 = r3(sb_("qf4", [128, 4 * T], BF16), 4)
        kf4 = r3(sb_("kf4", [128, 4 * T], BF16), 4)
        qr4 = r3(sb_("qr4", [128, 4 * T], BF16), 4)
        kr4 = r3(sb_("kr4", [128, 4 * T], BF16), 4)
        vt = r3(sb_("vt", [128, 16 * D], BF16), 16)
        ssb = sb_("ssb", [128, 1152], F32)
        psb = sb_("psb", [128, 1152], BF16)
        ptt = r3(sb_("ptt", [128, 9 * 128], BF16), 9)
        ofm = r3(sb_("ofm", [128, DC * 128], BF16), DC)
        masks = r3(sb_("masks", [128, 2 * 128], F32), 2)
        rmax = sb_("rmax", [128, 1], F32)
        nbias = sb_("nbias", [128, 1], F32)
        rsum = sb_("rsum", [128, 1], F32)
        rrec = sb_("rrec", [128, 1], F32)

        emit_identity(P, sb["ident"], idraw)
        wtoks = [load_weight_chunks(P, w, W[n], "dma_w")
                 for w, n in ((wqkv, "l0_wqkv"), (wqkr, "l0_wqkr"),
                              (wvr, "l0_wvr"), (wproj, "l0_wproj"))]
        mtok = P.emit("gpsimd", lambda g: g.dma_start(
            masks[:, :, :], W["l0_masks"].rearrange("(m p) k -> p m k", p=128)),
            dma="dma_w")
        P.wait_only("gpsimd", [mtok])
        P.barrier()

        for b in range(B_CORE):
            z1 = P.emit("gpsimd", lambda g: g.memset(qr4[:, :, :], 0.0))
            z2 = P.emit("gpsimd", lambda g: g.memset(kr4[:, :, :], 0.0))
            emit_ln_to_fm(P, xin, xn_fm, b, sb, ps)
            P.barrier()

            evtok = {0: None, 1: None}
            jobs = []
            for tg in range(4):
                for oc in range(4):
                    jobs.append(("qf", oc, tg))
                    jobs.append(("kf", oc, tg))
                for hg in range(4):
                    jobs.append(("qr", hg, tg))
                    jobs.append(("kr", hg, tg))
            for tc in range(16):
                jobs.append(("v", 0, tc))
                jobs.append(("v", 1, tc))
            for jn, (kind, a1, tg) in enumerate(jobs):
                slot = jn % 2
                pm = pmm[slot]
                wl = [evtok[slot]]
                if kind in ("qf", "kf"):
                    col0 = a1 * 128 + (0 if kind == "qf" else 512)
                    for dc in range(DC):
                        tk = P.emit("tensor", lambda t, pm=pm, col0=col0, tg=tg,
                                    dc=dc: t.matmul(
                            pm[:, :], wqkv[:, dc, col0:col0 + 128],
                            xn_fm[:, dc, tg * 512:(tg + 1) * 512],
                            start=(dc == 0), stop=(dc == DC - 1)),
                            pe_tok=(dc == DC - 1), wait=wl if dc == 0 else [])
                    dst = qf4 if kind == "qf" else kf4
                    scl = 0.125 if kind == "qf" else 1.0
                    eng = "scalar" if slot == 0 else "vector"
                    evtok[slot] = P.emit(
                        eng, lambda e, pm=pm, dst=dst, a1=a1, tg=tg, scl=scl,
                        eng=eng: (
                            e.activation(dst[:, a1, tg * 512:(tg + 1) * 512],
                                         pm[:, :], AF.Identity, scale=scl)
                            if eng == "scalar" else
                            e.tensor_scalar(dst[:, a1, tg * 512:(tg + 1) * 512],
                                            pm[:, :], scl, None, ALU.mult)),
                        wait=[tk])
                elif kind in ("qr", "kr"):
                    col0 = a1 * 64 + (0 if kind == "qr" else 256)
                    for dc in range(DC):
                        tk = P.emit("tensor", lambda t, pm=pm, col0=col0, tg=tg,
                                    dc=dc: t.matmul(
                            pm[0:64, :],
                            wqkr[:, dc, col0:col0 + 64],
                            xn_fm[:, dc, tg * 512:(tg + 1) * 512],
                            start=(dc == 0), stop=(dc == DC - 1)),
                            pe_tok=(dc == DC - 1), wait=wl if dc == 0 else [])
                    dst = qr4 if kind == "qr" else kr4
                    scl = 1.0 / np.sqrt(32.0) if kind == "qr" else 1.0
                    toks = []
                    for hh in range(2):
                        h = 2 * a1 + hh
                        rb = (64 * h) % 128
                        hc = h // 2
                        eng = "scalar" if hh == 0 else "vector"
                        toks.append(P.emit(
                            eng, lambda e, pm=pm, dst=dst, hh=hh, rb=rb, hc=hc,
                            tg=tg, scl=scl, eng=eng: (
                                e.activation(
                                    dst[rb:rb + 32, hc, tg * 512:(tg + 1) * 512],
                                    pm[32 * hh:32 * hh + 32, :], AF.Identity,
                                    scale=scl)
                                if eng == "scalar" else
                                e.tensor_scalar(
                                    dst[rb:rb + 32, hc, tg * 512:(tg + 1) * 512],
                                    pm[32 * hh:32 * hh + 32, :], scl, None,
                                    ALU.mult)),
                            wait=[tk, z1, z2]))
                    evtok[slot] = toks[-1]
                else:
                    half, tc = a1, tg
                    for dc in range(DC):
                        src = (wqkv[:, dc, 1024:1536] if half == 0
                               else wvr[:, dc, :])
                        tk = P.emit("tensor", lambda t, pm=pm, src=src, tc=tc,
                                    dc=dc: t.matmul(
                            pm[:, :], xn_fm[:, dc, tc * 128:(tc + 1) * 128],
                            src, start=(dc == 0), stop=(dc == DC - 1)),
                            pe_tok=(dc == DC - 1), wait=wl if dc == 0 else [])
                    eng = "scalar" if slot == 0 else "vector"
                    evtok[slot] = P.emit(
                        eng, lambda e, pm=pm, tc=tc, half=half, eng=eng: (
                            e.activation(vt[:, tc, half * 512:(half + 1) * 512],
                                         pm[:, :], AF.Identity)
                            if eng == "scalar" else
                            e.tensor_copy(vt[:, tc, half * 512:(half + 1) * 512],
                                          pm[:, :])), wait=[tk])
            P.barrier()

            sttok = {0: None, 1: None}
            trtok = {0: None, 1: None}
            oftok = None
            ofbank = {0: None, 1: None}
            last_ttr = None
            prev_pv = None
            ev = {0: None, 1: None}
            prev_proj = {0: None, 1: None}
            for qi in range(16):
                for h in range(16):
                    red = h >= 8
                    hh = h - 8 if red else h
                    qsrc = qr4 if red else qf4
                    ksrc = kr4 if red else kf4
                    wtiles = 2 if red else 8
                    kt0 = max(0, qi - wtiles)
                    ntiles = qi - kt0 + 1
                    w = ntiles * 128
                    rb = (64 * hh) % 128
                    hc = hh // 2
                    vcol = 512 + 64 * hh if red else 64 * hh
                    for r, kt in enumerate(range(kt0, qi + 1)):
                        slot = r % 2
                        pm = pmm[slot]
                        tk = P.emit("tensor", lambda t, pm=pm, qsrc=qsrc, ksrc=ksrc,
                                    rb=rb, hc=hc, qi=qi, kt=kt: t.matmul(
                            pm[0:128, 0:128], qsrc[rb:rb + 64, hc,
                                                   qi * 128:(qi + 1) * 128],
                            ksrc[rb:rb + 64, hc, kt * 128:(kt + 1) * 128],
                            start=True, stop=True),
                            pe_tok=True, wait=[ev[slot]])
                        if kt == qi:
                            mi = 0
                        elif kt == kt0 and qi - wtiles >= 0:
                            mi = 1
                        else:
                            mi = None
                        if mi is None:
                            ev[slot] = P.emit("vector", lambda v, pm=pm, r=r:
                                              v.tensor_copy(
                                ssb[:, r * 128:(r + 1) * 128], pm[0:128, 0:128]),
                                wait=[tk])
                        else:
                            ev[slot] = P.emit("vector", lambda v, pm=pm, r=r, mi=mi:
                                              v.tensor_add(
                                ssb[:, r * 128:(r + 1) * 128], pm[0:128, 0:128],
                                masks[:, mi, :]), wait=[tk])
                    P.emit("vector", lambda v, w=w: v.reduce_max(
                        rmax[:, :], ssb[:, 0:w], axis=AX.X), wait=[ev[0], ev[1]])
                    P.emit("vector", lambda v: v.tensor_scalar(
                        nbias[:, :], rmax[:, :], -1.0, None, ALU.mult))
                    tex = P.emit("scalar", lambda s, w=w: s.activation(
                        ssb[:, 0:w], ssb[:, 0:w], AF.Exp, bias=nbias[:, :],
                        accum_out=rsum[:, :]), wait=[("dve", P.count("dve"))])
                    P.emit("vector", lambda v: v.reciprocal(rrec[:, :], rsum[:, :]),
                           wait=[tex])
                    tp = P.emit("vector", lambda v, w=w: v.tensor_scalar(
                        psb[:, 0:w], ssb[:, 0:w], rrec[:, :], None, ALU.mult),
                        wait=[last_ttr])
                    for r in range(ntiles):
                        pslot = r % 2
                        ptr = ps[f"tr{pslot}"]
                        ttr = P.emit("tensor", lambda t, r=r, ptr=ptr: t.transpose(
                            ptr[:, :], psb[:, r * 128:(r + 1) * 128],
                            sb["ident"][:, :]), pe_tok=True,
                            wait=[tp, trtok[pslot]])
                        eng = "scalar" if pslot == 0 else "vector"
                        trtok[pslot] = P.emit(
                            eng, lambda e, r=r, ptr=ptr, eng=eng: (
                                e.activation(ptt[:, r, :], ptr[:, :], AF.Identity)
                                if eng == "scalar" else
                                e.tensor_copy(ptt[:, r, :], ptr[:, :])),
                            wait=[ttr, prev_pv])
                    last_ttr = ttr
                    pv = pvv[h % 2]
                    for r, kt in enumerate(range(kt0, qi + 1)):
                        tk = P.emit("tensor", lambda t, pv=pv, kt=kt, r=r,
                                    vcol=vcol: t.matmul(
                            pv[0:64, 0:128], vt[:, kt, vcol:vcol + 64],
                            ptt[:, r, :], start=(r == 0), stop=(r == ntiles - 1)),
                            pe_tok=(r == ntiles - 1),
                            wait=[trtok[0], trtok[1], ofbank[h % 2]]
                            if r == 0 else [])
                    prev_pv = tk
                    eng = "scalar" if h % 2 == 0 else "vector"
                    oftok = P.emit(eng, lambda e, pv=pv, rb=rb_out(h), hc=oc_out(h),
                                   eng=eng: (
                        e.activation(ofm[rb:rb + 64, hc, :], pv[0:64, 0:128],
                                     AF.Identity) if eng == "scalar" else
                        e.tensor_copy(ofm[rb:rb + 64, hc, :], pv[0:64, 0:128])),
                        wait=[tk])
                    ofbank[h % 2] = oftok
                qslot = qi % 2
                row0 = b * T + qi * 128
                ldx = P.emit("sync", lambda s, x_=xr[qslot], row0=row0: s.dma_start(
                    x_[:, :], xin[row0:row0 + 128, :]), dma=f"dma_xr{qslot}",
                    wait=[sttok[qslot]])
                at = {}
                for half in range(2):
                    pm = pmm[half]
                    wl = [ofbank[0], ofbank[1], prev_proj[half], ev[half]]
                    for dc in range(DC):
                        tk = P.emit("tensor", lambda t, pm=pm, half=half, dc=dc:
                                    t.matmul(
                            pm[:, :], ofm[:, dc, :],
                            wproj[:, dc, half * 512:(half + 1) * 512],
                            start=(dc == 0), stop=(dc == DC - 1)),
                            pe_tok=(dc == DC - 1), wait=wl if dc == 0 else [])
                    at[half] = P.emit("vector", lambda v, pm=pm, half=half,
                                      x_=xr[qslot]: v.tensor_add(
                        x_[:, half * 512:(half + 1) * 512], pm[:, :],
                        x_[:, half * 512:(half + 1) * 512]), wait=[tk, ldx])
                    prev_proj[half] = at[half]
                    ev[half] = at[half]
                sttok[qslot] = P.emit("gpsimd", lambda g, o=xr[qslot], row0=row0:
                                      g.dma_start(xout[row0:row0 + 128, :],
                                                  o[:, :]),
                                      dma=f"dma_o{qslot}", wait=[at[0], at[1]])
            P.wait_only("gpsimd", [sttok[0], sttok[1]])
            P.barrier()


def rb_out(h):
    return (64 * h) % 128


def oc_out(h):
    return h // 2


def emit_l1_attn(P, xin, xout, W, name="l1a"):
    nc = P.nc
    with contextlib.ExitStack() as pstk:
        sb_ = lambda n, s, d: pstk.enter_context(nc.sbuf_tensor(f"{name}_{n}", s, d))
        ps_ = lambda n, s, d: pstk.enter_context(nc.psum_tensor(f"{name}_{n}", s, d))
        r3 = lambda t, c: t.ap().rearrange("p (c t) -> p c t", c=c)
        qf = r3(sb_("qf", [128, DC * T], BF16), DC)
        kf = r3(sb_("kf", [128, DC * T], BF16), DC)
        vt = r3(sb_("vt", [128, 16 * D], BF16), 16)
        xr = [sb_("xr0", [128, D], F32), sb_("xr1", [128, D], F32)]
        sb = {
            "xt0": xr[0], "xt1": xr[1],
            "stats": sb_("stats", [128, 12], F32), "mv": sb_("mv", [128, 2], F32),
            "rstd": sb_("rstd", [128, 1], F32), "xntm": sb_("xntm", [128, D], BF16),
            "ident": sb_("ident", [128, 128], BF16),
        }
        idraw = sb_("idraw", [128, 128], I32)
        ps = {"tr0": ps_("tr0", [128, 1024], BF16).ap()[:, 0:128],
              "tr1": ps_("tr1", [128, 1024], BF16).ap()[:, 0:128]}
        pmm = [ps_("pmm0", [128, 512], F32), ps_("pmm1", [128, 512], F32)]
        pvv = [ps_("pvv0", [128, 512], F32), ps_("pvv1", [128, 512], F32)]
        emit_identity(P, sb["ident"], idraw)

        for b in range(B_CORE):
            with contextlib.ExitStack() as astk:
                sa_ = lambda n, s, d: astk.enter_context(
                    nc.sbuf_tensor(f"{name}A{b}_{n}", s, d))
                xn_fm = r3(sa_("xnfm", [128, DC * T], BF16), DC)
                wq = r3(sa_("wq", [128, DC * D], BF16), DC)
                wk = r3(sa_("wk", [128, DC * D], BF16), DC)
                wv = r3(sa_("wv", [128, DC * D], BF16), DC)
                qtm = sa_("qtm", [128, D], F32)
                rop = sa_("rop", [128, D], BF16)
                cs = [sa_("cs0", [128, 1024], F32), sa_("cs1", [128, 1024], F32)]
                tmp1 = sa_("tmp1", [128, 512], F32)
                tmp2 = sa_("tmp2", [128, 512], F32)
                wtoks = [load_weight_chunks(P, w, W[n], "dma_w")
                         for w, n in ((wq, "l1_wqT"), (wk, "l1_wkT"),
                                      (wv, "l1_wvT"))]
                P.wait_only("gpsimd", [wtoks[-1]])
                emit_ln_to_fm(P, xin, xn_fm, b, sb, ps)
                P.barrier()

                evtok = {0: None, 1: None}
                csuse = [None, None]
                trtok = {0: None, 1: None}
                ropetok = None
                roptr = None
                for tc in range(16):
                    cslot = tc % 2
                    ldc = P.emit("sync", lambda s, c=cs[cslot], tc=tc: s.dma_start(
                        c[:, :], W["l1_ropecs"][tc * 128:(tc + 1) * 128, :]),
                        dma=f"dma_cs{cslot}", wait=[csuse[cslot]])
                    for kind in ("v", "q", "k"):
                        wsb = {"q": wq, "k": wk, "v": wv}[kind]
                        etoks = []
                        for half in range(2):
                            pm = pmm[half]
                            for dc in range(DC):
                                tk = P.emit("tensor", lambda t, pm=pm, wsb=wsb,
                                            tc=tc, half=half, dc=dc: t.matmul(
                                    pm[:, :], xn_fm[:, dc, tc * 128:(tc + 1) * 128],
                                    wsb[:, dc, half * 512:(half + 1) * 512],
                                    start=(dc == 0), stop=(dc == DC - 1)),
                                    pe_tok=(dc == DC - 1),
                                    wait=[evtok[half]] if dc == 0 else [])
                            if kind == "v":
                                evtok[half] = P.emit(
                                    "vector", lambda v, pm=pm, tc=tc, half=half:
                                    v.tensor_copy(
                                        vt[:, tc, half * 512:(half + 1) * 512],
                                        pm[:, :]), wait=[tk])
                            else:
                                scl = (1.0 / 32.0) if kind == "q" else 1.0
                                evtok[half] = P.emit(
                                    "scalar", lambda s, pm=pm, half=half, scl=scl:
                                    s.activation(qtm[:, half * 512:(half + 1) * 512],
                                                 pm[:, :], AF.Identity, scale=scl),
                                    wait=[tk, ropetok])
                            etoks.append(evtok[half])
                        if kind == "v":
                            continue
                        cosap = cs[cslot][:, 0:512]
                        sinap = cs[cslot][:, 512:1024]
                        P.emit("vector", lambda v, c=cosap: v.tensor_mul(
                            tmp1[:, :], qtm[:, 0:512], c),
                            wait=etoks + [ldc])
                        P.emit("vector", lambda v, s=sinap: v.tensor_mul(
                            tmp2[:, :], qtm[:, 512:1024], s))
                        P.emit("vector", lambda v: v.tensor_sub(
                            rop[:, 0:512], tmp1[:, :], tmp2[:, :]),
                            wait=[roptr])
                        P.emit("vector", lambda v, s=sinap: v.tensor_mul(
                            tmp1[:, :], qtm[:, 0:512], s))
                        P.emit("vector", lambda v, c=cosap: v.tensor_mul(
                            tmp2[:, :], qtm[:, 512:1024], c))
                        tr6 = P.emit("vector", lambda v: v.tensor_add(
                            rop[:, 512:1024], tmp1[:, :], tmp2[:, :]))
                        csuse[cslot] = tr6
                        ropetok = tr6
                        dst = qf if kind == "q" else kf
                        for c in range(DC):
                            pslot = c % 2
                            ptr = ps[f"tr{pslot}"]
                            ttr = P.emit("tensor", lambda t, c=c, ptr=ptr:
                                         t.transpose(ptr[:, :],
                                                     rop[:, c * 128:(c + 1) * 128],
                                                     sb["ident"][:, :]),
                                         pe_tok=True, wait=[tr6, trtok[pslot]])
                            eng = "scalar" if pslot == 0 else "vector"
                            col0 = tc * 128
                            trtok[pslot] = P.emit(
                                eng, lambda e, c=c, ptr=ptr, dst=dst, col0=col0,
                                eng=eng: (
                                    e.activation(dst[:, c, col0:col0 + 128],
                                                 ptr[:, :], AF.Identity)
                                    if eng == "scalar" else
                                    e.tensor_copy(dst[:, c, col0:col0 + 128],
                                                  ptr[:, :])), wait=[ttr])
                        roptr = ttr
                P.barrier()

            with contextlib.ExitStack() as bstk:
                sbb_ = lambda n, s, d: bstk.enter_context(
                    nc.sbuf_tensor(f"{name}B{b}_{n}", s, d))
                ssb = sbb_("ssb", [128, T], F32)
                psb = sbb_("psb", [128, T], BF16)
                ptt4 = r3(sbb_("ptt4", [128, 16 * 512], BF16), 16)
                ofm = r3(sbb_("ofm", [128, DC * 512], BF16), DC)
                wo = r3(sbb_("wo", [128, DC * D], BF16), DC)
                rmax = sbb_("rmax", [128, 1], F32)
                nbias = sbb_("nbias", [128, 1], F32)
                rsum = sbb_("rsum", [128, 1], F32)
                rrec = sbb_("rrec", [128, 1], F32)
                wotok = load_weight_chunks(P, wo, W["l1_woT"], "dma_wo")
                P.wait_only("gpsimd", [wotok])

                ev = {0: None, 1: None}
                pvtok = {0: None, 1: None}
                sttok = {0: None, 1: None}
                trtok = {0: None, 1: None}
                oftok = {0: None, 1: None}
                last_ttr = None
                prev_pv = None
                prev_op = {0: None, 1: None}
                for qg in range(4):
                    for j in range(4):
                        qi = qg * 4 + j
                        for kg in range(4):
                            slot = kg % 2
                            pm = pmm[slot]
                            for dc in range(DC):
                                tk = P.emit("tensor", lambda t, pm=pm, qi=qi, kg=kg,
                                            dc=dc: t.matmul(
                                    pm[:, :], qf[:, dc, qi * 128:(qi + 1) * 128],
                                    kf[:, dc, kg * 512:(kg + 1) * 512],
                                    start=(dc == 0), stop=(dc == DC - 1)),
                                    pe_tok=(dc == DC - 1),
                                    wait=[ev[slot]] if dc == 0 else [])
                            ev[slot] = P.emit("vector", lambda v, pm=pm, kg=kg:
                                              v.tensor_copy(
                                ssb[:, kg * 512:(kg + 1) * 512], pm[:, :]),
                                wait=[tk])
                        P.emit("vector", lambda v: v.reduce_max(
                            rmax[:, :], ssb[:, :], axis=AX.X), wait=[ev[0], ev[1]])
                        P.emit("vector", lambda v: v.tensor_scalar(
                            nbias[:, :], rmax[:, :], -1.0, None, ALU.mult))
                        tex = P.emit("scalar", lambda s: s.activation(
                            ssb[:, :], ssb[:, :], AF.Exp, bias=nbias[:, :],
                            accum_out=rsum[:, :]),
                            wait=[("dve", P.count("dve"))])
                        P.emit("vector", lambda v: v.reciprocal(
                            rrec[:, :], rsum[:, :]), wait=[tex])
                        tp = P.emit("vector", lambda v: v.tensor_scalar(
                            psb[:, :], ssb[:, :], rrec[:, :], None, ALU.mult),
                            wait=[last_ttr])
                        for kc in range(16):
                            pslot = kc % 2
                            ptr = ps[f"tr{pslot}"]
                            ttr = P.emit("tensor", lambda t, kc=kc, ptr=ptr:
                                         t.transpose(
                                ptr[:, :], psb[:, kc * 128:(kc + 1) * 128],
                                sb["ident"][:, :]), pe_tok=True,
                                wait=[tp, trtok[pslot], prev_pv])
                            eng = "scalar" if pslot == 0 else "vector"
                            col0 = j * 128
                            trtok[pslot] = P.emit(
                                eng, lambda e, kc=kc, ptr=ptr, col0=col0, eng=eng: (
                                    e.activation(ptt4[:, kc, col0:col0 + 128],
                                                 ptr[:, :], AF.Identity)
                                    if eng == "scalar" else
                                    e.tensor_copy(ptt4[:, kc, col0:col0 + 128],
                                                  ptr[:, :])), wait=[ttr])
                        last_ttr = ttr
                    for dvc in range(DC):
                        slot = dvc % 2
                        pv = pvv[slot]
                        wl = [trtok[0], trtok[1], oftok[slot]]
                        for kc in range(16):
                            tk = P.emit("tensor", lambda t, pv=pv, kc=kc, dvc=dvc:
                                        t.matmul(
                                pv[:, :], vt[:, kc, dvc * 128:(dvc + 1) * 128],
                                ptt4[:, kc, :], start=(kc == 0), stop=(kc == 15)),
                                pe_tok=(kc == 15), wait=wl if kc == 0 else [])
                        prev_pv = tk
                        eng = "scalar" if slot == 0 else "vector"
                        oftok[slot] = P.emit(
                            eng, lambda e, pv=pv, dvc=dvc, eng=eng: (
                                e.activation(ofm[:, dvc, :], pv[:, :], AF.Identity)
                                if eng == "scalar" else
                                e.tensor_copy(ofm[:, dvc, :], pv[:, :])), wait=[tk])
                    for j in range(4):
                        qi = qg * 4 + j
                        qslot = j % 2
                        row0 = b * T + qi * 128
                        ldx = P.emit("sync", lambda s, x_=xr[qslot], row0=row0:
                                     s.dma_start(x_[:, :], xin[row0:row0 + 128, :]),
                                     dma=f"dma_xr{qslot}", wait=[sttok[qslot]])
                        at = {}
                        for half in range(2):
                            pm = pmm[half]
                            wl = [oftok[0], oftok[1], prev_op[half], ev[half]]
                            for dc in range(DC):
                                tk = P.emit("tensor", lambda t, pm=pm, j=j, half=half,
                                            dc=dc: t.matmul(
                                    pm[:, :], ofm[:, dc, j * 128:(j + 1) * 128],
                                    wo[:, dc, half * 512:(half + 1) * 512],
                                    start=(dc == 0), stop=(dc == DC - 1)),
                                    pe_tok=(dc == DC - 1),
                                    wait=wl if dc == 0 else [])
                            at[half] = P.emit("vector", lambda v, pm=pm, half=half,
                                              x_=xr[qslot]: v.tensor_add(
                                x_[:, half * 512:(half + 1) * 512], pm[:, :],
                                x_[:, half * 512:(half + 1) * 512]),
                                wait=[tk, ldx])
                            prev_op[half] = at[half]
                            ev[half] = at[half]
                        sttok[qslot] = P.emit("gpsimd", lambda g, o=xr[qslot],
                                              row0=row0: g.dma_start(
                            xout[row0:row0 + 128, :], o[:, :]),
                            dma=f"dma_o{qslot}", wait=[at[0], at[1]])
                P.wait_only("gpsimd", [sttok[0], sttok[1]])
                P.barrier()


def emit_l3_attn(P, xin, xout, W, name="l3t"):
    nc = P.nc
    with contextlib.ExitStack() as pstk:
        sb_ = lambda n, s, d: pstk.enter_context(nc.sbuf_tensor(f"{name}b_{n}", s, d))
        ps_ = lambda n, s, d: pstk.enter_context(nc.psum_tensor(f"{name}b_{n}", s, d))
        r3 = lambda t, c: t.ap().rearrange("p (c t) -> p c t", c=c)
        xr = [sb_("xr0", [128, D], F32), sb_("xr1", [128, D], F32)]
        sb = {
            "xt0": xr[0], "xt1": xr[1],
            "stats": sb_("stats", [128, 12], F32), "mv": sb_("mv", [128, 2], F32),
            "rstd": sb_("rstd", [128, 1], F32), "xntm": sb_("xntm", [128, D], BF16),
            "ident": sb_("ident", [128, 128], BF16),
        }
        idraw = sb_("idraw", [128, 128], I32)
        ps = {"tr0": ps_("tr0", [128, 1024], BF16).ap()[:, 0:128],
              "tr1": ps_("tr1", [128, 1024], BF16).ap()[:, 0:128]}
        pmm = [ps_("pmm0", [128, 512], F32), ps_("pmm1", [128, 512], F32)]
        pss = ps_("pss", [128, 512], F32)
        pwb = ps_("pwb", [128, 512], F32)
        vfm = sb_("vfm", [128, 2 * DC * T], BF16).ap().rearrange(
            "p (b c t) -> p b c t", b=2, c=DC)
        qwf = sb_("qwf", [128, 2 * DC * T], BF16).ap().rearrange(
            "p (b c t) -> p b c t", b=2, c=DC)
        s2 = sb_("s2", [2, 128], F32)
        esb = sb_("esb", [2, 128], F32)
        wvsb = sb_("wvsb", [2, 128], BF16)
        den = sb_("den", [2, 1], F32)
        rdt = sb_("rdt", [2, 1], F32)
        mask2r = sb_("mask2r", [1, 256], BF16)
        sel2 = sb_("sel2", [1, 4], BF16)
        ones2 = sb_("ones2", [2, 128], BF16)
        eb2 = sb_("eb2", [2, 1], F32)
        eb1 = sb_("eb1", [1, 1], F32)
        ones32 = sb_("ones32", [1, 128], F32)
        wvb = sb_("wvb", [128, 128], BF16)
        junkd = sb_("junkd", [128, 512], BF16)
        junkg = sb_("junkg", [128, 512], BF16)
        w2b11 = sb_("w2b11", [1, 1], F32)

        emit_identity(P, sb["ident"], idraw)
        P.emit("gpsimd", lambda g: g.memset(ones2[:, :], 1.0))
        P.emit("gpsimd", lambda g: g.memset(ones32[:, :], 1.0))
        P.emit("gpsimd", lambda g: g.dma_start(
            mask2r[:, :], W["l3a_mask2"].rearrange("b k -> (b k)")[None, :]
            if False else bass.AP(W["l3a_mask2"].tensor, 0, [[256, 1], [1, 256]])),
            dma="dma_w")
        P.emit("gpsimd", lambda g: g.memset(sel2[:, :], 0.0))
        P.emit("gpsimd", lambda g: g.memset(sel2[0:1, 0:1], 1.0))
        P.emit("gpsimd", lambda g: g.memset(sel2[0:1, 3:4], 1.0))
        ldb = P.emit("gpsimd", lambda g: g.dma_start(
            w2b11[:, :], W["l3a_w2b"][:, :]), dma="dma_w")
        teb = P.emit("scalar", lambda s: s.activation(eb1[:, :], w2b11[:, :],
                                                      AF.Exp), wait=[ldb])
        tk = P.emit("tensor", lambda t: t.matmul(
            pwb[0:2, 0:1], ones32[0:1, 0:2], eb1[0:1, :], start=True, stop=True),
            pe_tok=True, wait=[teb])
        P.emit("vector", lambda v: v.tensor_copy(eb2[:, :], pwb[0:2, 0:1]),
               wait=[tk])
        P.barrier()

        for b in range(B_CORE):
            with contextlib.ExitStack() as astk:
                sa_ = lambda n, s, d: astk.enter_context(
                    nc.sbuf_tensor(f"{name}A{b}_{n}", s, d))
                xn_fm = r3(sa_("xnfm", [128, DC * T], BF16), DC)
                wqw = r3(sa_("wqw", [128, DC * D], BF16), DC)
                wqb = sa_("wqb", [1, D], BF16)
                w2bb = sa_("w2bb", [128, D], BF16)
                qwtm = sa_("qwtm", [128, D], BF16)
                qtm = qwtm
                load_weight_chunks(P, wqw, W["l3a_wqw"], "dma_w")
                P.emit("gpsimd", lambda g: g.dma_start(wqb[:, :],
                                                       W["l3a_wqb"][:, :]),
                       dma="dma_w")
                P.emit("gpsimd", lambda g: g.dma_start(
                    w2bb[:, :], bass.AP(W["l3a_w2w"].tensor, 0,
                                        [[0, 128], [1, D]])), dma="dma_w")
                wtok = ("dma_w", P.count("dma_w"))
                P.wait_only("gpsimd", [wtok])
                emit_ln_to_fm(P, xin, xn_fm, b, sb, ps)
                P.barrier()
                P.emit("vector", lambda v, b=b: v.tensor_copy(
                    vfm[:, b, :, 0:64], xn_fm[:, :, 0:64]))
                evtok = {0: None, 1: None}
                trtok = {0: None, 1: None}
                qprev = None
                for tc in range(16):
                    for half in range(2):
                        slot = half
                        pm = pmm[slot]
                        for dc in range(DC):
                            P.emit("tensor", lambda t, pm=pm, tc=tc, half=half,
                                   dc=dc: t.matmul(
                                pm[:, :], xn_fm[:, dc, tc * 128:(tc + 1) * 128],
                                wqw[:, dc, half * 512:(half + 1) * 512],
                                start=(dc == 0), stop=False),
                                wait=[evtok[slot]] if dc == 0 else [])
                        tk = P.emit("tensor", lambda t, pm=pm, half=half:
                                    t.matmul(
                            pm[:, :], ones2[0:1, :],
                            wqb[0:1, half * 512:(half + 1) * 512],
                            start=False, stop=True), pe_tok=True)
                        evtok[slot] = P.emit("scalar", lambda s, pm=pm, half=half:
                                             s.activation(
                            qtm[:, half * 512:(half + 1) * 512], pm[:, :],
                            AF.Identity), wait=[tk, qprev])
                    tq = P.emit("vector", lambda v: v.tensor_mul(
                        qwtm[:, :], qwtm[:, :], w2bb[:, :]),
                        wait=[evtok[0], evtok[1]])
                    for c in range(DC):
                        pslot = c % 2
                        ptr = ps[f"tr{pslot}"]
                        ttr = P.emit("tensor", lambda t, c=c, ptr=ptr:
                                     t.transpose(
                            ptr[:, :], qwtm[:, c * 128:(c + 1) * 128],
                            sb["ident"][:, :]), pe_tok=True,
                            wait=[tq, trtok[pslot]])
                        col0 = tc * 128
                        eng = "scalar" if pslot == 0 else "vector"
                        trtok[pslot] = P.emit(
                            eng, lambda e, c=c, ptr=ptr, col0=col0, b=b,
                            eng=eng: (
                                e.activation(qwf[:, b, c, col0:col0 + 128],
                                             ptr[:, :], AF.Identity)
                                if eng == "scalar" else
                                e.tensor_copy(qwf[:, b, c, col0:col0 + 128],
                                              ptr[:, :])), wait=[ttr])
                    qprev = ttr
                P.barrier()

        W64 = 64
        prev_tex = None
        for t in range(W64, T):
            dvetok = ("dve", P.count("dve"))
            for dc in range(DC):
                tk = P.emit("tensor", lambda tt_, dc=dc, t=t: tt_.matmul(
                    pss[0:2, 0:128],
                    bass.AP(qwf.tensor, qwf.offset + dc * T + t,
                            [[qwf.ap[0][0], 128], [DC * T, 2]]),
                    bass.AP(vfm.tensor, vfm.offset + dc * T + (t - W64),
                            [[vfm.ap[0][0], 128], [DC * T, 2], [1, 64]]),
                    start=(dc == 0), stop=False),
                    wait=[dvetok, prev_tex] if dc == 0 else [])
            for bb_ in range(2):
                tk = P.emit("tensor", lambda tt_, bb_=bb_: tt_.matmul(
                    pss[0:2, 0:128], sel2[0:1, 2 * bb_:2 * bb_ + 2],
                    mask2r[0:1, bb_ * 128:(bb_ + 1) * 128],
                    start=False, stop=(bb_ == 1)), pe_tok=(bb_ == 1))
            tex = P.emit("scalar", lambda s: s.activation(
                esb[:, :], pss[0:2, 0:128], AF.Exp,
                accum_out=den[:, :]), wait=[tk, ("dve", P.count("dve"))])
            prev_tex = tex
            P.emit("vector", lambda v: v.tensor_add(den[:, :], den[:, :],
                                                    eb2[:, :]), wait=[tex])
            P.emit("vector", lambda v: v.reciprocal(rdt[:, :], den[:, :]))
            twv = P.emit("vector", lambda v: v.tensor_scalar(
                wvsb[:, :], esb[:, :], rdt[:, :], None, ALU.mult))
            tkb = P.emit("tensor", lambda tt_: tt_.matmul(
                pwb[:, 0:128], ones2[:, :], wvsb[:, :], start=True, stop=True),
                pe_tok=True, wait=[twv])
            tev = P.emit("vector", lambda v: v.tensor_copy(wvb[:, :],
                                                           pwb[:, 0:128]),
                         wait=[tkb])
            win0 = bass.AP(vfm.tensor, vfm.offset + (t - W64),
                           [[vfm.ap[0][0], 128], [T, DC], [1, 64]])
            win1 = bass.AP(vfm.tensor, vfm.offset + DC * T + (t - W64),
                           [[vfm.ap[0][0], 128], [T, DC], [1, 64]])
            wb0 = bass.AP(wvb, 0, [[128, 128], [0, DC], [1, 64]])
            wb1 = bass.AP(wvb, 64, [[128, 128], [0, DC], [1, 64]])
            j0 = junkd.ap().rearrange("p (c w) -> p c w", c=DC)
            j1 = junkg.ap().rearrange("p (c w) -> p c w", c=DC)
            t0a = P.emit("vector", lambda v, win0=win0, wb0=wb0: v.tensor_tensor(
                j0, win0, wb0, ALU.mult), wait=[tev])
            t1a = P.emit("gpsimd", lambda g, win1=win1, wb1=wb1: g.tensor_tensor(
                j1, win1, wb1, ALU.mult), wait=[tev])
            def _redv(v, t=t, which=0, src_=j0):
                with nc.allow_low_precision(reason="fp32-internal reduce, "
                                            "bf16 store into V"):
                    return v.tensor_reduce(vfm[:, which, :, t], src_,
                                           axis=AX.X, op=ALU.add)
            P.emit("vector", lambda v, f=_redv: f(v))
            t1b = P.emit("vector", lambda v, t=t: _redv(v, t, 1, j1),
                         wait=[t1a])
        P.barrier()

        with contextlib.ExitStack() as fstk:
            sf_ = lambda n, s, d: fstk.enter_context(
                nc.sbuf_tensor(f"{name}F_{n}", s, d))
            gb = sf_("gb", [128, D], F32)
            bb = sf_("bb", [128, D], F32)
            ftile = sf_("ftile", [128, D], F32)
            vltm = sf_("vltm", [128, D], F32)
            th = sf_("th", [128, D], F32)
            yt = sf_("yt", [128, D], F32)
            sb2 = dict(sb)
            sb2["lntmp"] = th
            P.emit("gpsimd", lambda g: g.dma_start(
                gb[:, :], bass.AP(W["l3a_lng"].tensor, 0, [[0, 128], [1, D]])),
                dma="dma_w")
            P.emit("gpsimd", lambda g: g.dma_start(
                bb[:, :], bass.AP(W["l3a_lnb"].tensor, 0, [[0, 128], [1, D]])),
                dma="dma_w")
            P.wait_only("gpsimd", [("dma_w", P.count("dma_w"))])
            P.barrier()
            sttok = {0: None, 1: None}
            prevf = {0: None, 1: None}
            trtok = {0: None, 1: None}
            for b in range(B_CORE):
                for i in range(16):
                    slot = i % 2
                    xt = xr[slot]
                    row0 = b * T + i * 128
                    ld = P.emit("sync", lambda s, xt=xt, row0=row0: s.dma_start(
                        xt[:, :], xin[row0:row0 + 128, :]), dma=f"dma_x{slot}",
                        wait=[prevf[slot], sttok[slot]])
                    tf = emit_ln_tile(P, xin, row0, xt, ld, sb, ftile[:, :],
                                      eps=1e-5)
                    prevf[slot] = tf
                    for c in range(DC):
                        pslot = c % 2
                        ptr = ps[f"tr{pslot}"]
                        ttr = P.emit("tensor", lambda t, c=c, ptr=ptr, b=b, i=i:
                                     t.transpose(
                            ptr[:, :], vfm[:, b, c, i * 128:(i + 1) * 128],
                            sb["ident"][:, :]), pe_tok=True,
                            wait=[trtok[pslot]])
                        trtok[pslot] = P.emit(
                            "vector", lambda v, c=c, ptr=ptr: v.tensor_copy(
                                vltm[:, c * 128:(c + 1) * 128], ptr[:, :]),
                            wait=[ttr])
                    tsc = trtok[1]
                    if i == 0:
                        tsc = P.emit("vector", lambda v: v.tensor_copy(
                            vltm[0:64, :], ftile[0:64, :]),
                            wait=[tf, trtok[0], trtok[1]])
                    tth = P.emit("scalar", lambda s: s.activation(
                        th[:, :], vltm[:, :], AF.Sigmoid, scale=2.0),
                        wait=[tsc, ("dve", P.count("dve"))])
                    P.emit("vector", lambda v: v.tensor_scalar(
                        th[:, :], th[:, :], 2.0, None, ALU.mult), wait=[tth])
                    ty = P.emit("vector", lambda v: v.tensor_mul(
                        yt[:, :], th[:, :], ftile[:, :]))
                    tln = emit_ln_tile(P, xin, row0, yt, ty, sb2, yt[:, :],
                                       eps=1e-5, gain=gb[:, :], bias_v=bb[:, :])
                    ta = P.emit("vector", lambda v, xt=xt: v.tensor_add(
                        xt[:, :], yt[:, :], xt[:, :]), wait=[tln])
                    prevf[slot] = ta
                    sttok[slot] = P.emit("gpsimd", lambda g, o=xt, row0=row0:
                                         g.dma_start(xout[row0:row0 + 128, :],
                                                     o[:, :]),
                                         dma=f"dma_o{slot}",
                                         wait=[ta, sttok[slot]])
            P.wait_only("gpsimd", [sttok[0], sttok[1]])
            P.barrier()


def emit_l0_ffn(P, xin, xout, W, name="l0f"):
    nc = P.nc
    FS, IS, RBIG = 12288, 4096, 0.7
    FC = FS // 128
    xs = [nc.dram_tensor(f"{name}_xs{i}", [NTOK, D], F32, kind="Internal").ap()
          for i in range(2)]
    cci = nc.dram_tensor(f"{name}_cci", [128, 128], F32, kind="Internal")
    cco = nc.dram_tensor(f"{name}_cco", [128, 128], F32, kind="Internal")
    with contextlib.ExitStack() as pstk:
        sb_ = lambda n, s, d: pstk.enter_context(nc.sbuf_tensor(f"{name}b_{n}", s, d))
        ps_ = lambda n, s, d: pstk.enter_context(nc.psum_tensor(f"{name}b_{n}", s, d))
        r3 = lambda t, c: t.ap().rearrange("p (c t) -> p c t", c=c)
        xr = [sb_("xr0", [128, D], F32), sb_("xr1", [128, D], F32)]
        x2t = [sb_("x2t0", [128, D], F32), sb_("x2t1", [128, D], F32)]
        sb = {
            "xt0": xr[0], "xt1": xr[1],
            "stats": sb_("stats", [128, 12], F32), "mv": sb_("mv", [128, 2], F32),
            "rstd": sb_("rstd", [128, 1], F32), "xntm": sb_("xntm", [128, D], BF16),
            "ident": sb_("ident", [128, 128], BF16),
        }
        idraw = sb_("idraw", [128, 128], I32)
        ps = {"tr0": ps_("tr0", [128, 1024], BF16).ap()[:, 0:128],
              "tr1": ps_("tr1", [128, 1024], BF16).ap()[:, 0:128]}
        pz = [ps_("pz0", [128, 512], F32), ps_("pz1", [128, 512], F32)]
        pa = [ps_("pa0", [128, 512], F32), ps_("pa1", [128, 512], F32),
              ps_("pa2", [128, 512], F32), ps_("pa3", [128, 512], F32)]
        xc = r3(sb_("xc", [128, DC * NTOK], BF16), DC)
        _wmraw = [sb_("wma", [128, DC * 1024], BF16),
                  sb_("wmb", [128, DC * 1024], BF16)]
        _wpraw = [sb_("wpa", [128, DC * 1024], BF16),
                  sb_("wpb", [128, DC * 1024], BF16)]
        wm = [r3(t, DC) for t in _wmraw]
        wp = [r3(t, DC) for t in _wpraw]
        wc2v = [t.ap().rearrange("p (rc f) -> p rc f", rc=2)
                for t in (_wmraw[0], _wmraw[1], _wpraw[0])]
        hbuf = r3(sb_("hbuf", [128, DC * 256], BF16), DC)
        wc1t = r3(sb_("wc1t", [128, DC * 256], BF16), DC)
        nm1 = sb_("nm1", [128, 2 * FC], F32).ap().rearrange(
            "p (b f) -> p b f", b=2)
        nm2 = sb_("nm2", [128, 2 * FC], F32).ap().rearrange(
            "p (b f) -> p b f", b=2)
        cmb = sb_("cmb", [128, 2 * FC], F32).ap().rearrange(
            "p (b f) -> p b f", b=2)
        a3m = sb_("a3m", [128, 2 * FC], F32).ap().rearrange(
            "p (b f) -> p b f", b=2)
        ck = sb_("ck", [128, 2 * FC], F32).ap().rearrange(
            "p (b f) -> p b f", b=2)
        idxg = sb_("idxg", [128, FC], F32)
        eqb = sb_("eqb", [128, FC], F32)
        junk = sb_("junk", [128, FC], F32)
        junk2 = sb_("junk2", [128, FC], F32)
        pooled = sb_("pooled", [128, DC * 2], BF16).ap().rearrange(
            "p (c b) -> p c b", c=DC)
        prow = sb_("prow", [1, D], BF16)
        ubf = sb_("ubf", [128, 2 * 2], BF16).ap().rearrange(
            "p (rc b) -> p rc b", rc=2)
        onescol32 = sb_("onescol32", [128, 1], F32)
        ones32 = sb_("ones32", [1, 128], F32)
        candbc = sb_("candbc", [128, 1], F32)
        pcnt = sb_("pcnt", [128, 1], F32)
        zztile = sb_("zztile", [128, 128], F32)
        dbc = sb_("dbc", [128, 4], F32)
        cells = sb_("cells", [1, 24], F32)
        icells = sb_("icells", [1, 24], I32)
        CAND, Q, CNT, R_, J, TF, TH, TL, T2H, T2L, QPM, QPN, TPOS, TNEG, TSTar, \
            GT_, D2, D3, NV, TMP1, TMP2 = range(21)

        def fcell(i):
            return cells[0:1, i:i + 1]

        def icell(i):
            return icells[0:1, i:i + 1]

        emit_identity(P, sb["ident"], idraw)
        P.emit("gpsimd", lambda g: g.memset(onescol32[:, :], 1.0))
        P.emit("gpsimd", lambda g: g.memset(ones32[:, :], 1.0))
        P.emit("gpsimd", lambda g: g.memset(zztile[:, :], 0.0))
        P.emit("gpsimd", lambda g: g.iota(idxg[:, :], [[128, FC]], base=0,
                                          channel_multiplier=1,
                                          allow_small_or_imprecise_dtypes=True))
        P.emit("gpsimd", lambda g: g.memset(cmb[:, :, :], 0.0))
        P.emit("gpsimd", lambda g: g.memset(cmb[:, 0, 0:32], 1.0))
        P.emit("gpsimd", lambda g: g.memset(cmb[:, 1, 0:32], 1.0))
        load_weight_chunks(P, wc1t, W["l0f_wc1t"], "dma_w")
        P.wait_only("gpsimd", [("dma_w", P.count("dma_w"))])
        P.barrier()

        for b in range(B_CORE):
            emit_ln_to_fm(P, xin, xc[:, :, b * T:(b + 1) * T], b, sb, ps)
            P.barrier()

        wld = {0: None, 1: None}
        wuse = {0: None, 1: None}

        def emit_step(step, nblocks, mask_ap, xs_out):
            zev = {0: None, 1: None}
            acev = {0: None, 1: None, 2: None, 3: None}
            sttok = {0: None, 1: None}
            ldpX = {0: None, 1: None}
            for tg in range(16):
                b = tg // 8
                tok0 = tg * 256
                for fb in range(nblocks):
                    es = fb % 2
                    wmt = P.emit("gpsimd", lambda g, es=es, fb=fb: g.dma_start(
                        wm[es][:, :, :],
                        W["l0f_wmt"][:, fb * 1024:(fb + 1) * 1024]
                        .rearrange("(c p) f -> p c f", p=128)), dma=f"dma_m{es}",
                        wait=[wuse[es]])
                    P.emit("gpsimd", lambda g, es=es, fb=fb: g.dma_start(
                        wp[es][:, :, :],
                        W["l0f_wpt"][fb * 1024:(fb + 1) * 1024, :]
                        .rearrange("(c p) f -> p c f", p=128)), dma=f"dma_m{es}")
                    wldt = (f"dma_m{es}", P.count(f"dma_m{es}"))
                    for fc in range(DC):
                        zs = fc % 2
                        pmz = pz[zs]
                        for dc in range(DC):
                            tk = P.emit("tensor", lambda t, pmz=pmz, es=es, fc=fc,
                                        dc=dc, tok0=tok0: t.matmul(
                                pmz[:, 0:256],
                                wm[es][:, dc, fc * 128:(fc + 1) * 128],
                                xc[:, dc, tok0:tok0 + 256],
                                start=(dc == 0), stop=(dc == DC - 1)),
                                pe_tok=(dc == DC - 1),
                                wait=[zev[zs], wldt] if dc == 0 else [])
                        if mask_ap is None:
                            zev[zs] = P.emit("vector", lambda v, pmz=pmz, fc=fc:
                                             v.tensor_scalar(
                                hbuf[:, fc, :], pmz[:, 0:256], 0.0, None,
                                ALU.max), wait=[tk])
                        else:
                            fg = fb * 8 + fc
                            zev[zs] = P.emit("vector", lambda v, pmz=pmz, fc=fc,
                                             fg=fg, b=b: v.tensor_scalar(
                                hbuf[:, fc, :], pmz[:, 0:256], 0.0,
                                mask_ap[:, b, fg:fg + 1], ALU.max, ALU.mult),
                                wait=[tk])
                    for tt in range(2):
                        for ch in range(2):
                            pslot = tt * 2 + ch
                            pmx = pa[pslot]
                            first = (fb == 0)
                            last = (fb == nblocks - 1)
                            wl = [zev[0], zev[1]]
                            if first:
                                wl.append(acev[pslot])
                            for fc in range(DC):
                                tk = P.emit("tensor", lambda t, pmx=pmx, es=es,
                                            tt=tt, ch=ch, fc=fc, first=first,
                                            last=last: t.matmul(
                                    pmx[:, :], hbuf[:, fc, tt * 128:(tt + 1) * 128],
                                    wp[es][:, fc, ch * 512:(ch + 1) * 512],
                                    start=(first and fc == 0),
                                    stop=(last and fc == DC - 1)),
                                    pe_tok=(fc == DC - 1),
                                    wait=wl if fc == 0 else [])
                            if last:
                                acev[pslot] = tk
                    wuse[es] = tk
                for tt in range(2):
                    qslot = tt
                    row0 = tok0 + tt * 128
                    if step >= 2:
                        ldp = P.emit("sync", lambda s, x_=x2t[qslot], row0=row0:
                                     s.dma_start(x_[:, :],
                                                 xs[step - 2][row0:row0 + 128, :]),
                                     dma=f"dma_p{qslot}", wait=[ldpX[qslot]])
                    if step == 3:
                        ldx = P.emit("sync", lambda s, x_=xr[qslot], row0=row0:
                                     s.dma_start(x_[:, :],
                                                 xin[row0:row0 + 128, :]),
                                     dma=f"dma_q{qslot}", wait=[sttok[qslot]])
                    ats = []
                    for ch in range(2):
                        pslot = tt * 2 + ch
                        pmx = pa[pslot]
                        if step == 1:
                            at = P.emit("vector", lambda v, pmx=pmx, x_=x2t[qslot],
                                        ch=ch: v.tensor_copy(
                                x_[:, ch * 512:(ch + 1) * 512], pmx[:, :]),
                                wait=[acev[pslot], ldpX[qslot]])
                        elif step == 2:
                            P.emit("vector", lambda v, x_=x2t[qslot], ch=ch:
                                   v.tensor_scalar(
                                x_[:, ch * 512:(ch + 1) * 512],
                                x_[:, ch * 512:(ch + 1) * 512], dbc[:, 0:1],
                                None, ALU.mult), wait=[ldp, acev[pslot]])
                            at = P.emit("vector", lambda v, pmx=pmx,
                                        x_=x2t[qslot], ch=ch:
                                        v.scalar_tensor_tensor(
                                x_[:, ch * 512:(ch + 1) * 512], pmx[:, :],
                                dbc[:, 1:2], x_[:, ch * 512:(ch + 1) * 512],
                                ALU.mult, ALU.add))
                        else:
                            P.emit("vector", lambda v, x_=x2t[qslot], ch=ch:
                                   v.tensor_scalar(
                                x_[:, ch * 512:(ch + 1) * 512],
                                x_[:, ch * 512:(ch + 1) * 512], dbc[:, 2:3],
                                None, ALU.mult), wait=[ldp, acev[pslot]])
                            P.emit("vector", lambda v, pmx=pmx, x_=x2t[qslot],
                                   ch=ch: v.scalar_tensor_tensor(
                                x_[:, ch * 512:(ch + 1) * 512], pmx[:, :],
                                dbc[:, 3:4], x_[:, ch * 512:(ch + 1) * 512],
                                ALU.mult, ALU.add))
                            at = P.emit("vector", lambda v, x_=x2t[qslot],
                                        xo=xr[qslot], ch=ch: v.tensor_add(
                                x_[:, ch * 512:(ch + 1) * 512],
                                x_[:, ch * 512:(ch + 1) * 512],
                                xo[:, ch * 512:(ch + 1) * 512]),
                                wait=[ldx])
                        acev[pslot] = at
                        ats.append(at)
                    dst = xout if step == 3 else xs_out
                    sttok[qslot] = P.emit("gpsimd", lambda g, o=x2t[qslot],
                                          row0=row0, dst=dst: g.dma_start(
                        dst[row0:row0 + 128, :], o[:, :]),
                        dma=f"dma_o{qslot}", wait=[at])
                    ldpX[qslot] = sttok[qslot]
            P.wait_only("gpsimd", [sttok[0], sttok[1]])
            P.barrier()

        def emit_load_fm(xs_in):
            trtok = {0: None, 1: None}
            cptok = {0: None, 1: None}
            prev = {0: None, 1: None}
            prevm = {0: None, 1: None}
            pmt = None
            pguard = None
            for i in range(32):
                b = i // 16
                slot = i % 2
                xt = xr[slot]
                row0 = i * 128
                ld = P.emit("sync", lambda s, xt=xt, row0=row0: s.dma_start(
                    xt[:, :], xs_in[row0:row0 + 128, :]), dma=f"dma_x{slot}",
                    wait=[prev[slot], prevm[slot]])
                txn = P.emit("vector", lambda v, xt=xt: v.tensor_copy(
                    sb["xntm"][:, :], xt[:, :]),
                    wait=[ld, trtok[0], trtok[1]])
                prev[slot] = txn
                for hh in range(2):
                    pmx = pz[hh]
                    pmt = P.emit("tensor", lambda t, pmx=pmx, hh=hh, xt=xt, i=i:
                                 t.matmul(
                        pmx[0:1, :], onescol32[:, :],
                        xt[:, hh * 512:(hh + 1) * 512],
                        start=(i % 16 == 0), stop=(i % 16 == 15)),
                        pe_tok=(hh == 1),
                        wait=[txn, pguard] if i % 16 == 0 else [txn])
                prevm[slot] = pmt
                for c in range(DC):
                    pslot = c % 2
                    ptr = ps[f"tr{pslot}"]
                    ttr = P.emit("tensor", lambda t, c=c, ptr=ptr: t.transpose(
                        ptr[:, :], sb["xntm"][:, c * 128:(c + 1) * 128],
                        sb["ident"][:, :]), pe_tok=True,
                        wait=[txn, cptok[pslot]])
                    col0 = i * 128
                    eng = "scalar" if pslot == 0 else "vector"
                    cptok[pslot] = P.emit(
                        eng, lambda e, c=c, ptr=ptr, col0=col0, eng=eng: (
                            e.activation(xc[:, c, col0:col0 + 128], ptr[:, :],
                                         AF.Identity) if eng == "scalar" else
                            e.tensor_copy(xc[:, c, col0:col0 + 128],
                                          ptr[:, :])), wait=[ttr])
                    trtok[pslot] = cptok[pslot]
                if i % 16 == 15:
                    e1 = P.emit("scalar", lambda s, b=b: s.activation(
                        prow[0:1, 0:512], pz[0][0:1, :], AF.Identity,
                        scale=1.0 / T), wait=[pmt])
                    e2 = P.emit("scalar", lambda s, b=b: s.activation(
                        prow[0:1, 512:1024], pz[1][0:1, :], AF.Identity,
                        scale=1.0 / T))
                    pguard = e2
                    for c in range(DC):
                        pslot = c % 2
                        ptr = ps[f"tr{pslot}"]
                        ttr = P.emit("tensor", lambda t, c=c, ptr=ptr: t.transpose(
                            ptr[:, 0:1], prow[0:1, c * 128:(c + 1) * 128],
                            sb["ident"][0:1, 0:1]), pe_tok=True,
                            wait=[e1, e2, cptok[pslot]])
                        cptok[pslot] = P.emit(
                            "vector", lambda v, c=c, ptr=ptr, b=b: v.tensor_copy(
                                pooled[:, c, b:b + 1], ptr[:, 0:1]), wait=[ttr])
                        trtok[pslot] = cptok[pslot]
            P.barrier()

        def emit_ck():
            for j in range(3):
                P.emit("gpsimd", lambda g, j=j: g.dma_start(
                    wc2v[j][:, :, :],
                    W["l0f_wc2t"][:, j * 4096:(j + 1) * 4096]
                    .rearrange("(rc p) f -> p rc f", p=128)), dma="dma_w")
            wc2tok = ("dma_w", P.count("dma_w"))
            uev = None
            for rc in range(2):
                pmz = pz[rc]
                for dc in range(DC):
                    tk = P.emit("tensor", lambda t, pmz=pmz, rc=rc, dc=dc:
                                t.matmul(
                        pmz[0:128, 0:2], wc1t[:, dc, rc * 128:(rc + 1) * 128],
                        pooled[:, dc, :], start=(dc == 0), stop=(dc == DC - 1)),
                        pe_tok=(dc == DC - 1), wait=[uev] if dc == 0 else [])
                uev = P.emit("vector", lambda v, pmz=pmz, rc=rc: v.tensor_scalar(
                    ubf[:, rc, :], pmz[0:128, 0:2], 0.0, None, ALU.max),
                    wait=[tk])
            ckev = {0: None, 1: None}
            for fc in range(FC):
                zs = fc % 2
                pmz = pz[zs]
                wcv = wc2v[fc // 32]
                fo = (fc % 32) * 128
                for rc in range(2):
                    tk = P.emit("tensor", lambda t, pmz=pmz, fo=fo, rc=rc,
                                wcv=wcv: t.matmul(
                        pmz[0:128, 0:2], wcv[:, rc, fo:fo + 128],
                        ubf[:, rc, :], start=(rc == 0), stop=(rc == 1)),
                        pe_tok=(rc == 1),
                        wait=[ckev[zs], uev, wc2tok] if rc == 0 else [])
                ckev[zs] = P.emit("vector", lambda v, pmz=pmz, fc=fc:
                                  v.tensor_copy(ck[:, :, fc], pmz[0:128, 0:2]),
                                  wait=[tk])
            wuse[0] = ("pe", P.count("pe"))
            wuse[1] = ("pe", P.count("pe"))
            return [ckev[0], ckev[1]]

        def bcast11(src_ap, dst_ap, waits):
            tk = P.emit("tensor", lambda t: t.matmul(
                pa[0][:, 0:1], ones32[:, :], src_ap, start=True, stop=True),
                pe_tok=True, wait=list(waits) + [("dve", P.count("dve"))])
            return P.emit("vector", lambda v: v.tensor_copy(dst_ap, pa[0][:, 0:1]),
                          wait=[tk])

        def count_ge(ck_b, thbc_tok, out_cell, op=ALU.is_ge):
            t1 = P.emit("vector", lambda v, ck_b=ck_b, op=op: v.tensor_scalar(
                junk[:, :], ck_b, candbc[:, :], None, op, ALU.add,
                accum_out=pcnt[:, :]),
                wait=[thbc_tok, ("pe", P.count("pe"))])
            t2 = P.emit("tensor", lambda t: t.matmul(
                pa[1][0:1, 0:1], pcnt[:, :], onescol32[:, :], start=True,
                stop=True), pe_tok=True, wait=[t1])
            return P.emit("vector", lambda v, out_cell=out_cell: v.tensor_copy(
                out_cell, pa[1][0:1, 0:1]), wait=[t2])

        def emit_topk(b, out_mask):
            ck_b = ck[:, b, :]
            z8 = P.emit("vector", lambda v: v.memset(icells[:, :], 0))
            tb = bcast11(fcell(TMP1), candbc[:, :], [P.emit(
                "vector", lambda v: v.memset(cells[0:1, TMP1:TMP1 + 1], 0.0))])
            cq = count_ge(ck_b, tb, fcell(QPM))
            P.emit("vector", lambda v: v.tensor_scalar(
                fcell(QPM), fcell(QPM), float(IS), None, ALU.is_ge))
            for bit in range(30, -1, -1):
                hb = float(1 << (bit - 16)) if bit >= 16 else 0.0
                lb = float(1 << bit) if bit < 16 else 0.0
                P.emit("vector", lambda v, hb=hb: v.tensor_scalar(
                    icell(T2H), icells[0:1, TH:TH + 1], hb, None, ALU.add))
                P.emit("vector", lambda v, lb=lb: v.tensor_scalar(
                    icell(T2L), icells[0:1, TL:TL + 1], lb, None, ALU.add))
                tc_ = P.emit("vector", lambda v: v.tensor_scalar(
                    icell(CAND), icell(T2H), 16, icell(T2L), ALU.logical_shift_left,
                    ALU.bitwise_or))
                tbb = bcast11(icell(CAND).bitcast(F32), candbc[:, :], [tc_])
                count_ge(ck_b, tbb, fcell(CNT))
                tq = P.emit("vector", lambda v: v.tensor_scalar(
                    fcell(Q), fcell(CNT), float(IS), None, ALU.is_ge))
                P.emit("vector", lambda v: v.tensor_copy(icell(Q), fcell(Q)))
                if bit >= 16:
                    P.emit("vector", lambda v, s=bit - 16: v.tensor_scalar(
                        icell(TMP1), icell(Q), s, None, ALU.logical_shift_left))
                    P.emit("vector", lambda v: v.tensor_tensor(
                        icell(TH), icell(TH), icell(TMP1), ALU.add))
                else:
                    P.emit("vector", lambda v, s=bit: v.tensor_scalar(
                        icell(TMP1), icell(Q), s, None, ALU.logical_shift_left))
                    P.emit("vector", lambda v: v.tensor_tensor(
                        icell(TL), icell(TL), icell(TMP1), ALU.add))
            P.emit("vector", lambda v: v.tensor_scalar(
                icell(TPOS), icell(TH), 16, icell(TL), ALU.logical_shift_left,
                ALU.bitwise_or))
            P.emit("vector", lambda v: v.memset(icell(T2H), 0))
            P.emit("vector", lambda v: v.memset(icell(T2L), 0))
            for bit in range(30, -1, -1):
                fill = (1 << bit) - 1
                fh, fl = float(fill >> 16), float(fill & 0xFFFF)
                P.emit("vector", lambda v, fh=fh: v.tensor_scalar(
                    icell(TH), icell(T2H), fh, None, ALU.add))
                P.emit("vector", lambda v, fl=fl: v.tensor_scalar(
                    icell(TL), icell(T2L), fl, None, ALU.add))
                tc_ = P.emit("vector", lambda v: v.tensor_scalar(
                    icell(CAND), icell(TH), 16, icell(TL), ALU.logical_shift_left,
                    ALU.bitwise_or))
                tc2 = P.emit("vector", lambda v: v.tensor_scalar(
                    icell(CAND), icell(CAND), -2147483648, None, ALU.bitwise_or))
                tbb = bcast11(icell(CAND).bitcast(F32), candbc[:, :], [tc2])
                count_ge(ck_b, tbb, fcell(CNT))
                P.emit("vector", lambda v: v.tensor_scalar(
                    fcell(Q), fcell(CNT), float(IS), None, ALU.is_lt))
                P.emit("vector", lambda v: v.tensor_copy(icell(Q), fcell(Q)))
                if bit >= 16:
                    P.emit("vector", lambda v, s=bit - 16: v.tensor_scalar(
                        icell(TMP1), icell(Q), s, None, ALU.logical_shift_left))
                    P.emit("vector", lambda v: v.tensor_tensor(
                        icell(T2H), icell(T2H), icell(TMP1), ALU.add))
                else:
                    P.emit("vector", lambda v, s=bit: v.tensor_scalar(
                        icell(TMP1), icell(Q), s, None, ALU.logical_shift_left))
                    P.emit("vector", lambda v: v.tensor_tensor(
                        icell(T2L), icell(T2L), icell(TMP1), ALU.add))
            P.emit("vector", lambda v: v.tensor_scalar(
                icell(TNEG), icell(T2H), 16, icell(T2L), ALU.logical_shift_left,
                ALU.bitwise_or))
            P.emit("vector", lambda v: v.tensor_scalar(
                icell(TNEG), icell(TNEG), -2147483648, None, ALU.bitwise_or))
            P.emit("vector", lambda v: v.tensor_copy(icell(TMP1), fcell(QPM)))
            P.emit("vector", lambda v: v.tensor_scalar(
                icell(TMP1), icell(TMP1), -1.0, None, ALU.mult))
            P.emit("vector", lambda v: v.tensor_scalar(
                icell(TMP2), icell(TMP1), 1.0, -1.0, ALU.add, ALU.mult))
            P.emit("vector", lambda v: v.tensor_scalar(
                icell(TPOS), icell(TPOS), icells[0:1, TMP1:TMP1 + 1], None,
                ALU.bitwise_and))
            P.emit("vector", lambda v: v.tensor_scalar(
                icell(TNEG), icell(TNEG), icells[0:1, TMP2:TMP2 + 1], None,
                ALU.bitwise_and))
            tsel = P.emit("vector", lambda v: v.tensor_scalar(
                icell(TSTar), icell(TPOS), icells[0:1, TNEG:TNEG + 1], None,
                ALU.bitwise_or))
            tbb = bcast11(icell(TSTar).bitcast(F32), candbc[:, :], [tsel])
            count_ge(ck_b, tbb, fcell(GT_), op=ALU.is_gt)
            P.emit("vector", lambda v: v.tensor_scalar(
                fcell(R_), fcell(GT_), -1.0, float(IS), ALU.mult, ALU.add))
            teq = P.emit("vector", lambda v, ck_b=ck_b: v.tensor_scalar(
                eqb[:, :], ck_b, candbc[:, :], None, ALU.is_equal))
            P.emit("vector", lambda v: v.memset(cells[0:1, J:J + 1], 0.0))
            for bit in range(13, -1, -1):
                bv = float(1 << bit)
                P.emit("vector", lambda v, bv=bv: v.tensor_scalar(
                    fcell(TMP1), fcell(J), bv - 1.0, None, ALU.add))
                tb2 = bcast11(fcell(TMP1), candbc[:, :],
                              [("dve", P.count("dve"))])
                t1 = P.emit("vector", lambda v: v.tensor_scalar(
                    junk[:, :], idxg[:, :], candbc[:, :], None, ALU.is_le),
                    wait=[tb2])
                P.emit("vector", lambda v: v.tensor_mul(junk2[:, :], junk[:, :],
                                                        eqb[:, :]))
                t3 = P.emit("vector", lambda v: v.tensor_scalar(
                    junk2[:, :], junk2[:, :], 0.0, None, ALU.add, ALU.add,
                    accum_out=pcnt[:, :]), wait=[("pe", P.count("pe"))])
                t4 = P.emit("tensor", lambda t: t.matmul(
                    pa[1][0:1, 0:1], pcnt[:, :], onescol32[:, :], start=True,
                    stop=True), pe_tok=True, wait=[t3])
                P.emit("vector", lambda v: v.tensor_copy(fcell(CNT),
                                                         pa[1][0:1, 0:1]),
                       wait=[t4])
                P.emit("vector", lambda v: v.tensor_tensor(
                    fcell(TF), fcell(CNT), fcell(R_), ALU.is_lt))
                P.emit("vector", lambda v, bv=bv: v.tensor_scalar(
                    fcell(TMP1), fcell(TF), bv, None, ALU.mult))
                P.emit("vector", lambda v: v.tensor_tensor(
                    fcell(J), fcell(J), fcell(TMP1), ALU.add))
            tbj = bcast11(fcell(J), candbc[:, :], [("dve", P.count("dve"))])
            tbt = bcast11(icell(TSTar).bitcast(F32), pcnt[:, :], [tbj])
            tgt = P.emit("vector", lambda v, ck_b=ck_b: v.tensor_scalar(
                junk[:, :], ck_b, pcnt[:, :], None, ALU.is_gt), wait=[tbt])
            P.emit("vector", lambda v: v.tensor_scalar(
                junk2[:, :], idxg[:, :], candbc[:, :], None, ALU.is_le),
                wait=[tbj])
            P.emit("vector", lambda v: v.tensor_mul(junk2[:, :], junk2[:, :],
                                                    eqb[:, :]))
            tm = P.emit("vector", lambda v, out_mask=out_mask: v.tensor_add(
                out_mask, junk[:, :], junk2[:, :]))
            return tm

        def emit_novelty_flags(nm, dcell):
            sums = []
            tmm = None
            for j, (src, sgn) in enumerate(((0, 1.0), (1, 1.0),
                                            (0, -1.0), (1, -1.0))):
                if sgn > 0:
                    tacc = P.emit("vector", lambda v, src=src: v.tensor_tensor(
                        junk[:, :], cmb[:, src, :], nm[:, src, :], ALU.max),
                        wait=[tmm])
                    tacc = P.emit("vector", lambda v: v.tensor_scalar(
                        junk2[:, :], junk[:, :], 0.0, None, ALU.add, ALU.add,
                        accum_out=pcnt[:, :]))
                else:
                    tacc = P.emit("vector", lambda v, src=src: v.tensor_scalar(
                        junk2[:, :], cmb[:, src, :], -1.0, None, ALU.mult,
                        ALU.add, accum_out=pcnt[:, :]), wait=[tmm])
                tmm = P.emit("tensor", lambda t, j=j: t.matmul(
                    pa[1][0:1, 0:1], pcnt[:, :], onescol32[:, :],
                    start=(j == 0), stop=(j == 3)), pe_tok=True, wait=[tacc])
            P.emit("vector", lambda v, nm=nm: v.tensor_tensor(
                cmb[:, 0, :], cmb[:, 0, :], nm[:, 0, :], ALU.max), wait=[tmm])
            P.emit("vector", lambda v, nm=nm: v.tensor_tensor(
                cmb[:, 1, :], cmb[:, 1, :], nm[:, 1, :], ALU.max))
            tz = P.emit("vector", lambda v: v.tensor_copy(
                zztile[0:1, 0:1], pa[1][0:1, 0:1]), wait=[tmm])
            st = P.emit("gpsimd", lambda g: g.dma_start(cci.ap(), zztile[:, :]),
                        dma="dma_c", wait=[tz])
            ar = P.emit("gpsimd", lambda g: g.collective_compute(
                "AllReduce", ALU.add, replica_groups=L0FFN_GROUPS,
                ins=[cci.ap().opt()], outs=[cco.ap().opt()]), wait=[st])
            ld = P.emit("sync", lambda s: s.dma_start(zztile[:, :], cco.ap()),
                        dma="dma_cl", wait=[ar])
            tk = P.emit("vector", lambda v, dcell=dcell: v.tensor_scalar(
                dcell, zztile[0:1, 0:1], L0FFN_THRESH, None, ALU.is_lt),
                wait=[ld])
            return tk

        emit_step(1, 4, None, xs[0])
        emit_load_fm(xs[0])
        ckt = emit_ck()
        P.wait_only("vector", ckt)
        emit_topk(0, nm1[:, 0, :])
        emit_topk(1, nm1[:, 1, :])
        t2 = emit_novelty_flags(nm1, fcell(D2))
        P.emit("vector", lambda v: v.tensor_scalar(
            fcell(TMP1), fcell(D2), -1.0, 1.0, ALU.mult, ALU.add), wait=[t2])
        bcast11(fcell(D2), dbc[:, 0:1], [("dve", P.count("dve"))])
        bcast11(fcell(TMP1), dbc[:, 1:2], [("dve", P.count("dve"))])
        P.barrier()
        emit_step(2, 12, nm1, xs[1])
        emit_load_fm(xs[1])
        ckt = emit_ck()
        P.wait_only("vector", ckt)
        emit_topk(0, nm2[:, 0, :])
        emit_topk(1, nm2[:, 1, :])
        t3 = emit_novelty_flags(nm2, fcell(D3))
        P.emit("vector", lambda v: v.tensor_tensor(
            fcell(D3), fcell(D3), fcell(D2), ALU.max), wait=[t3])
        P.emit("vector", lambda v: v.tensor_scalar(
            fcell(TMP1), fcell(D3), -1.0, 1.0, ALU.mult, ALU.add))
        bcast11(fcell(D3), dbc[:, 2:3], [("dve", P.count("dve"))])
        bcast11(fcell(TMP1), dbc[:, 3:4], [("dve", P.count("dve"))])
        for bb in range(2):
            P.emit("vector", lambda v, bb=bb: v.tensor_scalar(
                a3m[:, bb, :], nm1[:, bb, :], dbc[:, 0:1], None, ALU.mult))
            P.emit("vector", lambda v, bb=bb: v.scalar_tensor_tensor(
                a3m[:, bb, :], nm2[:, bb, :], dbc[:, 1:2], a3m[:, bb, :],
                ALU.mult, ALU.add))
        P.barrier()
        emit_step(3, 12, a3m, None)


def emit_l2_ffn(P, xin, xout, W, name="l2f"):
    nc = P.nc
    midd = nc.dram_tensor(f"{name}_mid", [NTOK, D], F32, kind="Internal").ap()
    xnd = nc.dram_tensor(f"{name}_xn", [NTOK, D], F32, kind="Internal").ap()
    with contextlib.ExitStack() as pstk:
        sb_ = lambda n, s, d: pstk.enter_context(nc.sbuf_tensor(f"{name}b_{n}", s, d))
        ps_ = lambda n, s, d: pstk.enter_context(nc.psum_tensor(f"{name}b_{n}", s, d))
        r3 = lambda t, c: t.ap().rearrange("p (c t) -> p c t", c=c)
        xr = [sb_("xr0", [128, D], F32), sb_("xr1", [128, D], F32)]
        xnt = [sb_("xnt0", [128, D], F32), sb_("xnt1", [128, D], F32)]
        sb = {
            "xt0": xr[0], "xt1": xr[1],
            "stats": sb_("stats", [128, 12], F32), "mv": sb_("mv", [128, 2], F32),
            "rstd": sb_("rstd", [128, 1], F32), "xntm": sb_("xntm", [128, D], BF16),
            "ident": sb_("ident", [128, 128], BF16),
            "lntmp": sb_("lntmp", [128, D], F32),
        }
        idraw = sb_("idraw", [128, 128], I32)
        ps = {"tr0": ps_("tr0", [128, 1024], BF16).ap()[:, 0:128],
              "tr1": ps_("tr1", [128, 1024], BF16).ap()[:, 0:128]}
        pmm = [ps_("pmm0", [128, 512], F32), ps_("pmm1", [128, 512], F32)]
        pss4 = ps_("pss4", [128, 2048], F32)
        pvv = [pss4.ap()[:, 0:512], pss4.ap()[:, 512:1024]]
        pmb = [pss4.ap()[:, 1024:1536], pss4.ap()[:, 1536:2048]]
        nf = r3(sb_("nf", [128, DC * T], BF16), DC)
        g1b = sb_("g1b", [128, D], F32)
        b1b = sb_("b1b", [128, D], F32)
        g2b, b2b = g1b, b1b
        qkvb = sb_("qkvb", [128, 2 * 2 * DC], F32).ap().rearrange(
            "p (w m c) -> p w m c", w=2, m=2)
        qbsc = sb_("qbsc", [128, 2 * DC], F32).ap().rearrange(
            "p (m c) -> p m c", m=2)
        vbrow = sb_("vbrow", [1, D], BF16)
        borow = sb_("borow", [1, D], BF16)
        bosc = sb_("bosc", [1, D], BF16)
        onescol = sb_("onescol", [128, 1], BF16)
        ones32 = sb_("ones32", [1, 128], F32)
        onesr = sb_("onesr", [1, 128], BF16)
        mrow = sb_("mrow", [1, D], BF16)
        mnfm = sb_("mnfm", [128, DC], BF16)
        rwm = sb_("rwm", [128, DC * 2], BF16).ap().rearrange(
            "p (c e) -> p c e", c=DC)
        rwf = sb_("rwf", [128, DC * 8], BF16).ap().rearrange(
            "p (c e) -> p c e", c=DC)
        rbm = sb_("rbm", [1, 2], F32)
        rbf = sb_("rbf", [1, 8], F32)
        lrow = sb_("lrow", [1, 8], F32)
        prow = sb_("prow", [1, 8], F32)
        jrow = sb_("jrow", [1, 8], F32)
        ei8 = sb_("ei8", [1, 8], F32)
        eim8 = sb_("eim8", [1, 8], F32)
        sc = [sb_("sc" + str(i), [1, 1], F32) for i in range(8)]
        pb = sb_("pb", [128, 4], F32)
        iotac = sb_("iotac", [128, 16], F32)
        idxf = sb_("idxf", [128, 16], F32)
        idxu = sb_("idxu", [128, 16], U32)
        rmax = sb_("rmax", [128, 1], F32)
        nbias = sb_("nbias", [128, 1], F32)
        rsum = sb_("rsum", [128, 1], F32)
        rrec = sb_("rrec", [128, 1], F32)

        emit_identity(P, sb["ident"], idraw)
        P.emit("gpsimd", lambda g: g.memset(onescol[:, :], 1.0))
        P.emit("gpsimd", lambda g: g.memset(ones32[:, :], 1.0))
        P.emit("gpsimd", lambda g: g.memset(onesr[:, :], 1.0))
        P.emit("gpsimd", lambda g: g.iota(ei8[:, :], [[1, 8]], base=0,
                                          channel_multiplier=0,
                                          allow_small_or_imprecise_dtypes=True))
        P.emit("gpsimd", lambda g: g.iota(eim8[:, :], [[1, 8]], base=-4096,
                                          channel_multiplier=0,
                                          allow_small_or_imprecise_dtypes=True))
        P.emit("gpsimd", lambda g: g.iota(iotac[:, :], [[128, 16]], base=0,
                                          channel_multiplier=1,
                                          allow_small_or_imprecise_dtypes=True))
        def load_lnp(which):
            for nm, dst in ((f"l2f_g{which}", g1b), (f"l2f_b{which}", b1b)):
                P.emit("gpsimd", lambda g, nm=nm, dst=dst: g.dma_start(
                    dst[:, :], bass.AP(W[nm].tensor, 0, [[0, 128], [1, D]])),
                    dma="dma_w")
            return ("dma_w", P.count("dma_w"))
        load_lnp(1)
        P.emit("gpsimd", lambda g: g.dma_start(
            qkvb[:, :, :, :],
            W["l2f_qkb"].rearrange("p (w m c) -> p w m c", w=2, m=2)),
            dma="dma_w")

        P.emit("gpsimd", lambda g: g.dma_start(
            rwm[:, :, :], W["l2f_rwm"].rearrange("(c p) e -> p c e", p=128)),
            dma="dma_w")
        P.emit("gpsimd", lambda g: g.dma_start(
            rwf[:, :, :], W["l2f_rwf"].rearrange("(c p) e -> p c e", p=128)),
            dma="dma_w")
        P.emit("gpsimd", lambda g: g.dma_start(rbm[:, :], W["l2f_rbm"][:, :]),
               dma="dma_w")
        P.emit("gpsimd", lambda g: g.dma_start(rbf[:, :], W["l2f_rbf"][:, :]),
               dma="dma_w")
        wt = ("dma_w", P.count("dma_w"))
        P.emit("vector", lambda v: v.tensor_scalar(
            qbsc[:, :, :], qkvb[:, 0, :, :], 0.125, None, ALU.mult), wait=[wt])
        P.wait_only("gpsimd", [wt])
        P.barrier()

        def meanfm(scale, war=(None, None)):
            e1 = P.emit("scalar", lambda s: s.activation(
                mrow[0:1, 0:512], pmb[0][0:1, :], AF.Identity, scale=scale))
            e2 = P.emit("scalar", lambda s: s.activation(
                mrow[0:1, 512:1024], pmb[1][0:1, :], AF.Identity, scale=scale))
            cpt = {0: war[0], 1: war[1]}
            for c in range(DC):
                pslot = c % 2
                ptr = ps[f"tr{pslot}"]
                ttr = P.emit("tensor", lambda t, c=c, ptr=ptr: t.transpose(
                    ptr[:, 0:1], mrow[0:1, c * 128:(c + 1) * 128],
                    sb["ident"][0:1, 0:1]), pe_tok=True,
                    wait=[e1, e2, cpt[pslot]])
                cpt[pslot] = P.emit("vector", lambda v, c=c, ptr=ptr: v.tensor_copy(
                    mnfm[:, c:c + 1], ptr[:, 0:1]), wait=[ttr])
            return cpt

        def bcast_scalar(src11, dstcol, waits):
            tk = P.emit("tensor", lambda t: t.matmul(
                pvv[0][:, 0:1], ones32[:, :], src11[0:1, :], start=True,
                stop=True), pe_tok=True,
                wait=list(waits) + [("dve", P.count("dve"))])
            return P.emit("vector", lambda v: v.tensor_copy(dstcol, pvv[0][:, 0:1]),
                          wait=[tk])

        for b in range(B_CORE):
            if b > 0:
                lnp_tok = load_lnp(1)
                P.wait_only("gpsimd", [lnp_tok])
                P.barrier()
            trtok = {0: None, 1: None}
            cptok = {0: None, 1: None}
            prev_inner = {0: None, 1: None}
            sxtok = {0: None, 1: None}
            last_ttr = None
            last_mm = None
            prev_n1out = None
            for i in range(16):
                slot = i % 2
                xt = sb[f"xt{slot}"]
                row0 = b * T + i * 128
                ld = P.emit("sync", lambda s, xt=xt, row0=row0: s.dma_start(
                    xt[:, :], xin[row0:row0 + 128, :]), dma=f"dma_x{slot}",
                    wait=[prev_inner[slot]])
                t1 = emit_ln_tile(P, xin, row0, xt, ld, sb, xnt[slot][:, :],
                                  eps=1e-5, out_wait=[sxtok[slot],
                                                      prev_inner[slot]])
                sxtok[slot] = P.emit("gpsimd", lambda g, row0=row0, slot=slot:
                                     g.dma_start(xnd[row0:row0 + 128, :],
                                                 xnt[slot][:, :]),
                                     dma=f"dma_sx{slot}", wait=[t1])
                txn = emit_ln_tile(P, xin, row0, xnt[slot], t1, sb,
                                   sb["xntm"][:, :], eps=1e-6, gain=g1b[:, :],
                                   bias_v=b1b[:, :],
                                   out_wait=[last_ttr, last_mm])
                prev_inner[slot] = txn
                for hh in range(2):
                    pmx = pmb[hh]
                    last_mm = P.emit("tensor", lambda t, pmx=pmx, hh=hh, i=i:
                                     t.matmul(
                        pmx[0:1, :], onescol[:, :],
                        sb["xntm"][:, hh * 512:(hh + 1) * 512],
                        start=(i == 0), stop=(i == 15)),
                        pe_tok=(i == 15 and hh == 1), wait=[txn])
                for c in range(DC):
                    pslot = c % 2
                    ptr = ps[f"tr{pslot}"]
                    ttr = P.emit("tensor", lambda t, c=c, ptr=ptr: t.transpose(
                        ptr[:, :], sb["xntm"][:, c * 128:(c + 1) * 128],
                        sb["ident"][:, :]), pe_tok=True, wait=[txn, cptok[pslot]])
                    col0 = i * 128
                    eng = "scalar" if pslot == 0 else "vector"
                    cptok[pslot] = P.emit(
                        eng, lambda e, c=c, ptr=ptr, col0=col0, eng=eng: (
                            e.activation(nf[:, c, col0:col0 + 128], ptr[:, :],
                                         AF.Identity) if eng == "scalar" else
                            e.tensor_copy(nf[:, c, col0:col0 + 128], ptr[:, :])),
                        wait=[ttr])
                    last_ttr = ttr
            cpt = meanfm(1.0 / T, war=(cptok[0], cptok[1]))
            for dc in range(DC):
                tk = P.emit("tensor", lambda t, dc=dc: t.matmul(
                    pmm[0][0:1, 0:2],
                    mnfm[:, dc:dc + 1], rwm[:, dc, :],
                    start=(dc == 0), stop=(dc == DC - 1)),
                    pe_tok=(dc == DC - 1),
                    wait=[cpt[0], cpt[1]] if dc == 0 else [])
            P.emit("vector", lambda v: v.tensor_add(
                lrow[0:1, 0:2], pmm[0][0:1, 0:2], rbm[0:1, :]), wait=[tk])
            P.emit("vector", lambda v: v.tensor_sub(
                sc[0][0:1, :], lrow[0:1, 0:1], lrow[0:1, 1:2]))
            tp0 = P.emit("scalar", lambda s: s.activation(
                sc[1][0:1, :], sc[0][0:1, :], AF.Sigmoid),
                wait=[("dve", P.count("dve"))])
            tp1 = P.emit("scalar", lambda s: s.activation(
                sc[2][0:1, :], sc[0][0:1, :], AF.Sigmoid, scale=-1.0))
            bcast_scalar(sc[1], pb[:, 0:1], [tp0])
            bcast_scalar(sc[2], pb[:, 1:2], [tp1])
            P.wait_only("gpsimd", [sxtok[0], sxtok[1]])
            P.barrier()

            with contextlib.ExitStack() as astk:
                sa_ = lambda n, s, d: astk.enter_context(
                    nc.sbuf_tensor(f"{name}A{b}_{n}", s, d))
                qf = r3(sa_("qf", [128, DC * T], BF16), DC)
                kf = r3(sa_("kf", [128, DC * T], BF16), DC)
                vt = r3(sa_("vt", [128, 16 * D], BF16), 16)
                wbuf = r3(sa_("wbuf", [128, DC * D], BF16), DC)
                psbp = [sa_("psbA", [128, T], BF16), sa_("psbB", [128, T], BF16)]
                ptt2 = r3(sa_("ptt2", [128, 16 * 256], BF16), 16)
                ofm = r3(sa_("ofm", [128, DC * 256], BF16), DC)
                rmaxp = sa_("rmaxp", [128, 2], F32)
                nbp = sa_("nbp", [128, 2], F32)
                rsump = sa_("rsump", [128, 2], F32)
                rrecp = sa_("rrecp", [128, 2], F32)
                mtile = xr
                for m in range(2):
                    P.emit("gpsimd", lambda g, m=m: g.dma_start(
                        vbrow[:, :], W["l2f_vb"][:, m * D:(m + 1) * D]),
                        dma="dma_vb", wait=[("pe", P.count("pe"))])
                    P.emit("gpsimd", lambda g, m=m: g.dma_start(
                        borow[:, :], W["l2f_bo"][:, m * D:(m + 1) * D]),
                        dma="dma_vb")
                    vbt = ("dma_vb", P.count("dma_vb"))
                    evtok = {0: None, 1: None}
                    wo_tok = None
                    for wname, kind, wi in (("l2f_wq", "q", 0), ("l2f_wk", "k", 1),
                                            ("l2f_wv", "v", 2),
                                            ("l2f_wo", "o", 3)):
                        wtok = load_weight_chunks(
                            P, wbuf, W[wname][m * D:(m + 1) * D, :],
                            f"dma_wm")
                        P.wait_only("gpsimd", [wtok])
                        if kind == "o":
                            wo_tok = wtok
                            break
                        if kind == "v":
                            for tc in range(16):
                                for half in range(2):
                                    slot = (tc * 2 + half) % 2
                                    pm = pmm[slot]
                                    for dc in range(DC):
                                        P.emit("tensor", lambda t, pm=pm, tc=tc,
                                               half=half, dc=dc: t.matmul(
                                            pm[:, :],
                                            nf[:, dc, tc * 128:(tc + 1) * 128],
                                            wbuf[:, dc,
                                                 half * 512:(half + 1) * 512],
                                            start=(dc == 0), stop=False),
                                            wait=[evtok[slot], wtok]
                                            if dc == 0 else [])
                                    tk = P.emit("tensor", lambda t, pm=pm,
                                                half=half: t.matmul(
                                        pm[:, :], onesr[0:1, :],
                                        vbrow[0:1, half * 512:(half + 1) * 512],
                                        start=False, stop=True), pe_tok=True,
                                        wait=[vbt])
                                    eng = "scalar" if slot == 0 else "vector"
                                    evtok[slot] = P.emit(
                                        eng, lambda e, pm=pm, tc=tc, half=half,
                                        eng=eng: (
                                            e.activation(
                                                vt[:, tc,
                                                   half * 512:(half + 1) * 512],
                                                pm[:, :], AF.Identity)
                                            if eng == "scalar" else
                                            e.tensor_copy(
                                                vt[:, tc,
                                                   half * 512:(half + 1) * 512],
                                                pm[:, :])), wait=[tk])
                        else:
                            dst = qf if kind == "q" else kf
                            for oc in range(DC):
                                for tg4 in range(4):
                                    slot = (oc * 4 + tg4) % 2
                                    pm = pmm[slot]
                                    for dc in range(DC):
                                        tk = P.emit("tensor", lambda t, pm=pm,
                                                    oc=oc, tg4=tg4, dc=dc:
                                                    t.matmul(
                                            pm[:, :],
                                            wbuf[:, dc, oc * 128:(oc + 1) * 128],
                                            nf[:, dc, tg4 * 512:(tg4 + 1) * 512],
                                            start=(dc == 0), stop=(dc == DC - 1)),
                                            pe_tok=(dc == DC - 1),
                                            wait=[evtok[slot], wtok]
                                            if dc == 0 else [])
                                    if kind == "q":
                                        evtok[slot] = P.emit(
                                            "scalar", lambda s, pm=pm, dst=dst,
                                            oc=oc, tg4=tg4, m=m: s.activation(
                                                dst[:, oc,
                                                    tg4 * 512:(tg4 + 1) * 512],
                                                pm[:, :], AF.Identity,
                                                scale=0.125,
                                                bias=qbsc[:, m, oc:oc + 1]),
                                            wait=[tk])
                                    else:
                                        evtok[slot] = P.emit(
                                            "vector", lambda v, pm=pm, dst=dst,
                                            oc=oc, tg4=tg4, m=m: v.tensor_scalar(
                                                dst[:, oc,
                                                    tg4 * 512:(tg4 + 1) * 512],
                                                pm[:, :],
                                                qkvb[:, 1, m, oc:oc + 1], None,
                                                ALU.add), wait=[tk])
                        P.barrier()
                    tbo = P.emit("vector", lambda v, m=m: v.tensor_scalar(
                        bosc[0:1, :], borow[0:1, :],
                        sc[1 + m][0:1, 0:1], None, ALU.mult),
                        wait=[("dma_vb", P.count("dma_vb"))])
                    ev4 = None
                    sttok = {0: None, 1: None}
                    trtok = {0: None, 1: None}
                    ldm = {0: None, 1: None}
                    oftok = {0: None, 1: None}
                    pxtok = {0: None, 1: None}
                    prev_pv = None
                    prev_op = {0: None, 1: None}
                    last_tp = {0: None, 1: None}
                    last_trp = {0: None, 1: None}
                    for qg in range(8):
                        for h in range(16):
                            rb_ = (64 * h) % 128
                            hc_ = h // 2
                            for jj in range(2):
                                qi = qg * 2 + jj
                                par = qi % 2
                                psb = psbp[par]
                                for kg in range(4):
                                    tk = P.emit(
                                        "tensor", lambda t, rb_=rb_, hc_=hc_,
                                        qi=qi, kg=kg: t.matmul(
                                            pss4.ap()[:, kg * 512:(kg + 1) * 512],
                                            qf[rb_:rb_ + 64, hc_,
                                               qi * 128:(qi + 1) * 128],
                                            kf[rb_:rb_ + 64, hc_,
                                               kg * 512:(kg + 1) * 512],
                                            start=True, stop=True),
                                        pe_tok=(kg == 3),
                                        wait=[ev4] if kg == 0 else [])
                                tmx = P.emit("vector", lambda v, par=par:
                                             v.reduce_max(
                                    rmaxp[:, par:par + 1], pss4.ap()[:, :],
                                    axis=AX.X), wait=[tk])
                                P.emit("vector", lambda v, par=par:
                                       v.tensor_scalar(
                                    nbp[:, par:par + 1], rmaxp[:, par:par + 1],
                                    -1.0, None, ALU.mult))
                                ev4 = P.emit("scalar", lambda s, par=par,
                                             psb=psb: s.activation(
                                    psb[:, :], pss4.ap()[:, :], AF.Exp,
                                    bias=nbp[:, par:par + 1],
                                    accum_out=rsump[:, par:par + 1]),
                                    wait=[("dve", P.count("dve")),
                                          last_tp[par], last_trp[par]])
                                P.emit("vector", lambda v, par=par:
                                       v.reciprocal(rrecp[:, par:par + 1],
                                                    rsump[:, par:par + 1]),
                                       wait=[ev4])
                                tp = P.emit("vector", lambda v, par=par,
                                            psb=psb: v.tensor_scalar(
                                    psb[:, :], psb[:, :],
                                    rrecp[:, par:par + 1], None, ALU.mult))
                                last_tp[par] = tp
                                for kc in range(16):
                                    pslot = kc % 2
                                    ptr = ps[f"tr{pslot}"]
                                    ttr = P.emit(
                                        "tensor", lambda t, kc=kc, ptr=ptr,
                                        psb=psb: t.transpose(
                                            ptr[:, :],
                                            psb[:, kc * 128:(kc + 1) * 128],
                                            sb["ident"][:, :]), pe_tok=True,
                                        wait=[tp, trtok[pslot], prev_pv])
                                    eng = "scalar" if pslot == 0 else "vector"
                                    col0 = jj * 128
                                    trtok[pslot] = P.emit(
                                        eng, lambda e, kc=kc, ptr=ptr,
                                        col0=col0, eng=eng: (
                                            e.activation(
                                                ptt2[:, kc, col0:col0 + 128],
                                                ptr[:, :], AF.Identity)
                                            if eng == "scalar" else
                                            e.tensor_copy(
                                                ptt2[:, kc, col0:col0 + 128],
                                                ptr[:, :])), wait=[ttr])
                                last_trp[par] = ttr
                            pv = pmm[h % 2]
                            vcol = 64 * h
                            for kc in range(16):
                                tk = P.emit("tensor", lambda t, pv=pv, kc=kc,
                                            vcol=vcol: t.matmul(
                                    pv[0:64, 0:256], vt[:, kc, vcol:vcol + 64],
                                    ptt2[:, kc, :], start=(kc == 0),
                                    stop=(kc == 15)), pe_tok=(kc == 15),
                                    wait=[trtok[0], trtok[1], oftok[h % 2]]
                                    if kc == 0 else [])
                            prev_pv = tk
                            eng = "scalar" if h % 2 == 0 else "vector"
                            oftok[h % 2] = P.emit(
                                eng, lambda e, pv=pv, rb_=rb_, hc_=hc_, eng=eng: (
                                    e.activation(ofm[rb_:rb_ + 64, hc_, :],
                                                 pv[0:64, 0:256], AF.Identity)
                                    if eng == "scalar" else
                                    e.tensor_copy(ofm[rb_:rb_ + 64, hc_, :],
                                                  pv[0:64, 0:256])), wait=[tk])
                        for jj in range(2):
                            qi = qg * 2 + jj
                            qslot = qi % 2
                            row0 = b * T + qi * 128
                            if m == 0:
                                ldm[qslot] = P.emit(
                                    "sync", lambda s, x_=mtile[qslot], row0=row0:
                                    s.dma_start(x_[:, :], xnd[row0:row0 + 128, :]),
                                    dma=f"dma_mt{qslot}", wait=[sttok[qslot]])
                            at = {}
                            for half in range(2):
                                pm = pmm[half]
                                wl = [oftok[0], oftok[1], prev_op[half],
                                      pxtok[half], tbo, wo_tok, prev_pv]
                                for dc in range(DC):
                                    P.emit("tensor", lambda t, pm=pm, jj=jj,
                                           half=half, dc=dc: t.matmul(
                                        pm[:, :],
                                        ofm[:, dc, jj * 128:(jj + 1) * 128],
                                        wbuf[:, dc, half * 512:(half + 1) * 512],
                                        start=(dc == 0), stop=False),
                                        wait=wl if dc == 0 else [])
                                tk = P.emit("tensor", lambda t, pm=pm, half=half:
                                            t.matmul(
                                    pm[:, :], onesr[0:1, :],
                                    bosc[0:1, half * 512:(half + 1) * 512],
                                    start=False, stop=True), pe_tok=True)
                                if m == 0:
                                    at[half] = P.emit(
                                        "vector", lambda v, pm=pm, half=half,
                                        x_=mtile[qslot]: v.scalar_tensor_tensor(
                                            x_[:, half * 512:(half + 1) * 512],
                                            pm[:, :], pb[:, 0:1],
                                            x_[:, half * 512:(half + 1) * 512],
                                            ALU.mult, ALU.add),
                                        wait=[tk, ldm[qslot]])
                                else:
                                    at[half] = P.emit(
                                        "vector", lambda v, pm=pm, half=half,
                                        x_=mtile[qslot]: v.tensor_scalar(
                                            x_[:, half * 512:(half + 1) * 512],
                                            pm[:, :], pb[:, 1:2], None,
                                            ALU.mult), wait=[tk, sttok[qslot]])
                                prev_op[half] = at[half]
                                pxtok[half] = at[half]
                            oftok[0] = at[0]
                            oftok[1] = at[1]
                            acc = None if m == 0 else ALU.add
                            sttok[qslot] = P.emit(
                                "gpsimd", lambda g, o=mtile[qslot], row0=row0,
                                acc=acc: g.dma_start(
                                    midd[row0:row0 + 128, :], o[:, :],
                                    accum_op=acc) if acc else g.dma_start(
                                    midd[row0:row0 + 128, :], o[:, :]),
                                dma=f"dma_mo{qslot}", wait=[at[0], at[1]])
                    P.wait_only("gpsimd", [sttok[0], sttok[1]])
                    P.barrier()

            with contextlib.ExitStack() as cstk:
                sc_ = lambda n, s, d: cstk.enter_context(
                    nc.sbuf_tensor(f"{name}C{b}_{n}", s, d))
                xb = [sc_("xb0", [128, D], F32), sc_("xb1", [128, D], F32)]
                lnp_tok = load_lnp(2)
                P.wait_only("gpsimd", [lnp_tok])
                P.barrier()
                trtok = {0: None, 1: None}
                cptok = {0: None, 1: None}
                prev_inner = {0: None, 1: None}
                sttok = {0: None, 1: None}
                last_ttr = None
                last_mm = None
                for i in range(16):
                    slot = i % 2
                    xt = sb[f"xt{slot}"]
                    row0 = b * T + i * 128
                    ld = P.emit("sync", lambda s, xt=xt, row0=row0: s.dma_start(
                        xt[:, :], midd[row0:row0 + 128, :]), dma=f"dma_x{slot}",
                        wait=[prev_inner[slot]])
                    ldx = P.emit("sync", lambda s, x_=xb[slot], row0=row0:
                                 s.dma_start(x_[:, :], xin[row0:row0 + 128, :]),
                                 dma=f"dma_xb{slot}", wait=[sttok[slot]])
                    txn = emit_ln_tile(P, xin, row0, xt, ld, sb,
                                       sb["xntm"][:, :], eps=1e-6,
                                       gain=g2b[:, :], bias_v=b2b[:, :],
                                       out_wait=[last_ttr, last_mm])
                    ta = P.emit("vector", lambda v, xt=xt, x_=xb[slot]:
                                v.tensor_add(x_[:, :], xt[:, :], x_[:, :]),
                                wait=[ld, ldx])
                    prev_inner[slot] = ta
                    sttok[slot] = P.emit("gpsimd", lambda g, o=xb[slot],
                                         row0=row0: g.dma_start(
                        xout[row0:row0 + 128, :], o[:, :]),
                        dma=f"dma_o{slot}", wait=[ta])
                    for hh in range(2):
                        pmx = pmb[hh]
                        last_mm = P.emit("tensor", lambda t, pmx=pmx, hh=hh, i=i:
                                         t.matmul(
                            pmx[0:1, :], onescol[:, :],
                            sb["xntm"][:, hh * 512:(hh + 1) * 512],
                            start=(i == 0), stop=(i == 15)),
                            pe_tok=(i == 15 and hh == 1), wait=[txn])
                    for c in range(DC):
                        pslot = c % 2
                        ptr = ps[f"tr{pslot}"]
                        ttr = P.emit("tensor", lambda t, c=c, ptr=ptr: t.transpose(
                            ptr[:, :], sb["xntm"][:, c * 128:(c + 1) * 128],
                            sb["ident"][:, :]), pe_tok=True,
                            wait=[txn, cptok[pslot]])
                        col0 = i * 128
                        eng = "scalar" if pslot == 0 else "vector"
                        cptok[pslot] = P.emit(
                            eng, lambda e, c=c, ptr=ptr, col0=col0, eng=eng: (
                                e.activation(nf[:, c, col0:col0 + 128],
                                             ptr[:, :], AF.Identity)
                                if eng == "scalar" else
                                e.tensor_copy(nf[:, c, col0:col0 + 128],
                                              ptr[:, :])), wait=[ttr])
                        last_ttr = ttr
                P.wait_only("gpsimd", [sttok[0], sttok[1]])
                cpt = meanfm(1.0 / T, war=(cptok[0], cptok[1]))
                for dc in range(DC):
                    tk = P.emit("tensor", lambda t, dc=dc: t.matmul(
                        pmm[0][0:1, 0:8], mnfm[:, dc:dc + 1], rwf[:, dc, :],
                        start=(dc == 0), stop=(dc == DC - 1)),
                        pe_tok=(dc == DC - 1),
                        wait=[cpt[0], cpt[1]] if dc == 0 else [])
                P.emit("vector", lambda v: v.tensor_add(
                    lrow[0:1, :], pmm[0][0:1, 0:8], rbf[0:1, :]), wait=[tk])
                P.emit("vector", lambda v: v.reduce_max(sc[0][0:1, :],
                                                        lrow[0:1, :], axis=AX.X))
                P.emit("vector", lambda v: v.tensor_scalar(
                    sc[1][0:1, :], sc[0][0:1, :], -1.0, None, ALU.mult))
                tex = P.emit("scalar", lambda s: s.activation(
                    prow[0:1, :], lrow[0:1, :], AF.Exp, bias=sc[1][0:1, :],
                    accum_out=sc[2][0:1, :]), wait=[("dve", P.count("dve"))])
                P.emit("vector", lambda v: v.reciprocal(sc[3][0:1, :],
                                                        sc[2][0:1, :]),
                       wait=[tex])
                P.emit("vector", lambda v: v.tensor_scalar(
                    prow[0:1, :], prow[0:1, :], sc[3][0:1, :], None, ALU.mult))
                for k in range(2):
                    P.emit("vector", lambda v, k=k: v.reduce_max(
                        sc[4 + k][0:1, :], prow[0:1, :], axis=AX.X))
                    P.emit("vector", lambda v, k=k: v.tensor_scalar(
                        jrow[0:1, :], prow[0:1, :], sc[4 + k][0:1, :], None,
                        ALU.is_equal))
                    P.emit("vector", lambda v: v.tensor_mul(
                        jrow[0:1, :], jrow[0:1, :], eim8[0:1, :]))
                    P.emit("vector", lambda v, k=k: v.tensor_scalar(
                        jrow[0:1, :], jrow[0:1, :], 4096.0, None, ALU.add,
                        ALU.min, accum_out=sc[6 + k][0:1, :]))
                    if k == 0:
                        P.emit("vector", lambda v: v.tensor_scalar(
                            jrow[0:1, :], ei8[0:1, :], sc[6][0:1, :], None,
                            ALU.is_equal))
                        P.emit("vector", lambda v: v.scalar_tensor_tensor(
                            prow[0:1, :], jrow[0:1, :], NEGBIG, prow[0:1, :],
                            ALU.mult, ALU.add))
                P.emit("vector", lambda v: v.tensor_add(
                    sc[0][0:1, :], sc[4][0:1, :], sc[5][0:1, :]))
                P.emit("vector", lambda v: v.reciprocal(sc[1][0:1, :],
                                                        sc[0][0:1, :]))
                P.emit("vector", lambda v: v.tensor_mul(
                    sc[2][0:1, :], sc[4][0:1, :], sc[1][0:1, :]))
                tt3 = P.emit("vector", lambda v: v.tensor_mul(
                    sc[3][0:1, :], sc[5][0:1, :], sc[1][0:1, :]))
                bcast_scalar(sc[2], pb[:, 2:3], [tt3])
                bcast_scalar(sc[3], pb[:, 3:4], [tt3])
                P.barrier()

            with contextlib.ExitStack() as dstk:
                sd_ = lambda n, s, d: dstk.enter_context(
                    nc.sbuf_tensor(f"{name}D{b}_{n}", s, d))
                fw1s = r3(sd_("fw1s", [128, DC * 2048], BF16), DC)
                fw2s = r3(sd_("fw2s", [128, 16 * D], BF16), 16)
                hfm = r3(sd_("hfm", [128, 16 * 512], BF16), 16)
                yt = [sd_("yt0", [128, D], F32), sd_("yt1", [128, D], F32)]
                prevacc = {0: None, 1: None}
                pvbank = {0: None, 1: None}
                for k in range(2):
                    bt = bcast_scalar(sc[6 + k], pb[:, 0:1], [])
                    t1 = P.emit("vector", lambda v: v.tensor_scalar(
                        idxf[:, :], iotac[:, :], pb[:, 0:1], None, ALU.bypass)
                        if False else v.tensor_scalar(
                        pb[:, 0:1], pb[:, 0:1], 1024.0, None, ALU.mult),
                        wait=[bt])
                    t2 = P.emit("vector", lambda v: v.tensor_scalar(
                        idxf[:, :], iotac[:, :], pb[:, 0:1], None, ALU.add))
                    t3 = P.emit("vector", lambda v: v.tensor_copy(idxu[:, :],
                                                                  idxf[:, :]))
                    gts = []
                    for c in range(DC):
                        gts.append(P.emit("gpsimd", lambda g, c=c: g.indirect_dma_start(
                            fw1s[:, c, :], None, W["l2f_fw1"],
                            bass.IndirectOffsetOnAxis(ap=idxu[:, c:c + 1],
                                                      axis=0)),
                            dma="dma_g", wait=[t3]))
                    t4 = P.emit("vector", lambda v: v.tensor_scalar(
                        idxf[:, :], idxf[:, :], pb[:, 0:1], None, ALU.add),
                        wait=[("dma_g", P.count("dma_g"))])
                    t5 = P.emit("vector", lambda v: v.tensor_copy(idxu[:, :],
                                                                  idxf[:, :]))
                    for c in range(16):
                        gts.append(P.emit("gpsimd", lambda g, c=c: g.indirect_dma_start(
                            fw2s[:, c, :], None, W["l2f_fw2"],
                            bass.IndirectOffsetOnAxis(ap=idxu[:, c:c + 1],
                                                      axis=0)),
                            dma="dma_g", wait=[t5]))
                    gtok = ("dma_g", P.count("dma_g"))
                    hev = {0: None, 1: None}
                    for tg4 in range(4):
                        for hc in range(16):
                            slot = (tg4 * 16 + hc) % 2
                            pm = pmm[slot]
                            for dc in range(DC):
                                tk = P.emit("tensor", lambda t, pm=pm, hc=hc,
                                            tg4=tg4, dc=dc: t.matmul(
                                    pm[:, :], fw1s[:, dc, hc * 128:(hc + 1) * 128],
                                    nf[:, dc, tg4 * 512:(tg4 + 1) * 512],
                                    start=(dc == 0), stop=(dc == DC - 1)),
                                    pe_tok=(dc == DC - 1),
                                    wait=[hev[slot], gtok] if dc == 0 else [])
                            hev[slot] = P.emit("scalar", lambda s, pm=pm, hc=hc:
                                               s.activation(
                                hfm[:, hc, :], pm[:, :], AF.Gelu), wait=[tk])
                        for j in range(4):
                            tt = tg4 * 4 + j
                            qslot = tt % 2
                            row0 = b * T + tt * 128
                            at = {}
                            for ch in range(2):
                                slot = ch
                                pm = pvv[slot]
                                wl = [hev[0], hev[1], prevacc[qslot],
                                      pvbank[ch]]
                                for fc in range(16):
                                    tk = P.emit("tensor", lambda t, pm=pm, j=j,
                                                ch=ch, fc=fc: t.matmul(
                                        pm[:, :], hfm[:, fc, j * 128:(j + 1) * 128],
                                        fw2s[:, fc, ch * 512:(ch + 1) * 512],
                                        start=(fc == 0), stop=(fc == 15)),
                                        pe_tok=(fc == 15),
                                        wait=wl if fc == 0 else [])
                                at[ch] = P.emit("vector", lambda v, pm=pm, ch=ch,
                                                k=k, y=yt[qslot]: v.tensor_scalar(
                                    y[:, ch * 512:(ch + 1) * 512], pm[:, :],
                                    pb[:, 2 + k:3 + k], None, ALU.mult),
                                    wait=[tk])
                                pvbank[ch] = at[ch]
                            prevacc[qslot] = P.emit(
                                "gpsimd", lambda g, y=yt[qslot], row0=row0:
                                g.dma_start(xout[row0:row0 + 128, :], y[:, :],
                                            accum_op=ALU.add),
                                dma=f"dma_a{qslot}", wait=[at[0], at[1]])
                    P.wait_only("gpsimd", [prevacc[0], prevacc[1]])
                    P.barrier()
            P.barrier()
def emit_l1_ffn(P, xin, xout, W, name="l1f"):
    nc = P.nc
    NE, EH = 16, 512
    with contextlib.ExitStack() as pstk:
        sb_ = lambda n, s, d: pstk.enter_context(nc.sbuf_tensor(f"{name}_{n}", s, d))
        ps_ = lambda n, s, d: pstk.enter_context(nc.psum_tensor(f"{name}_{n}", s, d))
        r3 = lambda t, c: t.ap().rearrange("p (c t) -> p c t", c=c)
        xr = [sb_("xr0", [128, D], F32), sb_("xr1", [128, D], F32)]
        sb = {
            "xt0": xr[0], "xt1": xr[1],
            "stats": sb_("stats", [128, 12], F32), "mv": sb_("mv", [128, 2], F32),
            "rstd": sb_("rstd", [128, 1], F32), "xntm": sb_("xntm", [128, D], BF16),
            "ident": sb_("ident", [128, 128], BF16),
        }
        idraw = sb_("idraw", [128, 128], I32)
        ps = {"tr0": ps_("tr0", [128, 1024], BF16).ap()[:, 0:128],
              "tr1": ps_("tr1", [128, 1024], BF16).ap()[:, 0:128]}
        pmm = [ps_("pmm0", [128, 512], F32), ps_("pmm1", [128, 512], F32)]
        GT = 1024
        NG = NTOK // GT
        NT8 = GT // 128
        xn_g = r3(sb_("xng", [128, DC * GT], BF16), DC)
        hs_g = r3(sb_("hsg", [128, DC * GT], BF16), DC)
        he_g = r3(sb_("heg", [128, 4 * GT], BF16), 4)
        acc = sb_("acc", [128, DC * GT], F32).ap().rearrange(
            "p (tt c) -> p tt c", tt=NT8)
        _wsg = r3(sb_("wsga", [128, DC * 1024], BF16), DC)
        _wsu = r3(sb_("wsua", [128, DC * 1024], BF16), DC)
        _wsd = r3(sb_("wsda", [128, DC * D], BF16), DC)
        wsg_ = [_wsg, _wsg]
        wsu_ = [_wsu, _wsu]
        wsd_ = [_wsd, _wsd]
        weg = [r3(sb_("wega", [128, DC * EH], BF16), DC),
               r3(sb_("wegb", [128, DC * EH], BF16), DC)]
        weu = [r3(sb_("weua", [128, DC * EH], BF16), DC),
               r3(sb_("weub", [128, DC * EH], BF16), DC)]
        wed = [r3(sb_("weda", [128, 4 * D], BF16), 4),
               r3(sb_("wedb", [128, 4 * D], BF16), 4)]
        wg = sb_("wg", [128, DC * NE], BF16).ap().rearrange(
            "p (c e) -> p c e", c=DC)
        wsgc = sb_("wsgc", [128, DC], BF16)
        ei = sb_("ei", [128, NE], F32)
        eim = sb_("eim", [128, NE], F32)
        cwg = sb_("cwg", [128, NT8 * NE], F32).ap().rearrange(
            "p (tt e) -> p tt e", tt=NT8)
        sgc = sb_("sgc", [128, NT8], F32)
        probs = sb_("probs", [128, NE], F32)
        junk = sb_("junk", [128, NE], F32)
        cwraw = sb_("cwraw", [128, NE], F32)
        tmpg = sb_("tmpg", [128, 512], BF16)
        rmax = sb_("rmax", [128, 1], F32)
        nbias = sb_("nbias", [128, 1], F32)
        rsum = sb_("rsum", [128, 1], F32)
        rrec = sb_("rrec", [128, 1], F32)
        idxk = sb_("idxk", [128, 1], F32)
        mk = sb_("mk", [128, 1], F32)

        emit_identity(P, sb["ident"], idraw)
        P.emit("gpsimd", lambda g: g.iota(ei[:, :], [[1, NE]], base=0,
                                          channel_multiplier=0,
                                          allow_small_or_imprecise_dtypes=True))
        P.emit("gpsimd", lambda g: g.iota(eim[:, :], [[1, NE]], base=-4096,
                                          channel_multiplier=0,
                                          allow_small_or_imprecise_dtypes=True))
        P.emit("gpsimd", lambda g: g.dma_start(
            wg[:, :, :], W["l1f_WgT"].rearrange("(c p) e -> p c e", p=128)),
            dma="dma_w")
        P.emit("gpsimd", lambda g: g.dma_start(
            wsgc[:, :], W["l1f_Wsgc"][:, :]), dma="dma_w")
        P.wait_only("gpsimd", [("dma_w", P.count("dma_w"))])
        P.barrier()

        shld = {0: None, 1: None}
        shuse = {0: None, 1: None}
        wld = {0: None, 1: None}
        wuse = {0: None, 1: None}
        for tg in range(NG):
            tok0 = tg * GT
            emit_ln_to_fm(P, xin, xn_g, 0, sb, ps, tiles=NT8, tok0=tok0)
            P.barrier()

            evtok = {0: None, 1: None}
            for tt in range(NT8):
                slot = tt % 2
                pm = pmm[slot]
                for dc in range(DC):
                    tk = P.emit("tensor", lambda t, pm=pm, tt=tt, dc=dc: t.matmul(
                        pm[0:128, 0:NE], xn_g[:, dc, tt * 128:(tt + 1) * 128],
                        wg[:, dc, :], start=(dc == 0), stop=(dc == DC - 1)),
                        pe_tok=(dc == DC - 1), wait=[evtok[slot]] if dc == 0 else [])
                for dc in range(DC):
                    tk2 = P.emit("tensor", lambda t, pm=pm, tt=tt, dc=dc: t.matmul(
                        pm[0:128, 16:17], xn_g[:, dc, tt * 128:(tt + 1) * 128],
                        wsgc[:, dc:dc + 1], start=(dc == 0), stop=(dc == DC - 1)),
                        pe_tok=(dc == DC - 1))
                tsg = P.emit("scalar", lambda s, pm=pm, tt=tt: s.activation(
                    sgc[:, tt:tt + 1], pm[0:128, 16:17], AF.Sigmoid),
                    wait=[tk2])
                P.emit("vector", lambda v, pm=pm: v.reduce_max(
                    rmax[:, :], pm[0:128, 0:NE], axis=AX.X), wait=[tk])
                P.emit("vector", lambda v: v.tensor_scalar(
                    nbias[:, :], rmax[:, :], -1.0, None, ALU.mult))
                tex = P.emit("scalar", lambda s, pm=pm: s.activation(
                    probs[:, :], pm[0:128, 0:NE], AF.Exp, bias=nbias[:, :],
                    accum_out=rsum[:, :]), wait=[("dve", P.count("dve"))])
                evtok[slot] = tex
                P.emit("vector", lambda v: v.reciprocal(rrec[:, :], rsum[:, :]),
                       wait=[tex])
                P.emit("vector", lambda v: v.tensor_scalar(
                    probs[:, :], probs[:, :], rrec[:, :], None, ALU.mult))
                for k in range(2):
                    P.emit("vector", lambda v: v.reduce_max(mk[:, :], probs[:, :],
                                                            axis=AX.X))
                    P.emit("vector", lambda v: v.tensor_scalar(
                        junk[:, :], probs[:, :], mk[:, :], None, ALU.is_equal))
                    P.emit("vector", lambda v: v.tensor_mul(junk[:, :], junk[:, :],
                                                            eim[:, :]))
                    P.emit("vector", lambda v: v.tensor_scalar(
                        junk[:, :], junk[:, :], 4096.0, None, ALU.add, ALU.min,
                        accum_out=idxk[:, :]))
                    P.emit("vector", lambda v: v.tensor_scalar(
                        junk[:, :], ei[:, :], idxk[:, :], None, ALU.is_equal))
                    if k == 0:
                        P.emit("vector", lambda v: v.tensor_scalar(
                            cwraw[:, :], junk[:, :], mk[:, :], None, ALU.mult))
                        P.emit("vector", lambda v: v.scalar_tensor_tensor(
                            probs[:, :], junk[:, :], NEGBIG, probs[:, :],
                            ALU.mult, ALU.add))
                    else:
                        P.emit("vector", lambda v: v.scalar_tensor_tensor(
                            cwraw[:, :], junk[:, :], mk[:, :], cwraw[:, :],
                            ALU.mult, ALU.add))
                P.emit("vector", lambda v: v.tensor_scalar(
                    junk[:, 0:1], cwraw[:, 0:1], 0.0, None, ALU.mult, ALU.add,
                    accum_out=rsum[:, :]) if False else v.reduce_sum(
                    rsum[:, :], cwraw[:, :], axis=AX.X))
                P.emit("vector", lambda v: v.reciprocal(rrec[:, :], rsum[:, :]))
                P.emit("vector", lambda v, tt=tt: v.tensor_scalar(
                    cwg[:, tt, :], cwraw[:, :], rrec[:, :], None, ALU.mult))

            hevtok = {0: evtok[0], 1: evtok[1]}
            acctok = {0: None, 1: None}
            first_half = True
            for sh in range(2):
                ss = sh % 2
                h0 = sh * 1024
                l1 = P.emit("gpsimd", lambda g, ss=ss, h0=h0: g.dma_start(
                    wsg_[ss][:, :, :], W["l1f_WsgateT"][:, h0:h0 + 1024]
                    .rearrange("(c p) h -> p c h", p=128)), dma="dma_s",
                    wait=[shuse[0], shuse[1], hevtok[0], hevtok[1]])
                l2 = P.emit("gpsimd", lambda g, ss=ss, h0=h0: g.dma_start(
                    wsu_[ss][:, :, :], W["l1f_WsupT"][:, h0:h0 + 1024]
                    .rearrange("(c p) h -> p c h", p=128)), dma="dma_s")
                l3 = P.emit("gpsimd", lambda g, ss=ss, h0=h0: g.dma_start(
                    wsd_[ss][:, :, :], W["l1f_WsdownT"][h0:h0 + 1024, :]
                    .rearrange("(c p) h -> p c h", p=128)), dma="dma_s")
                sldt = ("dma_s", P.count("dma_s"))
                for gh in range(GT // 512):
                    for hc in range(DC):
                        pmg, pmu = pmm[0], pmm[1]
                        for dc in range(DC):
                            tkg = P.emit("tensor", lambda t, pmg=pmg, hc=hc, gh=gh,
                                         dc=dc, ss=ss: t.matmul(
                                pmg[:, :], wsg_[ss][:, dc, hc * 128:(hc + 1) * 128],
                                xn_g[:, dc, gh * 512:(gh + 1) * 512],
                                start=(dc == 0), stop=(dc == DC - 1)),
                                pe_tok=(dc == DC - 1),
                                wait=[hevtok[0], sldt] if dc == 0 else [])
                        tg_e = P.emit("scalar", lambda s, pmg=pmg: s.activation(
                            tmpg[:, :], pmg[:, :], AF.Sigmoid), wait=[tkg])
                        tg_e = P.emit("vector", lambda v, pmg=pmg: v.tensor_tensor(
                            tmpg[:, :], tmpg[:, :], pmg[:, :], ALU.mult),
                            wait=[tg_e])
                        for dc in range(DC):
                            tku = P.emit("tensor", lambda t, pmu=pmu, hc=hc, gh=gh,
                                         dc=dc, ss=ss: t.matmul(
                                pmu[:, :], wsu_[ss][:, dc, hc * 128:(hc + 1) * 128],
                                xn_g[:, dc, gh * 512:(gh + 1) * 512],
                                start=(dc == 0), stop=(dc == DC - 1)),
                                pe_tok=(dc == DC - 1),
                                wait=[hevtok[1]] if dc == 0 else [])
                        hevtok[0] = P.emit("vector", lambda v, pmu=pmu, hc=hc,
                                           gh=gh: v.tensor_mul(
                            hs_g[:, hc, gh * 512:(gh + 1) * 512], pmu[:, :],
                            tmpg[:, :]), wait=[tku, tg_e])
                        hevtok[1] = hevtok[0]
                for tt in range(NT8):
                    for ch in range(2):
                        slot = (tt * 2 + ch) % 2
                        pm = pmm[slot]
                        wl = [hevtok[0], hevtok[1], acctok[slot]]
                        for hc in range(DC):
                            tk = P.emit("tensor", lambda t, pm=pm, tt=tt, ch=ch,
                                        hc=hc, ss=ss: t.matmul(
                                pm[:, :], hs_g[:, hc, tt * 128:(tt + 1) * 128],
                                wsd_[ss][:, hc, ch * 512:(ch + 1) * 512],
                                start=(hc == 0), stop=(hc == DC - 1)),
                                pe_tok=(hc == DC - 1), wait=wl if hc == 0 else [])
                        shuse[ss] = tk
                        if first_half:
                            acctok[slot] = P.emit(
                                "vector", lambda v, pm=pm, tt=tt, ch=ch:
                                v.tensor_scalar(
                                    acc[:, tt, ch * 512:(ch + 1) * 512], pm[:, :],
                                    sgc[:, tt:tt + 1], None, ALU.mult),
                                wait=[tk])
                        else:
                            acctok[slot] = P.emit(
                                "vector", lambda v, pm=pm, tt=tt, ch=ch:
                                v.scalar_tensor_tensor(
                                    acc[:, tt, ch * 512:(ch + 1) * 512], pm[:, :],
                                    sgc[:, tt:tt + 1],
                                    acc[:, tt, ch * 512:(ch + 1) * 512],
                                    ALU.mult, ALU.add), wait=[tk])
                first_half = False
                hevtok[0] = acctok[0]
                hevtok[1] = acctok[1]

            for e in range(NE):
                es = e % 2
                w1t = P.emit("gpsimd", lambda g, es=es, e=e: g.dma_start(
                    weg[es][:, :, :], W["l1f_WegT"][e * D:(e + 1) * D, :]
                    .rearrange("(c p) h -> p c h", p=128)), dma=f"dma_e{es}",
                    wait=[wuse[es]])
                P.emit("gpsimd", lambda g, es=es, e=e: g.dma_start(
                    weu[es][:, :, :], W["l1f_WeuT"][e * D:(e + 1) * D, :]
                    .rearrange("(c p) h -> p c h", p=128)), dma=f"dma_e{es}")
                P.emit("gpsimd", lambda g, es=es, e=e: g.dma_start(
                    wed[es][:, :, :], W["l1f_WedT"][e * EH:(e + 1) * EH, :]
                    .rearrange("(c p) h -> p c h", p=128)), dma=f"dma_e{es}")
                wldt = (f"dma_e{es}", P.count(f"dma_e{es}"))
                for gh in range(GT // 512):
                    for hc in range(4):
                        pmg, pmu = pmm[0], pmm[1]
                        for dc in range(DC):
                            tkg = P.emit("tensor", lambda t, pmg=pmg, hc=hc, gh=gh,
                                         dc=dc, es=es: t.matmul(
                                pmg[:, :], weg[es][:, dc, hc * 128:(hc + 1) * 128],
                                xn_g[:, dc, gh * 512:(gh + 1) * 512],
                                start=(dc == 0), stop=(dc == DC - 1)),
                                pe_tok=(dc == DC - 1),
                                wait=[hevtok[0], wldt] if dc == 0 else [])
                        tg_e = P.emit("scalar", lambda s, pmg=pmg: s.activation(
                            tmpg[:, :], pmg[:, :], AF.Sigmoid), wait=[tkg])
                        tg_e = P.emit("vector", lambda v, pmg=pmg: v.tensor_tensor(
                            tmpg[:, :], tmpg[:, :], pmg[:, :], ALU.mult),
                            wait=[tg_e])
                        for dc in range(DC):
                            tku = P.emit("tensor", lambda t, pmu=pmu, hc=hc, gh=gh,
                                         dc=dc, es=es: t.matmul(
                                pmu[:, :], weu[es][:, dc, hc * 128:(hc + 1) * 128],
                                xn_g[:, dc, gh * 512:(gh + 1) * 512],
                                start=(dc == 0), stop=(dc == DC - 1)),
                                pe_tok=(dc == DC - 1),
                                wait=[hevtok[1]] if dc == 0 else [])
                        hevtok[0] = P.emit("vector", lambda v, pmu=pmu, hc=hc,
                                           gh=gh: v.tensor_mul(
                            he_g[:, hc, gh * 512:(gh + 1) * 512], pmu[:, :],
                            tmpg[:, :]), wait=[tku, tg_e])
                        hevtok[1] = hevtok[0]
                for tt in range(NT8):
                    for ch in range(2):
                        slot = (tt * 2 + ch) % 2
                        pm = pmm[slot]
                        wl = [hevtok[0], hevtok[1], acctok[slot]]
                        for hc in range(4):
                            tk = P.emit("tensor", lambda t, pm=pm, tt=tt, ch=ch,
                                        hc=hc, es=es: t.matmul(
                                pm[:, :], he_g[:, hc, tt * 128:(tt + 1) * 128],
                                wed[es][:, hc, ch * 512:(ch + 1) * 512],
                                start=(hc == 0), stop=(hc == 3)),
                                pe_tok=(hc == 3), wait=wl if hc == 0 else [])
                        wuse[es] = tk
                        acctok[slot] = P.emit(
                            "vector", lambda v, pm=pm, tt=tt, ch=ch, e=e:
                            v.scalar_tensor_tensor(
                                acc[:, tt, ch * 512:(ch + 1) * 512], pm[:, :],
                                cwg[:, tt, e:e + 1],
                                acc[:, tt, ch * 512:(ch + 1) * 512],
                                ALU.mult, ALU.add), wait=[tk])
                hevtok[0] = acctok[0]
                hevtok[1] = acctok[1]

            sttok = {0: None, 1: None}
            for tt in range(NT8):
                qslot = tt % 2
                row0 = tok0 + tt * 128
                ldx = P.emit("sync", lambda s, x_=xr[qslot], row0=row0: s.dma_start(
                    x_[:, :], xin[row0:row0 + 128, :]), dma=f"dma_xr{qslot}",
                    wait=[sttok[qslot]])
                at = P.emit("vector", lambda v, x_=xr[qslot], tt=tt: v.tensor_add(
                    x_[:, :], acc[:, tt, :], x_[:, :]),
                    wait=[ldx, acctok[0], acctok[1]])
                sttok[qslot] = P.emit("gpsimd", lambda g, o=xr[qslot], row0=row0:
                                      g.dma_start(xout[row0:row0 + 128, :],
                                                  o[:, :]),
                                      dma=f"dma_o{qslot}", wait=[at])
            P.wait_only("gpsimd", [sttok[0], sttok[1]])
            P.barrier()


def emit_l3_ffn(P, xin, xout, W, name="l3f"):
    nc = P.nc
    NE, HH = 15, 1024
    with contextlib.ExitStack() as pstk:
        sb_ = lambda n, s, d: pstk.enter_context(nc.sbuf_tensor(f"{name}_{n}", s, d))
        ps_ = lambda n, s, d: pstk.enter_context(nc.psum_tensor(f"{name}_{n}", s, d))
        r3 = lambda t, c: t.ap().rearrange("p (c t) -> p c t", c=c)
        xr = [sb_("xr0", [128, D], F32), sb_("xr1", [128, D], F32)]
        sb = {
            "xt0": xr[0], "xt1": xr[1],
            "stats": sb_("stats", [128, 12], F32), "mv": sb_("mv", [128, 2], F32),
            "rstd": sb_("rstd", [128, 1], F32), "xntm": sb_("xntm", [128, D], BF16),
            "ident": sb_("ident", [128, 128], BF16),
        }
        idraw = sb_("idraw", [128, 128], I32)
        ps = {"tr0": ps_("tr0", [128, 1024], BF16).ap()[:, 0:128],
              "tr1": ps_("tr1", [128, 1024], BF16).ap()[:, 0:128]}
        pmm = [ps_("pmm0", [128, 512], F32), ps_("pmm1", [128, 512], F32)]
        GT = 1024
        NG = NTOK // GT
        xn_g = r3(sb_("xng", [128, DC * GT], BF16), DC)
        hs_g = r3(sb_("hsg", [128, DC * GT], BF16), DC)
        he_g = r3(sb_("heg", [128, DC * GT], BF16), DC)
        acc = sb_("acc", [128, DC * GT], F32).ap().rearrange(
            "p (tt c) -> p tt c", tt=GT // 128)
        sw1 = r3(sb_("sw1", [128, DC * HH], BF16), DC)
        sw2 = r3(sb_("sw2", [128, DC * D], BF16), DC)
        we1 = [r3(sb_("we1a", [128, DC * HH], BF16), DC),
               r3(sb_("we1b", [128, DC * HH], BF16), DC)]
        we2 = [r3(sb_("we2a", [128, DC * D], BF16), DC),
               r3(sb_("we2b", [128, DC * D], BF16), DC)]
        rw = sb_("rw", [128, DC * NE], BF16).ap().rearrange(
            "p (c e) -> p c e", c=DC)
        rbb = sb_("rbb", [128, NE], F32)
        eb1s = sb_("eb1s", [128, NE * DC], F32).ap().rearrange(
            "p (e c) -> p e c", e=NE)
        eb2c = [sb_("eb2c0", [1, D], BF16), sb_("eb2c1", [1, D], BF16)]
        sb1s = sb_("sb1s", [128, DC], F32)
        sb2r = sb_("sb2r", [1, D], BF16)
        onesr = sb_("onesr", [1, 512], BF16)
        ei = sb_("ei", [128, NE], F32)
        eim = sb_("eim", [128, NE], F32)
        cwg = sb_("cwg", [128, (GT // 128) * NE], F32).ap().rearrange(
            "p (tt e) -> p tt e", tt=GT // 128)
        logit = sb_("logit", [128, NE], F32)
        probs = sb_("probs", [128, NE], F32)
        junk = sb_("junk", [128, NE], F32)
        rmax = sb_("rmax", [128, 1], F32)
        nbias = sb_("nbias", [128, 1], F32)
        rsum = sb_("rsum", [128, 1], F32)
        rrec = sb_("rrec", [128, 1], F32)
        idxk = sb_("idxk", [128, 1], F32)
        mk = sb_("mk", [128, 1], F32)

        emit_identity(P, sb["ident"], idraw)
        P.emit("gpsimd", lambda g: g.iota(ei[:, :], [[1, NE]], base=0,
                                          channel_multiplier=0,
                                          allow_small_or_imprecise_dtypes=True))
        P.emit("gpsimd", lambda g: g.iota(eim[:, :], [[1, NE]], base=-4096,
                                          channel_multiplier=0,
                                          allow_small_or_imprecise_dtypes=True))
        P.emit("gpsimd", lambda g: g.memset(onesr[:, :], 1.0))
        load_weight_chunks(P, sw1, W["l3f_sW1"], "dma_w")
        load_weight_chunks(P, sw2, W["l3f_sW2"], "dma_w")
        P.emit("gpsimd", lambda g: g.dma_start(
            rw[:, :, :], W["l3f_rW"].rearrange("(c p) e -> p c e", p=128)),
            dma="dma_w")
        P.emit("gpsimd", lambda g: g.dma_start(
            rbb[:, :], bass.AP(W["l3f_rb"].tensor, 0, [[0, 128], [1, NE]])),
            dma="dma_w")
        P.emit("gpsimd", lambda g: g.dma_start(
            eb1s[:, :, :], W["l3f_eb1"].rearrange("p (e c) -> p e c", e=NE)),
            dma="dma_w")
        P.emit("gpsimd", lambda g: g.dma_start(
            sb1s[:, :], W["l3f_sb1"][:, :]), dma="dma_w")
        P.emit("gpsimd", lambda g: g.dma_start(sb2r[:, :], W["l3f_sb2"][None, :]),
            dma="dma_w")
        wtok = ("dma_w", P.count("dma_w"))
        P.wait_only("gpsimd", [wtok])
        P.barrier()

        wld = {0: None, 1: None}
        wuse = {0: None, 1: None}
        for tg in range(NG):
            tok0 = tg * GT
            emit_ln_to_fm(P, xin, xn_g, 0, sb, ps, tiles=GT // 128, tok0=tok0)
            P.barrier()

            evtok = {0: None, 1: None}
            for tt in range(GT // 128):
                slot = tt % 2
                pm = pmm[slot]
                for dc in range(DC):
                    tk = P.emit("tensor", lambda t, pm=pm, tt=tt, dc=dc: t.matmul(
                        pm[0:128, 0:NE], xn_g[:, dc, tt * 128:(tt + 1) * 128],
                        rw[:, dc, :], start=(dc == 0), stop=(dc == DC - 1)),
                        pe_tok=(dc == DC - 1), wait=[evtok[slot]] if dc == 0 else [])
                tl = P.emit("vector", lambda v, pm=pm: v.tensor_add(
                    logit[:, :], pm[0:128, 0:NE], rbb[:, :]), wait=[tk])
                evtok[slot] = tl
                P.emit("vector", lambda v: v.reduce_max(rmax[:, :], logit[:, :],
                                                        axis=AX.X))
                P.emit("vector", lambda v: v.tensor_scalar(
                    nbias[:, :], rmax[:, :], -1.0, None, ALU.mult))
                tex = P.emit("scalar", lambda s: s.activation(
                    probs[:, :], logit[:, :], AF.Exp, bias=nbias[:, :],
                    accum_out=rsum[:, :]), wait=[("dve", P.count("dve"))])
                P.emit("vector", lambda v: v.reciprocal(rrec[:, :], rsum[:, :]),
                       wait=[tex])
                P.emit("vector", lambda v: v.tensor_scalar(
                    probs[:, :], probs[:, :], rrec[:, :], None, ALU.mult))
                for k in range(3):
                    P.emit("vector", lambda v: v.reduce_max(mk[:, :], probs[:, :],
                                                            axis=AX.X))
                    P.emit("vector", lambda v: v.tensor_scalar(
                        junk[:, :], probs[:, :], mk[:, :], None, ALU.is_equal))
                    P.emit("vector", lambda v: v.tensor_mul(junk[:, :], junk[:, :],
                                                            eim[:, :]))
                    P.emit("vector", lambda v: v.tensor_scalar(
                        junk[:, :], junk[:, :], 4096.0, None, ALU.add, ALU.min,
                        accum_out=idxk[:, :]))
                    P.emit("vector", lambda v: v.tensor_scalar(
                        junk[:, :], ei[:, :], idxk[:, :], None, ALU.is_equal))
                    if k == 0:
                        P.emit("vector", lambda v, tt=tt: v.tensor_scalar(
                            cwg[:, tt, :], junk[:, :], mk[:, :], None, ALU.mult))
                    else:
                        P.emit("vector", lambda v, tt=tt: v.scalar_tensor_tensor(
                            cwg[:, tt, :], junk[:, :], mk[:, :], cwg[:, tt, :],
                            ALU.mult, ALU.add))
                    if k < 2:
                        P.emit("vector", lambda v: v.scalar_tensor_tensor(
                            probs[:, :], junk[:, :], NEGBIG, probs[:, :],
                            ALU.mult, ALU.add))

            hevtok = {0: evtok[0], 1: evtok[1]}
            for gh in range(GT // 512):
                for hc in range(DC):
                    slot = (gh * DC + hc) % 2
                    pm = pmm[slot]
                    for dc in range(DC):
                        tk = P.emit("tensor", lambda t, pm=pm, hc=hc, gh=gh, dc=dc:
                                    t.matmul(
                            pm[:, :], sw1[:, dc, hc * 128:(hc + 1) * 128],
                            xn_g[:, dc, gh * 512:(gh + 1) * 512],
                            start=(dc == 0), stop=(dc == DC - 1)),
                            pe_tok=(dc == DC - 1),
                            wait=[hevtok[slot]] if dc == 0 else [])
                    hevtok[slot] = P.emit("vector", lambda v, pm=pm, hc=hc, gh=gh:
                                          v.tensor_scalar(
                        hs_g[:, hc, gh * 512:(gh + 1) * 512], pm[:, :],
                        sb1s[:, hc:hc + 1], 0.0, ALU.add, ALU.max), wait=[tk])
            for tt in range(GT // 128):
                for ch in range(2):
                    slot = (tt * 2 + ch) % 2
                    pm = pmm[slot]
                    wl = [hevtok[0], hevtok[1], evtok[0], evtok[1]]
                    for hc in range(DC):
                        tk = P.emit("tensor", lambda t, pm=pm, tt=tt, ch=ch, hc=hc:
                                    t.matmul(
                            pm[:, :], hs_g[:, hc, tt * 128:(tt + 1) * 128],
                            sw2[:, hc, ch * 512:(ch + 1) * 512],
                            start=(hc == 0), stop=False),
                            wait=wl if hc == 0 else [])
                    tk = P.emit("tensor", lambda t, pm=pm, ch=ch: t.matmul(
                        pm[:, :], onesr[0:1, 0:128],
                        sb2r[0:1, ch * 512:(ch + 1) * 512],
                        start=False, stop=True), pe_tok=True)
                    hevtok[slot] = P.emit("vector", lambda v, pm=pm, tt=tt, ch=ch:
                                          v.tensor_copy(
                        acc[:, tt, ch * 512:(ch + 1) * 512], pm[:, :]), wait=[tk])

            for e in range(NE):
                es = e % 2
                w1t = P.emit("gpsimd", lambda g, es=es, e=e: g.dma_start(
                    we1[es][:, :, :],
                    W["l3f_eW1"][e * D:(e + 1) * D, :].rearrange(
                        "(c p) h -> p c h", p=128)), dma=f"dma_e{es}",
                    wait=[wuse[es]])
                w2t = P.emit("gpsimd", lambda g, es=es, e=e: g.dma_start(
                    we2[es][:, :, :],
                    W["l3f_eW2"][e * HH:(e + 1) * HH, :].rearrange(
                        "(c p) h -> p c h", p=128)), dma=f"dma_e{es}")
                P.emit("gpsimd", lambda g, es=es, e=e: g.dma_start(
                    eb2c[es][:, :], W["l3f_eb2"][:, e * D:(e + 1) * D]),
                    dma=f"dma_e{es}")
                wldt = ("dma_e" + str(es), P.count(f"dma_e{es}"))
                for gh in range(GT // 512):
                    for hc in range(DC):
                        slot = (gh * DC + hc) % 2
                        pm = pmm[slot]
                        for dc in range(DC):
                            tk = P.emit("tensor", lambda t, pm=pm, hc=hc, gh=gh,
                                        dc=dc, es=es: t.matmul(
                                pm[:, :], we1[es][:, dc, hc * 128:(hc + 1) * 128],
                                xn_g[:, dc, gh * 512:(gh + 1) * 512],
                                start=(dc == 0), stop=(dc == DC - 1)),
                                pe_tok=(dc == DC - 1),
                                wait=[hevtok[slot], wldt] if dc == 0 else [])
                        hevtok[slot] = P.emit(
                            "vector", lambda v, pm=pm, hc=hc, gh=gh, e=e:
                            v.tensor_scalar(
                                he_g[:, hc, gh * 512:(gh + 1) * 512], pm[:, :],
                                eb1s[:, e, hc:hc + 1], 0.0, ALU.add, ALU.max),
                            wait=[tk])
                for tt in range(GT // 128):
                    for ch in range(2):
                        slot = (tt * 2 + ch) % 2
                        pm = pmm[slot]
                        wl = [hevtok[0], hevtok[1]]
                        for hc in range(DC):
                            tk = P.emit("tensor", lambda t, pm=pm, tt=tt, ch=ch,
                                        hc=hc, es=es: t.matmul(
                                pm[:, :], he_g[:, hc, tt * 128:(tt + 1) * 128],
                                we2[es][:, hc, ch * 512:(ch + 1) * 512],
                                start=(hc == 0), stop=False),
                                wait=wl if hc == 0 else [])
                        tk = P.emit("tensor", lambda t, pm=pm, ch=ch, es=es: t.matmul(
                            pm[:, :], onesr[0:1, 0:128],
                            eb2c[es][0:1, ch * 512:(ch + 1) * 512],
                            start=False, stop=True), pe_tok=True)
                        wuse[es] = tk
                        hevtok[slot] = P.emit(
                            "vector", lambda v, pm=pm, tt=tt, ch=ch, e=e:
                            v.scalar_tensor_tensor(
                                acc[:, tt, ch * 512:(ch + 1) * 512], pm[:, :],
                                cwg[:, tt, e:e + 1],
                                acc[:, tt, ch * 512:(ch + 1) * 512],
                                ALU.mult, ALU.add), wait=[tk])

            sttok = {0: None, 1: None}
            for tt in range(GT // 128):
                qslot = tt % 2
                row0 = tok0 + tt * 128
                ldx = P.emit("sync", lambda s, x_=xr[qslot], row0=row0: s.dma_start(
                    x_[:, :], xin[row0:row0 + 128, :]), dma=f"dma_xr{qslot}",
                    wait=[sttok[qslot]])
                at = P.emit("vector", lambda v, x_=xr[qslot], tt=tt: v.tensor_add(
                    x_[:, :], acc[:, tt, :], x_[:, :]), wait=[ldx])
                sttok[qslot] = P.emit("gpsimd", lambda g, o=xr[qslot], row0=row0:
                                      g.dma_start(xout[row0:row0 + 128, :],
                                                  o[:, :]),
                                      dma=f"dma_o{qslot}", wait=[at])
            P.wait_only("gpsimd", [sttok[0], sttok[1]])
            P.barrier()


def _ct(a):
    return np.ascontiguousarray(np.asarray(a))


def prep_l2_attn(ins):
    return {"l2_WqT": _ct(np.asarray(ins["L2_attn_Wq"]).T),
            "l2_WkT": _ct(np.asarray(ins["L2_attn_Wk"]).T),
            "l2_WvT": _ct(np.asarray(ins["L2_attn_Wv"]).T)}


DECL_L2_ATTN = [("l2_WqT", [D, D]), ("l2_WkT", [D, D]), ("l2_WvT", [D, D])]


def prep_l1_attn(ins):
    perm = np.concatenate([np.arange(0, D, 2), np.arange(1, D, 2)])
    inv = 1.0 / (10000.0 ** (np.arange(0, D, 2, dtype=np.float64) / D))
    ang = np.outer(np.arange(T, dtype=np.float64), inv)
    ropecs = np.concatenate([np.cos(ang), np.sin(ang)], axis=1).astype(np.float32)
    return {"l1_wqT": _ct(np.asarray(ins["L1_attn_wq"]).T[:, perm]),
            "l1_wkT": _ct(np.asarray(ins["L1_attn_wk"]).T[:, perm]),
            "l1_wvT": _ct(np.asarray(ins["L1_attn_wv"]).T),
            "l1_woT": _ct(np.asarray(ins["L1_attn_wo"]).T),
            "l1_ropecs": ropecs}


DECL_L1_ATTN = [("l1_wqT", [D, D]), ("l1_wkT", [D, D]), ("l1_wvT", [D, D]),
                ("l1_woT", [D, D]), ("l1_ropecs", [T, D])]


def prep_l0_attn(ins):
    i = np.arange(128)[:, None]
    j = np.arange(128)[None, :]
    causal = np.where(j <= i, 0.0, NEGBIG).astype(np.float32)
    edge = np.where(i < j, 0.0, NEGBIG).astype(np.float32)
    return {"l0_wqkv": _ct(ins["L0_attn_w_qkv_full"]),
            "l0_wqkr": _ct(ins["L0_attn_w_qk_red"]),
            "l0_wvr": _ct(ins["L0_attn_w_v_red"]),
            "l0_wproj": _ct(ins["L0_attn_w_proj"]),
            "l0_masks": np.concatenate([causal, edge], axis=0)}


DECL_L0_ATTN = [("l0_wqkv", [D, 1536]), ("l0_wqkr", [D, 512]),
                ("l0_wvr", [D, 512]), ("l0_wproj", [D, D]),
                ("l0_masks", [256, 128])]

def prep_l3_ffn(ins):
    return {"l3f_sW1": _ct(ins["L3_ffn_sW1"]),
            "l3f_sW2": _ct(ins["L3_ffn_sW2"]),
            "l3f_rW": _ct(ins["L3_ffn_router_W"]),
            "l3f_rb": _ct(np.asarray(ins["L3_ffn_router_b"]).reshape(1, 15)),
            "l3f_eW1": _ct(np.asarray(ins["L3_ffn_eW1"]).reshape(15 * D, D)),
            "l3f_eW2": _ct(np.asarray(ins["L3_ffn_eW2"]).reshape(15 * D, D)),
            "l3f_eb1": _ct(np.asarray(ins["L3_ffn_eb1"]).reshape(15, 8, 128)
                           .transpose(2, 0, 1).reshape(128, 15 * 8)),
            "l3f_eb2": _ct(np.asarray(ins["L3_ffn_eb2"]).reshape(1, 15 * D)),
            "l3f_sb1": _ct(np.asarray(ins["L3_ffn_sb1"]).reshape(8, 128).T),
            "l3f_sb2": _ct(np.asarray(ins["L3_ffn_sb2"]).reshape(1, D))}


DECL_L3_FFN = [("l3f_sW1", [D, D]), ("l3f_sW2", [D, D]), ("l3f_rW", [D, 15]),
               ("l3f_rb", [1, 15]), ("l3f_eW1", [15 * D, D]),
               ("l3f_eW2", [15 * D, D]), ("l3f_eb1", [128, 15 * 8]),
               ("l3f_eb2", [1, 15 * D]), ("l3f_sb1", [128, 8]), ("l3f_sb2", [1, D])]

def prep_l1_ffn(ins):
    weg = np.asarray(ins["L1_ffn_We_gate"]).transpose(0, 2, 1)
    weu = np.asarray(ins["L1_ffn_We_up"]).transpose(0, 2, 1)
    wed = np.asarray(ins["L1_ffn_We_down"]).transpose(0, 2, 1)
    return {"l1f_WgT": _ct(np.asarray(ins["L1_ffn_Wg"]).T),
            "l1f_Wsgc": _ct(np.asarray(ins["L1_ffn_Wsg"]).reshape(8, 128).T),
            "l1f_WsgateT": _ct(np.asarray(ins["L1_ffn_Ws_gate"]).T),
            "l1f_WsupT": _ct(np.asarray(ins["L1_ffn_Ws_up"]).T),
            "l1f_WsdownT": _ct(np.asarray(ins["L1_ffn_Ws_down"]).T),
            "l1f_WegT": _ct(weg.reshape(16 * D, 512)),
            "l1f_WeuT": _ct(weu.reshape(16 * D, 512)),
            "l1f_WedT": _ct(wed.reshape(16 * 512, D))}


DECL_L1_FFN = [("l1f_WgT", [D, 16]), ("l1f_Wsgc", [128, 8]),
               ("l1f_WsgateT", [D, 2048]), ("l1f_WsupT", [D, 2048]),
               ("l1f_WsdownT", [2048, D]), ("l1f_WegT", [16 * D, 512]),
               ("l1f_WeuT", [16 * D, 512]), ("l1f_WedT", [16 * 512, D])]

def prep_l2_ffn(ins):
    bq = np.asarray(ins["L2_ffn_mha_bq"]).reshape(2, D)
    bk = np.asarray(ins["L2_ffn_mha_bk"]).reshape(2, D)
    qkb = np.stack([bq, bk]).reshape(2, 2, 8, 128).transpose(3, 0, 1, 2)
    return {"l2f_g1": _ct(np.asarray(ins["L2_ffn_ln1_g"]).reshape(1, D)),
            "l2f_b1": _ct(np.asarray(ins["L2_ffn_ln1_b"]).reshape(1, D)),
            "l2f_g2": _ct(np.asarray(ins["L2_ffn_ln2_g"]).reshape(1, D)),
            "l2f_b2": _ct(np.asarray(ins["L2_ffn_ln2_b"]).reshape(1, D)),
            "l2f_qkb": _ct(qkb.reshape(128, 32)),
            "l2f_vb": _ct(np.asarray(ins["L2_ffn_mha_bv"]).reshape(1, 2 * D)),
            "l2f_bo": _ct(np.asarray(ins["L2_ffn_mha_bo"]).reshape(1, 2 * D)),
            "l2f_wq": _ct(np.asarray(ins["L2_ffn_mha_wq"]).reshape(2 * D, D)),
            "l2f_wk": _ct(np.asarray(ins["L2_ffn_mha_wk"]).reshape(2 * D, D)),
            "l2f_wv": _ct(np.asarray(ins["L2_ffn_mha_wv"]).reshape(2 * D, D)),
            "l2f_wo": _ct(np.asarray(ins["L2_ffn_mha_wo"]).reshape(2 * D, D)),
            "l2f_rwm": _ct(ins["L2_ffn_mha_rw"]),
            "l2f_rbm": _ct(np.asarray(ins["L2_ffn_mha_rb"]).reshape(1, 2)),
            "l2f_rwf": _ct(ins["L2_ffn_ffn_rw"]),
            "l2f_rbf": _ct(np.asarray(ins["L2_ffn_ffn_rb"]).reshape(1, 8)),
            "l2f_fw1": _ct(np.asarray(ins["L2_ffn_fw1"]).reshape(8 * D, 2048)),
            "l2f_fw2": _ct(np.asarray(ins["L2_ffn_fw2"]).reshape(8 * 2048, D))}


DECL_L2_FFN = [("l2f_g1", [1, D]), ("l2f_b1", [1, D]), ("l2f_g2", [1, D]),
               ("l2f_b2", [1, D]), ("l2f_qkb", [128, 32]), ("l2f_vb", [1, 2 * D]),
               ("l2f_bo", [1, 2 * D]), ("l2f_wq", [2 * D, D]),
               ("l2f_wk", [2 * D, D]), ("l2f_wv", [2 * D, D]),
               ("l2f_wo", [2 * D, D]), ("l2f_rwm", [D, 2]), ("l2f_rbm", [1, 2]),
               ("l2f_rwf", [D, 8]), ("l2f_rbf", [1, 8]),
               ("l2f_fw1", [8 * D, 2048]), ("l2f_fw2", [8 * 2048, D])]

def prep_l0_ffn(ins):
    return {"l0f_wmt": _ct(np.asarray(ins["L0_ffn_Wm"]).T),
            "l0f_wpt": _ct(np.asarray(ins["L0_ffn_Wp"]).T),
            "l0f_wc1t": _ct(np.asarray(ins["L0_ffn_Wc1"]).T),
            "l0f_wc2t": _ct(np.asarray(ins["L0_ffn_Wc2"]).T)}


DECL_L0_FFN = [("l0f_wmt", [D, 12288]), ("l0f_wpt", [12288, D]),
               ("l0f_wc1t", [D, 256]), ("l0f_wc2t", [256, 12288])]

def prep_l3_attn(ins):
    m2 = np.zeros((2, 128), np.float32)
    m2[0, 64:] = NEGBIG
    m2[1, :64] = NEGBIG
    return {"l3a_wqw": _ct(ins["L3_attn_wq_w"]),
            "l3a_wqb": _ct(np.asarray(ins["L3_attn_wq_b"]).reshape(1, D)),
            "l3a_w2w": _ct(np.asarray(ins["L3_attn_w2_w"]).reshape(1, D)),
            "l3a_w2b": _ct(np.asarray(ins["L3_attn_w2_b"]).reshape(1, 1)),
            "l3a_lng": _ct(np.asarray(ins["L3_attn_ln_g"]).reshape(1, D)),
            "l3a_lnb": _ct(np.asarray(ins["L3_attn_ln_b"]).reshape(1, D)),
            "l3a_mask2": m2}


DECL_L3_ATTN = [("l3a_wqw", [D, D]), ("l3a_wqb", [1, D]), ("l3a_w2w", [1, D]),
                ("l3a_w2b", [1, 1]), ("l3a_lng", [1, D]), ("l3a_lnb", [1, D]),
                ("l3a_mask2", [2, 128])]

PHASES = {
    "l3_attn": (emit_l3_attn, DECL_L3_ATTN, prep_l3_attn),
    "l0_ffn": (emit_l0_ffn, DECL_L0_FFN, prep_l0_ffn),
    "l2_ffn": (emit_l2_ffn, DECL_L2_FFN, prep_l2_ffn),
    "l2_attn": (emit_l2_attn, DECL_L2_ATTN, prep_l2_attn),
    "l1_attn": (emit_l1_attn, DECL_L1_ATTN, prep_l1_attn),
    "l0_attn": (emit_l0_attn, DECL_L0_ATTN, prep_l0_attn),
    "l3_ffn": (emit_l3_ffn, DECL_L3_FFN, prep_l3_ffn),
    "l1_ffn": (emit_l1_ffn, DECL_L1_FFN, prep_l1_ffn),
}

ALL_PHASES = ["l0_attn", "l0_ffn", "l1_attn", "l1_ffn",
              "l2_attn", "l2_ffn", "l3_attn", "l3_ffn"]


def build_program(phase_names):
    nc = bass.Bass("TRN2", target_bir_lowering=False)
    xin = nc.dram_tensor("xin", [NTOK, D], F32, kind="ExternalInput").ap()
    xout = nc.dram_tensor("xout", [NTOK, D], F32, kind="ExternalOutput").ap()
    n = len(phase_names)
    bufs = []
    for i in range(n - 1):
        bufs.append(nc.dram_tensor(f"xmid{i}", [NTOK, D], F32, kind="Internal").ap())
    W = {}
    for pn in phase_names:
        for key, shape in PHASES[pn][1]:
            W[key] = nc.dram_tensor(key, shape, F32, kind="ExternalInput").ap()
    with contextlib.ExitStack() as stk:
        P = Prog(nc, stk)
        P.setup_consts([1e-5, 1e-6])
        for i, pn in enumerate(phase_names):
            src = xin if i == 0 else bufs[i - 1]
            dst = xout if i == n - 1 else bufs[i]
            PHASES[pn][0](P, src, dst, W)
            P.barrier()
    return nc


def run_phases(inputs, phase_names):
    nc = build_program(phase_names)
    wshared = {}
    for pn in phase_names:
        wshared.update(PHASES[pn][2](inputs))
    in_maps = []
    for c in range(NCORES):
        b0 = c * B_CORE
        d = {"xin": _ct(np.asarray(inputs["L0_attn_x"])[b0:b0 + B_CORE]
                        .reshape(NTOK, D))}
        d.update(wshared)
        in_maps.append(d)
    res = run_bass_kernel_spmd(nc, in_maps, core_ids=list(range(NCORES)))
    out = np.empty((16, T, D), np.float32)
    for c in range(NCORES):
        out[c * B_CORE:(c + 1) * B_CORE] = res.results[c]["xout"].reshape(B_CORE, T, D)
    return out


KERNEL_PHASES = ["l0_attn", "l0_ffn", "l1_attn", "l1_ffn",
                 "l2_attn", "l2_ffn", "l3_attn", "l3_ffn"]


def kernel(**inputs):
    return run_phases(inputs, KERNEL_PHASES)


# version 06
# speedup vs baseline: 1.1690x; 1.1690x over previous
import contextlib
import numpy as np
import concourse.bass as bass
import concourse.mybir as mybir
import concourse.bass_utils as _bass_utils
from concourse.bass_utils import run_bass_kernel_spmd

_orig_run_command = _bass_utils.run_command


def _patched_run_command(cmd, *a, **kw):
    cmd = [("--enable-ldw-opt=true" if c == "--enable-ldw-opt=false" else c)
           for c in cmd]
    return _orig_run_command(cmd, *a, **kw)


_bass_utils.run_command = _patched_run_command

F32 = mybir.dt.float32
BF16 = mybir.dt.bfloat16
I32 = mybir.dt.int32
U32 = mybir.dt.uint32
AF = mybir.ActivationFunctionType
ALU = mybir.AluOpType
AX = mybir.AxisListType

B_CORE = 2
T = 2048
D = 1024
NTOK = B_CORE * T
NCORES = 8
DC = D // 128

NEGBIG = -1.0e30

L0FFN_GROUPS = [[0, 1, 2, 3, 4, 5, 6, 7]]
L0FFN_THRESH = 45875.2


class Prog:
    ENGINES = ("tensor", "vector", "scalar", "gpsimd", "sync")
    SELF_SERIAL = {"vector": "dve", "scalar": "act", "gpsimd": "pool"}

    def __init__(self, nc, stk):
        self.nc = nc
        self.stk = stk
        self.ops = {e: [] for e in self.ENGINES}
        self._sems = {}
        self.nblocks = 0

    def sem(self, name):
        if name not in self._sems:
            h = self.stk.enter_context(self.nc.semaphore(name))
            self._sems[name] = [h, 0]
        return self._sems[name]

    def count(self, name):
        return self.sem(name)[1]

    def emit(self, engine, fn, wait=(), dma=None, pe_tok=False):
        waits = [w for w in wait if w is not None]
        tok = None
        selfwait = None
        if dma is not None:
            s = self.sem(dma)
            s[1] += 16
            tok = (dma, s[1])
            semname, inc = dma, 16
        elif engine in self.SELF_SERIAL:
            semname = self.SELF_SERIAL[engine]
            s = self.sem(semname)
            if s[1] > 0:
                selfwait = (semname, s[1])
            s[1] += 1
            tok = (semname, s[1])
            inc = 1
        elif engine == "tensor" and pe_tok:
            s = self.sem("pe")
            s[1] += 1
            tok = ("pe", s[1])
            semname, inc = "pe", 1
        else:
            semname, inc = None, 0

        def closure(eng, fn=fn, waits=waits, selfwait=selfwait,
                    semname=semname, inc=inc):
            for (wn, wt) in waits:
                eng.wait_ge(self._sems[wn][0], wt)
            if selfwait is not None:
                eng.wait_ge(self._sems[selfwait[0]][0], selfwait[1])
            instr = fn(eng)
            if semname is not None and instr is not None:
                instr.then_inc(self._sems[semname][0], inc)

        self.ops[engine].append(closure)
        return tok

    def wait_only(self, engine, wait):
        waits = [w for w in wait if w is not None]

        def closure(eng, waits=waits):
            for (wn, wt) in waits:
                eng.wait_ge(self._sems[wn][0], wt)

        self.ops[engine].append(closure)

    def setup_consts(self, values):
        t = self.stk.enter_context(
            self.nc.sbuf_tensor("constpool", [128, max(1, len(values))], F32))
        for i, v in enumerate(values):
            self.emit("gpsimd", lambda g, i=i, v=v: g.memset(t[:, i:i + 1], v))
            self.nc.const_aps.register(F32, v, t[:, i:i + 1])

    def barrier(self):
        nonempty = [e for e in self.ENGINES if self.ops[e]]
        if not nonempty:
            return
        self.nblocks += 1
        with self.nc.Block(f"blk{self.nblocks}") as blk:
            for e in self.ENGINES:
                oplist = self.ops[e]
                if not oplist:
                    continue

                def body(eng, oplist=oplist):
                    for fnc in oplist:
                        fnc(eng)

                getattr(blk, e)(body)
        self.ops = {e: [] for e in self.ENGINES}


def emit_identity(P, ident, idraw):
    t = P.emit("gpsimd", lambda g: g.iota(idraw[:, :], [[1, 128]], base=0,
                                          channel_multiplier=-1))
    P.emit("vector", lambda v: v.tensor_scalar(ident[:, :], idraw[:, :], 0, None,
                                               ALU.is_equal), wait=[t])


def emit_ln_tile(P, xin, row0, xt, ldtok, sb, xn_out, eps=1e-5, out_dtype_bf=True,
                 gain=None, bias_v=None, out_wait=()):
    stats, mv, rstd = sb["stats"], sb["mv"], sb["rstd"]
    P.emit("vector", lambda v: v.bn_stats(stats[:, 0:6], xt[:, 0:512]), wait=[ldtok])
    P.emit("vector", lambda v: v.bn_stats(stats[:, 6:12], xt[:, 512:1024]))
    ta = P.emit("vector", lambda v: v.bn_aggr(mv[:, :], stats[:, :]))
    ts = P.emit("scalar", lambda s: s.activation(rstd[:, :], mv[:, 1:2], AF.Ln,
                                                 bias=eps), wait=[ta])
    tsx = P.emit("scalar", lambda s: s.activation(rstd[:, :], rstd[:, :], AF.Exp,
                                                  scale=-0.5))
    P.wait_only("vector", [tsx])
    if gain is None:
        tk = P.emit("vector", lambda v: v.tensor_scalar(
            xn_out, xt[:, :], mv[:, 0:1], rstd[:, :], ALU.subtract, ALU.mult),
            wait=list(out_wait))
        return tk
    tmp = sb["lntmp"]
    P.emit("vector", lambda v: v.tensor_scalar(
        tmp[:, :], xt[:, :], mv[:, 0:1], rstd[:, :], ALU.subtract, ALU.mult))
    P.emit("vector", lambda v: v.tensor_mul(tmp[:, :], tmp[:, :], gain))
    tk = P.emit("vector", lambda v: v.tensor_add(xn_out, tmp[:, :], bias_v),
                wait=list(out_wait))
    return tk


def emit_ln_to_fm(P, xin, xn_fm, b, sb, ps, tiles=16, eps=1e-5, tok0=None):
    if tok0 is None:
        tok0 = b * T
    trtok = {0: None, 1: None}
    cptok = {0: None, 1: None}
    prev_txn = {0: None, 1: None}
    for i in range(tiles):
        slot = i % 2
        xt = sb[f"xt{slot}"]
        row0 = tok0 + i * 128
        ld = P.emit("sync", lambda s, xt=xt, row0=row0: s.dma_start(
            xt[:, :], xin[row0:row0 + 128, :]), dma=f"dma_x{slot}",
            wait=[prev_txn[slot]])
        txn = emit_ln_tile(P, xin, row0, xt, ld, sb, sb["xntm"][:, :], eps=eps)
        prev_txn[slot] = txn
        for c in range(DC):
            pslot = c % 2
            ptr = ps[f"tr{pslot}"]
            ttr = P.emit("tensor", lambda t, c=c, ptr=ptr: t.transpose(
                ptr[:, :], sb["xntm"][:, c * 128:(c + 1) * 128], sb["ident"][:, :]),
                pe_tok=True, wait=[txn, cptok[pslot]])
            col0 = i * 128
            eng = "scalar" if pslot == 0 else "vector"
            cptok[pslot] = P.emit(eng, lambda e, c=c, ptr=ptr, col0=col0, eng=eng: (
                e.activation(xn_fm[:, c, col0:col0 + 128], ptr[:, :], AF.Identity)
                if eng == "scalar" else
                e.tensor_copy(xn_fm[:, c, col0:col0 + 128], ptr[:, :])),
                wait=[ttr])
    return [cptok[0], cptok[1]]


def load_weight_chunks(P, dst_ap, src_ap, name):
    src = src_ap.rearrange("(c p) o -> p c o", p=128)
    return P.emit("gpsimd", lambda g: g.dma_start(dst_ap, src), dma=name)


def emit_l2_attn(P, xin, xout, W, name="l2a"):
    nc = P.nc
    with contextlib.ExitStack() as pstk:
        sb_ = lambda n, s, d: pstk.enter_context(nc.sbuf_tensor(f"{name}_{n}", s, d))
        ps_ = lambda n, s, d: pstk.enter_context(nc.psum_tensor(f"{name}_{n}", s, d))
        xr = [sb_("xr0", [128, D], F32), sb_("xr1", [128, D], F32)]
        sb = {
            "xt0": xr[0], "xt1": xr[1],
            "stats": sb_("stats", [128, 12], F32), "mv": sb_("mv", [128, 2], F32),
            "rstd": sb_("rstd", [128, 1], F32), "xntm": sb_("xntm", [128, D], BF16),
            "ident": sb_("ident", [128, 128], BF16),
        }
        idraw = sb_("idraw", [128, 128], I32)
        ps = {"tr0": ps_("tr0", [128, 1024], BF16).ap()[:, 0:128],
              "tr1": ps_("tr1", [128, 1024], BF16).ap()[:, 0:128]}
        r3 = lambda t, c: t.ap().rearrange("p (c t) -> p c t", c=c)
        xn_fm = r3(sb_("xnfm", [128, DC * T], BF16), DC)
        wq = r3(sb_("wq", [128, DC * D], BF16), DC)
        wk = r3(sb_("wk", [128, DC * D], BF16), DC)
        wv = r3(sb_("wv", [128, DC * D], BF16), DC)
        qf = r3(sb_("qf", [128, DC * T], BF16), DC)
        kf = r3(sb_("kf", [128, DC * T], BF16), DC)
        vt = r3(sb_("vt", [128, 16 * D], BF16), 16)
        ssb = sb_("ssb", [128, T], F32)
        psb = sb_("psb", [128, T], BF16)
        ptt = r3(sb_("ptt", [128, 16 * 128], BF16), 16)
        rmax = sb_("rmax", [128, 1], F32)
        nbias = sb_("nbias", [128, 1], F32)
        rsum = sb_("rsum", [128, 1], F32)
        rrec = sb_("rrec", [128, 1], F32)
        pmm = [ps_("pmm0", [128, 512], F32), ps_("pmm1", [128, 512], F32)]
        pvv = [ps_("pvv0", [128, 512], F32), ps_("pvv1", [128, 512], F32)]

        emit_identity(P, sb["ident"], idraw)
        wtoks = [load_weight_chunks(P, w, W[n], "dma_w")
                 for w, n in ((wq, "l2_WqT"), (wk, "l2_WkT"), (wv, "l2_WvT"))]
        P.wait_only("gpsimd", [wtoks[-1]])
        P.barrier()

        for b in range(B_CORE):
            emit_ln_to_fm(P, xin, xn_fm, b, sb, ps)
            P.barrier()

            evtok = {0: None, 1: None}
            jobs = []
            for oc in range(DC):
                for tg in range(4):
                    jobs.append(("q", oc, tg))
                    jobs.append(("k", oc, tg))
            for tc in range(16):
                for half in range(2):
                    jobs.append(("v", tc, half))
            for jn, (kind, a1, a2) in enumerate(jobs):
                slot = jn % 2
                pm = pmm[slot]
                wl = [evtok[slot]]
                for dc in range(DC):
                    if kind == "q" or kind == "k":
                        wsb = wq if kind == "q" else wk
                        fn = lambda t, pm=pm, wsb=wsb, oc=a1, tg=a2, dc=dc: t.matmul(
                            pm[:, :], wsb[:, dc, oc * 128:(oc + 1) * 128],
                            xn_fm[:, dc, tg * 512:(tg + 1) * 512],
                            start=(dc == 0), stop=(dc == DC - 1))
                    else:
                        fn = lambda t, pm=pm, tc=a1, half=a2, dc=dc: t.matmul(
                            pm[:, :], xn_fm[:, dc, tc * 128:(tc + 1) * 128],
                            wv[:, dc, half * 512:(half + 1) * 512],
                            start=(dc == 0), stop=(dc == DC - 1))
                    tk = P.emit("tensor", fn, pe_tok=(dc == DC - 1),
                                wait=wl if dc == 0 else [])
                if kind == "q":
                    dstap = qf[:, a1, a2 * 512:(a2 + 1) * 512]
                    evtok[slot] = P.emit("scalar", lambda s, dstap=dstap, pm=pm:
                                         s.activation(dstap, pm[:, :], AF.Identity,
                                                      scale=1.0 / 32.0), wait=[tk])
                elif kind == "k":
                    dstap = kf[:, a1, a2 * 512:(a2 + 1) * 512]
                    evtok[slot] = P.emit("vector", lambda v, dstap=dstap, pm=pm:
                                         v.tensor_copy(dstap, pm[:, :]), wait=[tk])
                else:
                    dstap = vt[:, a1, a2 * 512:(a2 + 1) * 512]
                    eng = "scalar" if slot == 0 else "vector"
                    evtok[slot] = P.emit(eng, lambda e, dstap=dstap, pm=pm, eng=eng: (
                        e.activation(dstap, pm[:, :], AF.Identity) if eng == "scalar"
                        else e.tensor_copy(dstap, pm[:, :])), wait=[tk])
            P.barrier()

            ev = {0: None, 1: None}
            pvtok = {0: None, 1: None}
            addtok_hist = {0: None, 1: None}
            sttok = {0: None, 1: None}
            trtok = {0: None, 1: None}
            last_ttr = None
            prev_pv = [None, None]
            for qi in range(16):
                for kg in range(4):
                    slot = kg % 2
                    pm = pmm[slot]
                    wl = [ev[slot]]
                    for dc in range(DC):
                        tk = P.emit("tensor", lambda t, pm=pm, qi=qi, kg=kg, dc=dc:
                                    t.matmul(pm[:, :],
                                             qf[:, dc, qi * 128:(qi + 1) * 128],
                                             kf[:, dc, kg * 512:(kg + 1) * 512],
                                             start=(dc == 0), stop=(dc == DC - 1)),
                                    pe_tok=(dc == DC - 1), wait=wl if dc == 0 else [])
                    ev[slot] = P.emit("vector", lambda v, pm=pm, kg=kg:
                                      v.tensor_copy(ssb[:, kg * 512:(kg + 1) * 512],
                                                    pm[:, :]), wait=[tk])
                P.emit("vector", lambda v: v.reduce_max(rmax[:, :], ssb[:, :],
                                                        axis=AX.X),
                       wait=[ev[0], ev[1]])
                tnb = P.emit("vector", lambda v: v.tensor_scalar(
                    nbias[:, :], rmax[:, :], -1.0, None, ALU.mult))
                tex = P.emit("scalar", lambda s: s.activation(
                    ssb[:, :], ssb[:, :], AF.Exp, bias=nbias[:, :],
                    accum_out=rsum[:, :]), wait=[tnb])
                P.emit("vector", lambda v: v.reciprocal(rrec[:, :], rsum[:, :]),
                       wait=[tex])
                tp = P.emit("vector", lambda v: v.tensor_scalar(
                    psb[:, :], ssb[:, :], rrec[:, :], None, ALU.mult),
                    wait=[last_ttr])
                for kc in range(16):
                    pslot = kc % 2
                    ptr = ps[f"tr{pslot}"]
                    ttr = P.emit("tensor", lambda t, kc=kc, ptr=ptr: t.transpose(
                        ptr[:, :], psb[:, kc * 128:(kc + 1) * 128],
                        sb["ident"][:, :]), pe_tok=True, wait=[tp, trtok[pslot]])
                    eng = "scalar" if pslot == 0 else "vector"
                    trtok[pslot] = P.emit(eng, lambda e, kc=kc, ptr=ptr, eng=eng: (
                        e.activation(ptt[:, kc, :], ptr[:, :], AF.Identity)
                        if eng == "scalar" else e.tensor_copy(ptt[:, kc, :],
                                                              ptr[:, :])),
                        wait=[ttr, prev_pv[0], prev_pv[1]])
                last_ttr = ttr
                qslot = qi % 2
                row0 = b * T + qi * 128
                ldx = P.emit("sync", lambda s, xr_=xr[qslot], row0=row0: s.dma_start(
                    xr_[:, :], xin[row0:row0 + 128, :]), dma=f"dma_xr{qslot}",
                    wait=[sttok[qslot]])
                lastpv = {}
                for half in range(2):
                    pv = pvv[half]
                    wl = [trtok[0], trtok[1], pvtok[half]]
                    for kc in range(16):
                        tk = P.emit("tensor", lambda t, pv=pv, kc=kc, half=half:
                                    t.matmul(pv[:, :], ptt[:, kc, :],
                                             vt[:, kc, half * 512:(half + 1) * 512],
                                             start=(kc == 0), stop=(kc == 15)),
                                    pe_tok=(kc == 15), wait=wl if kc == 0 else [])
                    lastpv[half] = tk
                prev_pv = [lastpv[0], lastpv[1]]
                at = {}
                for half in range(2):
                    at[half] = P.emit("vector", lambda v, pv=pvv[half], half=half,
                                      x_=xr[qslot]: v.tensor_add(
                        x_[:, half * 512:(half + 1) * 512], pv[:, :],
                        x_[:, half * 512:(half + 1) * 512]),
                        wait=[lastpv[half], ldx])
                    pvtok[half] = at[half]
                addtok_hist[qslot] = at[1]
                sttok[qslot] = P.emit("gpsimd", lambda g, o=xr[qslot], row0=row0:
                                      g.dma_start(xout[row0:row0 + 128, :], o[:, :]),
                                      dma=f"dma_o{qslot}", wait=[at[0], at[1]])
            P.wait_only("gpsimd", [sttok[0], sttok[1]])
            P.barrier()


def emit_l0_attn(P, xin, xout, W, name="l0a"):
    nc = P.nc
    with contextlib.ExitStack() as pstk:
        sb_ = lambda n, s, d: pstk.enter_context(nc.sbuf_tensor(f"{name}_{n}", s, d))
        ps_ = lambda n, s, d: pstk.enter_context(nc.psum_tensor(f"{name}_{n}", s, d))
        r3 = lambda t, c: t.ap().rearrange("p (c t) -> p c t", c=c)
        xr = [sb_("xr0", [128, D], F32), sb_("xr1", [128, D], F32)]
        sb = {
            "xt0": xr[0], "xt1": xr[1],
            "stats": sb_("stats", [128, 12], F32), "mv": sb_("mv", [128, 2], F32),
            "rstd": sb_("rstd", [128, 1], F32), "xntm": sb_("xntm", [128, D], BF16),
            "ident": sb_("ident", [128, 128], BF16),
        }
        idraw = sb_("idraw", [128, 128], I32)
        ps = {"tr0": ps_("tr0", [128, 1024], BF16).ap()[:, 0:128],
              "tr1": ps_("tr1", [128, 1024], BF16).ap()[:, 0:128]}
        pmm = [ps_("pmm0", [128, 512], F32), ps_("pmm1", [128, 512], F32)]
        pvv = [ps_("pvv0", [128, 512], F32), ps_("pvv1", [128, 512], F32)]
        xn_fm = r3(sb_("xnfm", [128, DC * T], BF16), DC)
        wqkv = r3(sb_("wqkv", [128, DC * 1536], BF16), DC)
        wqkr = r3(sb_("wqkr", [128, DC * 512], BF16), DC)
        wvr = r3(sb_("wvr", [128, DC * 512], BF16), DC)
        wproj = r3(sb_("wproj", [128, DC * D], BF16), DC)
        qf4 = r3(sb_("qf4", [128, 4 * T], BF16), 4)
        kf4 = r3(sb_("kf4", [128, 4 * T], BF16), 4)
        qr4 = r3(sb_("qr4", [128, 4 * T], BF16), 4)
        kr4 = r3(sb_("kr4", [128, 4 * T], BF16), 4)
        vt = r3(sb_("vt", [128, 16 * D], BF16), 16)
        ssb = sb_("ssb", [128, 1152], F32)
        psb = sb_("psb", [128, 1152], BF16)
        ptt = r3(sb_("ptt", [128, 9 * 128], BF16), 9)
        ofm = r3(sb_("ofm", [128, DC * 128], BF16), DC)
        masks = r3(sb_("masks", [128, 2 * 128], F32), 2)
        rmax = sb_("rmax", [128, 1], F32)
        nbias = sb_("nbias", [128, 1], F32)
        rsum = sb_("rsum", [128, 1], F32)
        rrec = sb_("rrec", [128, 1], F32)

        emit_identity(P, sb["ident"], idraw)
        wtoks = [load_weight_chunks(P, w, W[n], "dma_w")
                 for w, n in ((wqkv, "l0_wqkv"), (wqkr, "l0_wqkr"),
                              (wvr, "l0_wvr"), (wproj, "l0_wproj"))]
        mtok = P.emit("gpsimd", lambda g: g.dma_start(
            masks[:, :, :], W["l0_masks"].rearrange("(m p) k -> p m k", p=128)),
            dma="dma_w")
        P.wait_only("gpsimd", [mtok])
        P.barrier()

        for b in range(B_CORE):
            z1 = P.emit("gpsimd", lambda g: g.memset(qr4[:, :, :], 0.0))
            z2 = P.emit("gpsimd", lambda g: g.memset(kr4[:, :, :], 0.0))
            emit_ln_to_fm(P, xin, xn_fm, b, sb, ps)
            P.barrier()

            evtok = {0: None, 1: None}
            jobs = []
            for tg in range(4):
                for oc in range(4):
                    jobs.append(("qf", oc, tg))
                    jobs.append(("kf", oc, tg))
                for hg in range(4):
                    jobs.append(("qr", hg, tg))
                    jobs.append(("kr", hg, tg))
            for tc in range(16):
                jobs.append(("v", 0, tc))
                jobs.append(("v", 1, tc))
            for jn, (kind, a1, tg) in enumerate(jobs):
                slot = jn % 2
                pm = pmm[slot]
                wl = [evtok[slot]]
                if kind in ("qf", "kf"):
                    col0 = a1 * 128 + (0 if kind == "qf" else 512)
                    for dc in range(DC):
                        tk = P.emit("tensor", lambda t, pm=pm, col0=col0, tg=tg,
                                    dc=dc: t.matmul(
                            pm[:, :], wqkv[:, dc, col0:col0 + 128],
                            xn_fm[:, dc, tg * 512:(tg + 1) * 512],
                            start=(dc == 0), stop=(dc == DC - 1)),
                            pe_tok=(dc == DC - 1), wait=wl if dc == 0 else [])
                    dst = qf4 if kind == "qf" else kf4
                    scl = 0.125 if kind == "qf" else 1.0
                    eng = "scalar" if slot == 0 else "vector"
                    evtok[slot] = P.emit(
                        eng, lambda e, pm=pm, dst=dst, a1=a1, tg=tg, scl=scl,
                        eng=eng: (
                            e.activation(dst[:, a1, tg * 512:(tg + 1) * 512],
                                         pm[:, :], AF.Identity, scale=scl)
                            if eng == "scalar" else
                            e.tensor_scalar(dst[:, a1, tg * 512:(tg + 1) * 512],
                                            pm[:, :], scl, None, ALU.mult)),
                        wait=[tk])
                elif kind in ("qr", "kr"):
                    col0 = a1 * 64 + (0 if kind == "qr" else 256)
                    for dc in range(DC):
                        tk = P.emit("tensor", lambda t, pm=pm, col0=col0, tg=tg,
                                    dc=dc: t.matmul(
                            pm[0:64, :],
                            wqkr[:, dc, col0:col0 + 64],
                            xn_fm[:, dc, tg * 512:(tg + 1) * 512],
                            start=(dc == 0), stop=(dc == DC - 1)),
                            pe_tok=(dc == DC - 1), wait=wl if dc == 0 else [])
                    dst = qr4 if kind == "qr" else kr4
                    scl = 1.0 / np.sqrt(32.0) if kind == "qr" else 1.0
                    toks = []
                    for hh in range(2):
                        h = 2 * a1 + hh
                        rb = (64 * h) % 128
                        hc = h // 2
                        eng = "scalar" if hh == 0 else "vector"
                        toks.append(P.emit(
                            eng, lambda e, pm=pm, dst=dst, hh=hh, rb=rb, hc=hc,
                            tg=tg, scl=scl, eng=eng: (
                                e.activation(
                                    dst[rb:rb + 32, hc, tg * 512:(tg + 1) * 512],
                                    pm[32 * hh:32 * hh + 32, :], AF.Identity,
                                    scale=scl)
                                if eng == "scalar" else
                                e.tensor_scalar(
                                    dst[rb:rb + 32, hc, tg * 512:(tg + 1) * 512],
                                    pm[32 * hh:32 * hh + 32, :], scl, None,
                                    ALU.mult)),
                            wait=[tk, z1, z2]))
                    evtok[slot] = toks[-1]
                else:
                    half, tc = a1, tg
                    for dc in range(DC):
                        src = (wqkv[:, dc, 1024:1536] if half == 0
                               else wvr[:, dc, :])
                        tk = P.emit("tensor", lambda t, pm=pm, src=src, tc=tc,
                                    dc=dc: t.matmul(
                            pm[:, :], xn_fm[:, dc, tc * 128:(tc + 1) * 128],
                            src, start=(dc == 0), stop=(dc == DC - 1)),
                            pe_tok=(dc == DC - 1), wait=wl if dc == 0 else [])
                    eng = "scalar" if slot == 0 else "vector"
                    evtok[slot] = P.emit(
                        eng, lambda e, pm=pm, tc=tc, half=half, eng=eng: (
                            e.activation(vt[:, tc, half * 512:(half + 1) * 512],
                                         pm[:, :], AF.Identity)
                            if eng == "scalar" else
                            e.tensor_copy(vt[:, tc, half * 512:(half + 1) * 512],
                                          pm[:, :])), wait=[tk])
            P.barrier()

            sttok = {0: None, 1: None}
            trtok = {0: None, 1: None}
            oftok = None
            ofbank = {0: None, 1: None}
            last_ttr = None
            prev_pv = None
            ev = {0: None, 1: None}
            prev_proj = {0: None, 1: None}
            for qi in range(16):
                for h in range(16):
                    red = h >= 8
                    hh = h - 8 if red else h
                    qsrc = qr4 if red else qf4
                    ksrc = kr4 if red else kf4
                    wtiles = 2 if red else 8
                    kt0 = max(0, qi - wtiles)
                    ntiles = qi - kt0 + 1
                    w = ntiles * 128
                    rb = (64 * hh) % 128
                    hc = hh // 2
                    vcol = 512 + 64 * hh if red else 64 * hh
                    for r, kt in enumerate(range(kt0, qi + 1)):
                        slot = r % 2
                        pm = pmm[slot]
                        tk = P.emit("tensor", lambda t, pm=pm, qsrc=qsrc, ksrc=ksrc,
                                    rb=rb, hc=hc, qi=qi, kt=kt: t.matmul(
                            pm[0:128, 0:128], qsrc[rb:rb + 64, hc,
                                                   qi * 128:(qi + 1) * 128],
                            ksrc[rb:rb + 64, hc, kt * 128:(kt + 1) * 128],
                            start=True, stop=True),
                            pe_tok=True, wait=[ev[slot]])
                        if kt == qi:
                            mi = 0
                        elif kt == kt0 and qi - wtiles >= 0:
                            mi = 1
                        else:
                            mi = None
                        if mi is None:
                            ev[slot] = P.emit("vector", lambda v, pm=pm, r=r:
                                              v.tensor_copy(
                                ssb[:, r * 128:(r + 1) * 128], pm[0:128, 0:128]),
                                wait=[tk])
                        else:
                            ev[slot] = P.emit("vector", lambda v, pm=pm, r=r, mi=mi:
                                              v.tensor_add(
                                ssb[:, r * 128:(r + 1) * 128], pm[0:128, 0:128],
                                masks[:, mi, :]), wait=[tk])
                    P.emit("vector", lambda v, w=w: v.reduce_max(
                        rmax[:, :], ssb[:, 0:w], axis=AX.X), wait=[ev[0], ev[1]])
                    P.emit("vector", lambda v: v.tensor_scalar(
                        nbias[:, :], rmax[:, :], -1.0, None, ALU.mult))
                    tex = P.emit("scalar", lambda s, w=w: s.activation(
                        ssb[:, 0:w], ssb[:, 0:w], AF.Exp, bias=nbias[:, :],
                        accum_out=rsum[:, :]), wait=[("dve", P.count("dve"))])
                    P.emit("vector", lambda v: v.reciprocal(rrec[:, :], rsum[:, :]),
                           wait=[tex])
                    tp = P.emit("vector", lambda v, w=w: v.tensor_scalar(
                        psb[:, 0:w], ssb[:, 0:w], rrec[:, :], None, ALU.mult),
                        wait=[last_ttr])
                    for r in range(ntiles):
                        pslot = r % 2
                        ptr = ps[f"tr{pslot}"]
                        ttr = P.emit("tensor", lambda t, r=r, ptr=ptr: t.transpose(
                            ptr[:, :], psb[:, r * 128:(r + 1) * 128],
                            sb["ident"][:, :]), pe_tok=True,
                            wait=[tp, trtok[pslot]])
                        eng = "scalar" if pslot == 0 else "vector"
                        trtok[pslot] = P.emit(
                            eng, lambda e, r=r, ptr=ptr, eng=eng: (
                                e.activation(ptt[:, r, :], ptr[:, :], AF.Identity)
                                if eng == "scalar" else
                                e.tensor_copy(ptt[:, r, :], ptr[:, :])),
                            wait=[ttr, prev_pv])
                    last_ttr = ttr
                    pv = pvv[h % 2]
                    for r, kt in enumerate(range(kt0, qi + 1)):
                        tk = P.emit("tensor", lambda t, pv=pv, kt=kt, r=r,
                                    vcol=vcol: t.matmul(
                            pv[0:64, 0:128], vt[:, kt, vcol:vcol + 64],
                            ptt[:, r, :], start=(r == 0), stop=(r == ntiles - 1)),
                            pe_tok=(r == ntiles - 1),
                            wait=[trtok[0], trtok[1], ofbank[h % 2]]
                            if r == 0 else [])
                    prev_pv = tk
                    eng = "scalar" if h % 2 == 0 else "vector"
                    oftok = P.emit(eng, lambda e, pv=pv, rb=rb_out(h), hc=oc_out(h),
                                   eng=eng: (
                        e.activation(ofm[rb:rb + 64, hc, :], pv[0:64, 0:128],
                                     AF.Identity) if eng == "scalar" else
                        e.tensor_copy(ofm[rb:rb + 64, hc, :], pv[0:64, 0:128])),
                        wait=[tk])
                    ofbank[h % 2] = oftok
                qslot = qi % 2
                row0 = b * T + qi * 128
                ldx = P.emit("sync", lambda s, x_=xr[qslot], row0=row0: s.dma_start(
                    x_[:, :], xin[row0:row0 + 128, :]), dma=f"dma_xr{qslot}",
                    wait=[sttok[qslot]])
                at = {}
                for half in range(2):
                    pm = pmm[half]
                    wl = [ofbank[0], ofbank[1], prev_proj[half], ev[half]]
                    for dc in range(DC):
                        tk = P.emit("tensor", lambda t, pm=pm, half=half, dc=dc:
                                    t.matmul(
                            pm[:, :], ofm[:, dc, :],
                            wproj[:, dc, half * 512:(half + 1) * 512],
                            start=(dc == 0), stop=(dc == DC - 1)),
                            pe_tok=(dc == DC - 1), wait=wl if dc == 0 else [])
                    at[half] = P.emit("vector", lambda v, pm=pm, half=half,
                                      x_=xr[qslot]: v.tensor_add(
                        x_[:, half * 512:(half + 1) * 512], pm[:, :],
                        x_[:, half * 512:(half + 1) * 512]), wait=[tk, ldx])
                    prev_proj[half] = at[half]
                    ev[half] = at[half]
                sttok[qslot] = P.emit("gpsimd", lambda g, o=xr[qslot], row0=row0:
                                      g.dma_start(xout[row0:row0 + 128, :],
                                                  o[:, :]),
                                      dma=f"dma_o{qslot}", wait=[at[0], at[1]])
            P.wait_only("gpsimd", [sttok[0], sttok[1]])
            P.barrier()


def rb_out(h):
    return (64 * h) % 128


def oc_out(h):
    return h // 2


def emit_l1_attn(P, xin, xout, W, name="l1a"):
    nc = P.nc
    with contextlib.ExitStack() as pstk:
        sb_ = lambda n, s, d: pstk.enter_context(nc.sbuf_tensor(f"{name}_{n}", s, d))
        ps_ = lambda n, s, d: pstk.enter_context(nc.psum_tensor(f"{name}_{n}", s, d))
        r3 = lambda t, c: t.ap().rearrange("p (c t) -> p c t", c=c)
        qf = r3(sb_("qf", [128, DC * T], BF16), DC)
        kf = r3(sb_("kf", [128, DC * T], BF16), DC)
        vt = r3(sb_("vt", [128, 16 * D], BF16), 16)
        xr = [sb_("xr0", [128, D], F32), sb_("xr1", [128, D], F32)]
        sb = {
            "xt0": xr[0], "xt1": xr[1],
            "stats": sb_("stats", [128, 12], F32), "mv": sb_("mv", [128, 2], F32),
            "rstd": sb_("rstd", [128, 1], F32), "xntm": sb_("xntm", [128, D], BF16),
            "ident": sb_("ident", [128, 128], BF16),
        }
        idraw = sb_("idraw", [128, 128], I32)
        ps = {"tr0": ps_("tr0", [128, 1024], BF16).ap()[:, 0:128],
              "tr1": ps_("tr1", [128, 1024], BF16).ap()[:, 0:128]}
        pmm = [ps_("pmm0", [128, 512], F32), ps_("pmm1", [128, 512], F32)]
        pvv = [ps_("pvv0", [128, 512], F32), ps_("pvv1", [128, 512], F32)]
        emit_identity(P, sb["ident"], idraw)

        for b in range(B_CORE):
            with contextlib.ExitStack() as astk:
                sa_ = lambda n, s, d: astk.enter_context(
                    nc.sbuf_tensor(f"{name}A{b}_{n}", s, d))
                xn_fm = r3(sa_("xnfm", [128, DC * T], BF16), DC)
                wq = r3(sa_("wq", [128, DC * D], BF16), DC)
                wk = r3(sa_("wk", [128, DC * D], BF16), DC)
                wv = r3(sa_("wv", [128, DC * D], BF16), DC)
                qtm = sa_("qtm", [128, D], F32)
                rop = sa_("rop", [128, D], BF16)
                cs = [sa_("cs0", [128, 1024], F32), sa_("cs1", [128, 1024], F32)]
                tmp1 = sa_("tmp1", [128, 512], F32)
                tmp2 = sa_("tmp2", [128, 512], F32)
                wtoks = [load_weight_chunks(P, w, W[n], "dma_w")
                         for w, n in ((wq, "l1_wqT"), (wk, "l1_wkT"),
                                      (wv, "l1_wvT"))]
                P.wait_only("gpsimd", [wtoks[-1]])
                emit_ln_to_fm(P, xin, xn_fm, b, sb, ps)
                P.barrier()

                evtok = {0: None, 1: None}
                csuse = [None, None]
                trtok = {0: None, 1: None}
                ropetok = None
                roptr = None
                for tc in range(16):
                    cslot = tc % 2
                    ldc = P.emit("sync", lambda s, c=cs[cslot], tc=tc: s.dma_start(
                        c[:, :], W["l1_ropecs"][tc * 128:(tc + 1) * 128, :]),
                        dma=f"dma_cs{cslot}", wait=[csuse[cslot]])
                    for kind in ("v", "q", "k"):
                        wsb = {"q": wq, "k": wk, "v": wv}[kind]
                        etoks = []
                        for half in range(2):
                            pm = pmm[half]
                            for dc in range(DC):
                                tk = P.emit("tensor", lambda t, pm=pm, wsb=wsb,
                                            tc=tc, half=half, dc=dc: t.matmul(
                                    pm[:, :], xn_fm[:, dc, tc * 128:(tc + 1) * 128],
                                    wsb[:, dc, half * 512:(half + 1) * 512],
                                    start=(dc == 0), stop=(dc == DC - 1)),
                                    pe_tok=(dc == DC - 1),
                                    wait=[evtok[half]] if dc == 0 else [])
                            if kind == "v":
                                evtok[half] = P.emit(
                                    "vector", lambda v, pm=pm, tc=tc, half=half:
                                    v.tensor_copy(
                                        vt[:, tc, half * 512:(half + 1) * 512],
                                        pm[:, :]), wait=[tk])
                            else:
                                scl = (1.0 / 32.0) if kind == "q" else 1.0
                                evtok[half] = P.emit(
                                    "scalar", lambda s, pm=pm, half=half, scl=scl:
                                    s.activation(qtm[:, half * 512:(half + 1) * 512],
                                                 pm[:, :], AF.Identity, scale=scl),
                                    wait=[tk, ropetok])
                            etoks.append(evtok[half])
                        if kind == "v":
                            continue
                        cosap = cs[cslot][:, 0:512]
                        sinap = cs[cslot][:, 512:1024]
                        P.emit("vector", lambda v, c=cosap: v.tensor_mul(
                            tmp1[:, :], qtm[:, 0:512], c),
                            wait=etoks + [ldc])
                        P.emit("vector", lambda v, s=sinap: v.tensor_mul(
                            tmp2[:, :], qtm[:, 512:1024], s))
                        P.emit("vector", lambda v: v.tensor_sub(
                            rop[:, 0:512], tmp1[:, :], tmp2[:, :]),
                            wait=[roptr])
                        P.emit("vector", lambda v, s=sinap: v.tensor_mul(
                            tmp1[:, :], qtm[:, 0:512], s))
                        P.emit("vector", lambda v, c=cosap: v.tensor_mul(
                            tmp2[:, :], qtm[:, 512:1024], c))
                        tr6 = P.emit("vector", lambda v: v.tensor_add(
                            rop[:, 512:1024], tmp1[:, :], tmp2[:, :]))
                        csuse[cslot] = tr6
                        ropetok = tr6
                        dst = qf if kind == "q" else kf
                        for c in range(DC):
                            pslot = c % 2
                            ptr = ps[f"tr{pslot}"]
                            ttr = P.emit("tensor", lambda t, c=c, ptr=ptr:
                                         t.transpose(ptr[:, :],
                                                     rop[:, c * 128:(c + 1) * 128],
                                                     sb["ident"][:, :]),
                                         pe_tok=True, wait=[tr6, trtok[pslot]])
                            eng = "scalar" if pslot == 0 else "vector"
                            col0 = tc * 128
                            trtok[pslot] = P.emit(
                                eng, lambda e, c=c, ptr=ptr, dst=dst, col0=col0,
                                eng=eng: (
                                    e.activation(dst[:, c, col0:col0 + 128],
                                                 ptr[:, :], AF.Identity)
                                    if eng == "scalar" else
                                    e.tensor_copy(dst[:, c, col0:col0 + 128],
                                                  ptr[:, :])), wait=[ttr])
                        roptr = ttr
                P.barrier()

            with contextlib.ExitStack() as bstk:
                sbb_ = lambda n, s, d: bstk.enter_context(
                    nc.sbuf_tensor(f"{name}B{b}_{n}", s, d))
                ssb = sbb_("ssb", [128, T], F32)
                psb = sbb_("psb", [128, T], BF16)
                ptt4 = r3(sbb_("ptt4", [128, 16 * 512], BF16), 16)
                ofm = r3(sbb_("ofm", [128, DC * 512], BF16), DC)
                wo = r3(sbb_("wo", [128, DC * D], BF16), DC)
                rmax = sbb_("rmax", [128, 1], F32)
                nbias = sbb_("nbias", [128, 1], F32)
                rsum = sbb_("rsum", [128, 1], F32)
                rrec = sbb_("rrec", [128, 1], F32)
                wotok = load_weight_chunks(P, wo, W["l1_woT"], "dma_wo")
                P.wait_only("gpsimd", [wotok])

                ev = {0: None, 1: None}
                pvtok = {0: None, 1: None}
                sttok = {0: None, 1: None}
                trtok = {0: None, 1: None}
                oftok = {0: None, 1: None}
                last_ttr = None
                prev_pv = None
                prev_op = {0: None, 1: None}
                for qg in range(4):
                    for j in range(4):
                        qi = qg * 4 + j
                        for kg in range(4):
                            slot = kg % 2
                            pm = pmm[slot]
                            for dc in range(DC):
                                tk = P.emit("tensor", lambda t, pm=pm, qi=qi, kg=kg,
                                            dc=dc: t.matmul(
                                    pm[:, :], qf[:, dc, qi * 128:(qi + 1) * 128],
                                    kf[:, dc, kg * 512:(kg + 1) * 512],
                                    start=(dc == 0), stop=(dc == DC - 1)),
                                    pe_tok=(dc == DC - 1),
                                    wait=[ev[slot]] if dc == 0 else [])
                            ev[slot] = P.emit("vector", lambda v, pm=pm, kg=kg:
                                              v.tensor_copy(
                                ssb[:, kg * 512:(kg + 1) * 512], pm[:, :]),
                                wait=[tk])
                        P.emit("vector", lambda v: v.reduce_max(
                            rmax[:, :], ssb[:, :], axis=AX.X), wait=[ev[0], ev[1]])
                        P.emit("vector", lambda v: v.tensor_scalar(
                            nbias[:, :], rmax[:, :], -1.0, None, ALU.mult))
                        tex = P.emit("scalar", lambda s: s.activation(
                            ssb[:, :], ssb[:, :], AF.Exp, bias=nbias[:, :],
                            accum_out=rsum[:, :]),
                            wait=[("dve", P.count("dve"))])
                        P.emit("vector", lambda v: v.reciprocal(
                            rrec[:, :], rsum[:, :]), wait=[tex])
                        tp = P.emit("vector", lambda v: v.tensor_scalar(
                            psb[:, :], ssb[:, :], rrec[:, :], None, ALU.mult),
                            wait=[last_ttr])
                        for kc in range(16):
                            pslot = kc % 2
                            ptr = ps[f"tr{pslot}"]
                            ttr = P.emit("tensor", lambda t, kc=kc, ptr=ptr:
                                         t.transpose(
                                ptr[:, :], psb[:, kc * 128:(kc + 1) * 128],
                                sb["ident"][:, :]), pe_tok=True,
                                wait=[tp, trtok[pslot], prev_pv])
                            eng = "scalar" if pslot == 0 else "vector"
                            col0 = j * 128
                            trtok[pslot] = P.emit(
                                eng, lambda e, kc=kc, ptr=ptr, col0=col0, eng=eng: (
                                    e.activation(ptt4[:, kc, col0:col0 + 128],
                                                 ptr[:, :], AF.Identity)
                                    if eng == "scalar" else
                                    e.tensor_copy(ptt4[:, kc, col0:col0 + 128],
                                                  ptr[:, :])), wait=[ttr])
                        last_ttr = ttr
                    for dvc in range(DC):
                        slot = dvc % 2
                        pv = pvv[slot]
                        wl = [trtok[0], trtok[1], oftok[slot]]
                        for kc in range(16):
                            tk = P.emit("tensor", lambda t, pv=pv, kc=kc, dvc=dvc:
                                        t.matmul(
                                pv[:, :], vt[:, kc, dvc * 128:(dvc + 1) * 128],
                                ptt4[:, kc, :], start=(kc == 0), stop=(kc == 15)),
                                pe_tok=(kc == 15), wait=wl if kc == 0 else [])
                        prev_pv = tk
                        eng = "scalar" if slot == 0 else "vector"
                        oftok[slot] = P.emit(
                            eng, lambda e, pv=pv, dvc=dvc, eng=eng: (
                                e.activation(ofm[:, dvc, :], pv[:, :], AF.Identity)
                                if eng == "scalar" else
                                e.tensor_copy(ofm[:, dvc, :], pv[:, :])), wait=[tk])
                    for j in range(4):
                        qi = qg * 4 + j
                        qslot = j % 2
                        row0 = b * T + qi * 128
                        ldx = P.emit("sync", lambda s, x_=xr[qslot], row0=row0:
                                     s.dma_start(x_[:, :], xin[row0:row0 + 128, :]),
                                     dma=f"dma_xr{qslot}", wait=[sttok[qslot]])
                        at = {}
                        for half in range(2):
                            pm = pmm[half]
                            wl = [oftok[0], oftok[1], prev_op[half], ev[half]]
                            for dc in range(DC):
                                tk = P.emit("tensor", lambda t, pm=pm, j=j, half=half,
                                            dc=dc: t.matmul(
                                    pm[:, :], ofm[:, dc, j * 128:(j + 1) * 128],
                                    wo[:, dc, half * 512:(half + 1) * 512],
                                    start=(dc == 0), stop=(dc == DC - 1)),
                                    pe_tok=(dc == DC - 1),
                                    wait=wl if dc == 0 else [])
                            at[half] = P.emit("vector", lambda v, pm=pm, half=half,
                                              x_=xr[qslot]: v.tensor_add(
                                x_[:, half * 512:(half + 1) * 512], pm[:, :],
                                x_[:, half * 512:(half + 1) * 512]),
                                wait=[tk, ldx])
                            prev_op[half] = at[half]
                            ev[half] = at[half]
                        sttok[qslot] = P.emit("gpsimd", lambda g, o=xr[qslot],
                                              row0=row0: g.dma_start(
                            xout[row0:row0 + 128, :], o[:, :]),
                            dma=f"dma_o{qslot}", wait=[at[0], at[1]])
                P.wait_only("gpsimd", [sttok[0], sttok[1]])
                P.barrier()


def emit_l3_attn(P, xin, xout, W, name="l3t"):
    nc = P.nc
    with contextlib.ExitStack() as pstk:
        sb_ = lambda n, s, d: pstk.enter_context(nc.sbuf_tensor(f"{name}b_{n}", s, d))
        ps_ = lambda n, s, d: pstk.enter_context(nc.psum_tensor(f"{name}b_{n}", s, d))
        r3 = lambda t, c: t.ap().rearrange("p (c t) -> p c t", c=c)
        xr = [sb_("xr0", [128, D], F32), sb_("xr1", [128, D], F32)]
        sb = {
            "xt0": xr[0], "xt1": xr[1],
            "stats": sb_("stats", [128, 12], F32), "mv": sb_("mv", [128, 2], F32),
            "rstd": sb_("rstd", [128, 1], F32), "xntm": sb_("xntm", [128, D], BF16),
            "ident": sb_("ident", [128, 128], BF16),
        }
        idraw = sb_("idraw", [128, 128], I32)
        ps = {"tr0": ps_("tr0", [128, 1024], BF16).ap()[:, 0:128],
              "tr1": ps_("tr1", [128, 1024], BF16).ap()[:, 0:128]}
        pmm = [ps_("pmm0", [128, 512], F32), ps_("pmm1", [128, 512], F32)]
        pss = ps_("pss", [128, 512], F32)
        pwb = ps_("pwb", [128, 512], F32)
        vfm = sb_("vfm", [128, 2 * DC * T], BF16).ap().rearrange(
            "p (b c t) -> p b c t", b=2, c=DC)
        qwf = sb_("qwf", [128, 2 * DC * T], BF16).ap().rearrange(
            "p (b c t) -> p b c t", b=2, c=DC)
        s2 = sb_("s2", [2, 128], F32)
        esb = sb_("esb", [2, 128], F32)
        wvsb = sb_("wvsb", [2, 128], BF16)
        den = sb_("den", [2, 1], F32)
        rdt = sb_("rdt", [2, 1], F32)
        mask2r = sb_("mask2r", [1, 256], BF16)
        sel2 = sb_("sel2", [1, 4], BF16)
        ones2 = sb_("ones2", [2, 128], BF16)
        eb2 = sb_("eb2", [2, 1], F32)
        eb1 = sb_("eb1", [1, 1], F32)
        ones32 = sb_("ones32", [1, 128], F32)
        wvb = sb_("wvb", [128, 128], BF16)
        junkd = sb_("junkd", [128, 512], BF16)
        junkg = sb_("junkg", [128, 512], BF16)
        w2b11 = sb_("w2b11", [1, 1], F32)

        emit_identity(P, sb["ident"], idraw)
        P.emit("gpsimd", lambda g: g.memset(ones2[:, :], 1.0))
        P.emit("gpsimd", lambda g: g.memset(ones32[:, :], 1.0))
        P.emit("gpsimd", lambda g: g.dma_start(
            mask2r[:, :], W["l3a_mask2"].rearrange("b k -> (b k)")[None, :]
            if False else bass.AP(W["l3a_mask2"].tensor, 0, [[256, 1], [1, 256]])),
            dma="dma_w")
        P.emit("gpsimd", lambda g: g.memset(sel2[:, :], 0.0))
        P.emit("gpsimd", lambda g: g.memset(sel2[0:1, 0:1], 1.0))
        P.emit("gpsimd", lambda g: g.memset(sel2[0:1, 3:4], 1.0))
        ldb = P.emit("gpsimd", lambda g: g.dma_start(
            w2b11[:, :], W["l3a_w2b"][:, :]), dma="dma_w")
        teb = P.emit("scalar", lambda s: s.activation(eb1[:, :], w2b11[:, :],
                                                      AF.Exp), wait=[ldb])
        tk = P.emit("tensor", lambda t: t.matmul(
            pwb[0:2, 0:1], ones32[0:1, 0:2], eb1[0:1, :], start=True, stop=True),
            pe_tok=True, wait=[teb])
        P.emit("vector", lambda v: v.tensor_copy(eb2[:, :], pwb[0:2, 0:1]),
               wait=[tk])
        P.barrier()

        for b in range(B_CORE):
            with contextlib.ExitStack() as astk:
                sa_ = lambda n, s, d: astk.enter_context(
                    nc.sbuf_tensor(f"{name}A{b}_{n}", s, d))
                xn_fm = r3(sa_("xnfm", [128, DC * T], BF16), DC)
                wqw = r3(sa_("wqw", [128, DC * D], BF16), DC)
                wqb = sa_("wqb", [1, D], BF16)
                w2bb = sa_("w2bb", [128, D], BF16)
                qwtm = sa_("qwtm", [128, D], BF16)
                qtm = qwtm
                load_weight_chunks(P, wqw, W["l3a_wqw"], "dma_w")
                P.emit("gpsimd", lambda g: g.dma_start(wqb[:, :],
                                                       W["l3a_wqb"][:, :]),
                       dma="dma_w")
                P.emit("gpsimd", lambda g: g.dma_start(
                    w2bb[:, :], bass.AP(W["l3a_w2w"].tensor, 0,
                                        [[0, 128], [1, D]])), dma="dma_w")
                wtok = ("dma_w", P.count("dma_w"))
                P.wait_only("gpsimd", [wtok])
                emit_ln_to_fm(P, xin, xn_fm, b, sb, ps)
                P.barrier()
                P.emit("vector", lambda v, b=b: v.tensor_copy(
                    vfm[:, b, :, 0:64], xn_fm[:, :, 0:64]))
                evtok = {0: None, 1: None}
                trtok = {0: None, 1: None}
                qprev = None
                for tc in range(16):
                    for half in range(2):
                        slot = half
                        pm = pmm[slot]
                        for dc in range(DC):
                            P.emit("tensor", lambda t, pm=pm, tc=tc, half=half,
                                   dc=dc: t.matmul(
                                pm[:, :], xn_fm[:, dc, tc * 128:(tc + 1) * 128],
                                wqw[:, dc, half * 512:(half + 1) * 512],
                                start=(dc == 0), stop=False),
                                wait=[evtok[slot]] if dc == 0 else [])
                        tk = P.emit("tensor", lambda t, pm=pm, half=half:
                                    t.matmul(
                            pm[:, :], ones2[0:1, :],
                            wqb[0:1, half * 512:(half + 1) * 512],
                            start=False, stop=True), pe_tok=True)
                        evtok[slot] = P.emit("scalar", lambda s, pm=pm, half=half:
                                             s.activation(
                            qtm[:, half * 512:(half + 1) * 512], pm[:, :],
                            AF.Identity), wait=[tk, qprev])
                    tq = P.emit("vector", lambda v: v.tensor_mul(
                        qwtm[:, :], qwtm[:, :], w2bb[:, :]),
                        wait=[evtok[0], evtok[1]])
                    for c in range(DC):
                        pslot = c % 2
                        ptr = ps[f"tr{pslot}"]
                        ttr = P.emit("tensor", lambda t, c=c, ptr=ptr:
                                     t.transpose(
                            ptr[:, :], qwtm[:, c * 128:(c + 1) * 128],
                            sb["ident"][:, :]), pe_tok=True,
                            wait=[tq, trtok[pslot]])
                        col0 = tc * 128
                        eng = "scalar" if pslot == 0 else "vector"
                        trtok[pslot] = P.emit(
                            eng, lambda e, c=c, ptr=ptr, col0=col0, b=b,
                            eng=eng: (
                                e.activation(qwf[:, b, c, col0:col0 + 128],
                                             ptr[:, :], AF.Identity)
                                if eng == "scalar" else
                                e.tensor_copy(qwf[:, b, c, col0:col0 + 128],
                                              ptr[:, :])), wait=[ttr])
                    qprev = ttr
                P.barrier()

        W64 = 64
        prev_tex = None
        prev_tkb = None
        prev_t1a = None
        for t in range(W64, T):
            dvetok = ("dve", P.count("dve"))
            for dc in range(DC):
                tk = P.emit("tensor", lambda tt_, dc=dc, t=t: tt_.matmul(
                    pss[0:2, 0:128],
                    bass.AP(qwf.tensor, qwf.offset + dc * T + t,
                            [[qwf.ap[0][0], 128], [DC * T, 2]]),
                    bass.AP(vfm.tensor, vfm.offset + dc * T + (t - W64),
                            [[vfm.ap[0][0], 128], [DC * T, 2], [1, 64]]),
                    start=(dc == 0), stop=False),
                    wait=[dvetok, prev_tex] if dc == 0 else [])
            for bb_ in range(2):
                tk = P.emit("tensor", lambda tt_, bb_=bb_: tt_.matmul(
                    pss[0:2, 0:128], sel2[0:1, 2 * bb_:2 * bb_ + 2],
                    mask2r[0:1, bb_ * 128:(bb_ + 1) * 128],
                    start=False, stop=(bb_ == 1)), pe_tok=(bb_ == 1))
            tex = P.emit("scalar", lambda s: s.activation(
                esb[:, :], pss[0:2, 0:128], AF.Exp,
                accum_out=den[:, :]), wait=[tk])
            prev_tex = tex
            P.emit("scalar", lambda s: s.activation(
                rdt[:, :], den[:, :], AF.Ln, bias=eb2[:, :]))
            P.emit("scalar", lambda s: s.activation(
                rdt[:, :], rdt[:, :], AF.Exp, scale=-1.0))
            twv = P.emit("scalar", lambda s: s.activation(
                wvsb[:, :], esb[:, :], AF.Identity, scale=rdt[:, 0:1]),
                wait=[prev_tkb])
            tkb = P.emit("tensor", lambda tt_: tt_.matmul(
                pwb[:, 0:128], ones2[:, :], wvsb[:, :], start=True, stop=True),
                pe_tok=True, wait=[twv, prev_t1a])
            prev_tkb = tkb
            tev = P.emit("vector", lambda v: v.tensor_copy(wvb[:, :],
                                                           pwb[:, 0:128]),
                         wait=[tkb])
            prev_t1a = tev
            win0 = bass.AP(vfm.tensor, vfm.offset + (t - W64),
                           [[vfm.ap[0][0], 128], [T, DC], [1, 64]])
            win1 = bass.AP(vfm.tensor, vfm.offset + DC * T + (t - W64),
                           [[vfm.ap[0][0], 128], [T, DC], [1, 64]])
            wb0 = bass.AP(wvb, 0, [[128, 128], [0, DC], [1, 64]])
            wb1 = bass.AP(wvb, 64, [[128, 128], [0, DC], [1, 64]])
            j0 = junkd.ap().rearrange("p (c w) -> p c w", c=DC)
            j1 = junkg.ap().rearrange("p (c w) -> p c w", c=DC)
            t0a = P.emit("vector", lambda v, win0=win0, wb0=wb0: v.tensor_tensor(
                j0, win0, wb0, ALU.mult), wait=[tev])
            t1a = P.emit("gpsimd", lambda g, win1=win1, wb1=wb1: g.tensor_tensor(
                j1, win1, wb1, ALU.mult), wait=[tev])
            def _redv(v, t=t, which=0, src_=j0):
                with nc.allow_low_precision(reason="fp32-internal reduce, "
                                            "bf16 store into V"):
                    return v.tensor_reduce(vfm[:, which, :, t], src_,
                                           axis=AX.X, op=ALU.add)
            P.emit("vector", lambda v, f=_redv: f(v))
            t1b = P.emit("vector", lambda v, t=t: _redv(v, t, 1, j1),
                         wait=[t1a])
        P.barrier()

        with contextlib.ExitStack() as fstk:
            sf_ = lambda n, s, d: fstk.enter_context(
                nc.sbuf_tensor(f"{name}F_{n}", s, d))
            gb = sf_("gb", [128, D], F32)
            bb = sf_("bb", [128, D], F32)
            ftile = sf_("ftile", [128, D], F32)
            vltm = sf_("vltm", [128, D], F32)
            th = sf_("th", [128, D], F32)
            yt = sf_("yt", [128, D], F32)
            sb2 = dict(sb)
            sb2["lntmp"] = th
            P.emit("gpsimd", lambda g: g.dma_start(
                gb[:, :], bass.AP(W["l3a_lng"].tensor, 0, [[0, 128], [1, D]])),
                dma="dma_w")
            P.emit("gpsimd", lambda g: g.dma_start(
                bb[:, :], bass.AP(W["l3a_lnb"].tensor, 0, [[0, 128], [1, D]])),
                dma="dma_w")
            P.wait_only("gpsimd", [("dma_w", P.count("dma_w"))])
            P.barrier()
            sttok = {0: None, 1: None}
            prevf = {0: None, 1: None}
            trtok = {0: None, 1: None}
            for b in range(B_CORE):
                for i in range(16):
                    slot = i % 2
                    xt = xr[slot]
                    row0 = b * T + i * 128
                    ld = P.emit("sync", lambda s, xt=xt, row0=row0: s.dma_start(
                        xt[:, :], xin[row0:row0 + 128, :]), dma=f"dma_x{slot}",
                        wait=[prevf[slot], sttok[slot]])
                    tf = emit_ln_tile(P, xin, row0, xt, ld, sb, ftile[:, :],
                                      eps=1e-5)
                    prevf[slot] = tf
                    for c in range(DC):
                        pslot = c % 2
                        ptr = ps[f"tr{pslot}"]
                        ttr = P.emit("tensor", lambda t, c=c, ptr=ptr, b=b, i=i:
                                     t.transpose(
                            ptr[:, :], vfm[:, b, c, i * 128:(i + 1) * 128],
                            sb["ident"][:, :]), pe_tok=True,
                            wait=[trtok[pslot]])
                        trtok[pslot] = P.emit(
                            "vector", lambda v, c=c, ptr=ptr: v.tensor_copy(
                                vltm[:, c * 128:(c + 1) * 128], ptr[:, :]),
                            wait=[ttr])
                    tsc = trtok[1]
                    if i == 0:
                        tsc = P.emit("vector", lambda v: v.tensor_copy(
                            vltm[0:64, :], ftile[0:64, :]),
                            wait=[tf, trtok[0], trtok[1]])
                    tth = P.emit("scalar", lambda s: s.activation(
                        th[:, :], vltm[:, :], AF.Sigmoid, scale=2.0),
                        wait=[tsc, ("dve", P.count("dve"))])
                    P.emit("vector", lambda v: v.tensor_scalar(
                        th[:, :], th[:, :], 2.0, None, ALU.mult), wait=[tth])
                    ty = P.emit("vector", lambda v: v.tensor_mul(
                        yt[:, :], th[:, :], ftile[:, :]))
                    tln = emit_ln_tile(P, xin, row0, yt, ty, sb2, yt[:, :],
                                       eps=1e-5, gain=gb[:, :], bias_v=bb[:, :])
                    ta = P.emit("vector", lambda v, xt=xt: v.tensor_add(
                        xt[:, :], yt[:, :], xt[:, :]), wait=[tln])
                    prevf[slot] = ta
                    sttok[slot] = P.emit("gpsimd", lambda g, o=xt, row0=row0:
                                         g.dma_start(xout[row0:row0 + 128, :],
                                                     o[:, :]),
                                         dma=f"dma_o{slot}",
                                         wait=[ta, sttok[slot]])
            P.wait_only("gpsimd", [sttok[0], sttok[1]])
            P.barrier()


def emit_l0_ffn(P, xin, xout, W, name="l0f"):
    nc = P.nc
    FS, IS, RBIG = 12288, 4096, 0.7
    FC = FS // 128
    xs = [nc.dram_tensor(f"{name}_xs{i}", [NTOK, D], F32, kind="Internal").ap()
          for i in range(2)]
    cci = nc.dram_tensor(f"{name}_cci", [128, 128], F32, kind="Internal")
    cco = nc.dram_tensor(f"{name}_cco", [128, 128], F32, kind="Internal")
    with contextlib.ExitStack() as pstk:
        sb_ = lambda n, s, d: pstk.enter_context(nc.sbuf_tensor(f"{name}b_{n}", s, d))
        ps_ = lambda n, s, d: pstk.enter_context(nc.psum_tensor(f"{name}b_{n}", s, d))
        r3 = lambda t, c: t.ap().rearrange("p (c t) -> p c t", c=c)
        xr = [sb_("xr0", [128, D], F32), sb_("xr1", [128, D], F32)]
        x2t = [sb_("x2t0", [128, D], F32), sb_("x2t1", [128, D], F32)]
        sb = {
            "xt0": xr[0], "xt1": xr[1],
            "stats": sb_("stats", [128, 12], F32), "mv": sb_("mv", [128, 2], F32),
            "rstd": sb_("rstd", [128, 1], F32), "xntm": sb_("xntm", [128, D], BF16),
            "ident": sb_("ident", [128, 128], BF16),
        }
        idraw = sb_("idraw", [128, 128], I32)
        ps = {"tr0": ps_("tr0", [128, 1024], BF16).ap()[:, 0:128],
              "tr1": ps_("tr1", [128, 1024], BF16).ap()[:, 0:128]}
        pz = [ps_("pz0", [128, 512], F32), ps_("pz1", [128, 512], F32)]
        pa = [ps_("pa0", [128, 512], F32), ps_("pa1", [128, 512], F32),
              ps_("pa2", [128, 512], F32), ps_("pa3", [128, 512], F32)]
        xc = r3(sb_("xc", [128, DC * NTOK], BF16), DC)
        _wmraw = [sb_("wma", [128, DC * 1024], BF16),
                  sb_("wmb", [128, DC * 1024], BF16)]
        _wpraw = [sb_("wpa", [128, DC * 1024], BF16),
                  sb_("wpb", [128, DC * 1024], BF16)]
        wm = [r3(t, DC) for t in _wmraw]
        wp = [r3(t, DC) for t in _wpraw]
        wc2v = [t.ap().rearrange("p (rc f) -> p rc f", rc=2)
                for t in (_wmraw[0], _wmraw[1], _wpraw[0])]
        hbuf = r3(sb_("hbuf", [128, DC * 256], BF16), DC)
        wc1t = r3(sb_("wc1t", [128, DC * 256], BF16), DC)
        nm1 = sb_("nm1", [128, 2 * FC], F32).ap().rearrange(
            "p (b f) -> p b f", b=2)
        nm2 = sb_("nm2", [128, 2 * FC], F32).ap().rearrange(
            "p (b f) -> p b f", b=2)
        cmb = sb_("cmb", [128, 2 * FC], F32).ap().rearrange(
            "p (b f) -> p b f", b=2)
        a3m = sb_("a3m", [128, 2 * FC], F32).ap().rearrange(
            "p (b f) -> p b f", b=2)
        ck = sb_("ck", [128, 2 * FC], F32).ap().rearrange(
            "p (b f) -> p b f", b=2)
        idxg = sb_("idxg", [128, FC], F32)
        eqb = sb_("eqb", [128, FC], F32)
        junk = sb_("junk", [128, FC], F32)
        junk2 = sb_("junk2", [128, FC], F32)
        pooled = sb_("pooled", [128, DC * 2], BF16).ap().rearrange(
            "p (c b) -> p c b", c=DC)
        prow = sb_("prow", [1, D], BF16)
        ubf = sb_("ubf", [128, 2 * 2], BF16).ap().rearrange(
            "p (rc b) -> p rc b", rc=2)
        onescol32 = sb_("onescol32", [128, 1], F32)
        ones32 = sb_("ones32", [1, 128], F32)
        candbc = sb_("candbc", [128, 1], F32)
        pcnt = sb_("pcnt", [128, 1], F32)
        zztile = sb_("zztile", [128, 128], F32)
        dbc = sb_("dbc", [128, 4], F32)
        cells = sb_("cells", [1, 24], F32)
        icells = sb_("icells", [1, 24], I32)
        CAND, Q, CNT, R_, J, TF, TH, TL, T2H, T2L, QPM, QPN, TPOS, TNEG, TSTar, \
            GT_, D2, D3, NV, TMP1, TMP2 = range(21)

        def fcell(i):
            return cells[0:1, i:i + 1]

        def icell(i):
            return icells[0:1, i:i + 1]

        emit_identity(P, sb["ident"], idraw)
        P.emit("gpsimd", lambda g: g.memset(onescol32[:, :], 1.0))
        P.emit("gpsimd", lambda g: g.memset(ones32[:, :], 1.0))
        P.emit("gpsimd", lambda g: g.memset(zztile[:, :], 0.0))
        P.emit("gpsimd", lambda g: g.iota(idxg[:, :], [[128, FC]], base=0,
                                          channel_multiplier=1,
                                          allow_small_or_imprecise_dtypes=True))
        P.emit("gpsimd", lambda g: g.memset(cmb[:, :, :], 0.0))
        P.emit("gpsimd", lambda g: g.memset(cmb[:, 0, 0:32], 1.0))
        P.emit("gpsimd", lambda g: g.memset(cmb[:, 1, 0:32], 1.0))
        load_weight_chunks(P, wc1t, W["l0f_wc1t"], "dma_w")
        P.wait_only("gpsimd", [("dma_w", P.count("dma_w"))])
        P.barrier()

        for b in range(B_CORE):
            emit_ln_to_fm(P, xin, xc[:, :, b * T:(b + 1) * T], b, sb, ps)
            P.barrier()

        wld = {0: None, 1: None}
        wuse = {0: None, 1: None}

        def emit_step(step, nblocks, mask_ap, xs_out):
            zev = {0: None, 1: None}
            acev = {0: None, 1: None, 2: None, 3: None}
            sttok = {0: None, 1: None}
            ldpX = {0: None, 1: None}
            for tg in range(16):
                b = tg // 8
                tok0 = tg * 256
                for fb in range(nblocks):
                    es = fb % 2
                    wmt = P.emit("gpsimd", lambda g, es=es, fb=fb: g.dma_start(
                        wm[es][:, :, :],
                        W["l0f_wmt"][:, fb * 1024:(fb + 1) * 1024]
                        .rearrange("(c p) f -> p c f", p=128)), dma=f"dma_m{es}",
                        wait=[wuse[es]])
                    P.emit("gpsimd", lambda g, es=es, fb=fb: g.dma_start(
                        wp[es][:, :, :],
                        W["l0f_wpt"][fb * 1024:(fb + 1) * 1024, :]
                        .rearrange("(c p) f -> p c f", p=128)), dma=f"dma_m{es}")
                    wldt = (f"dma_m{es}", P.count(f"dma_m{es}"))
                    for fc in range(DC):
                        zs = fc % 2
                        pmz = pz[zs]
                        for dc in range(DC):
                            tk = P.emit("tensor", lambda t, pmz=pmz, es=es, fc=fc,
                                        dc=dc, tok0=tok0: t.matmul(
                                pmz[:, 0:256],
                                wm[es][:, dc, fc * 128:(fc + 1) * 128],
                                xc[:, dc, tok0:tok0 + 256],
                                start=(dc == 0), stop=(dc == DC - 1)),
                                pe_tok=(dc == DC - 1),
                                wait=[zev[zs], wldt] if dc == 0 else [])
                        if mask_ap is None:
                            zev[zs] = P.emit("vector", lambda v, pmz=pmz, fc=fc:
                                             v.tensor_scalar(
                                hbuf[:, fc, :], pmz[:, 0:256], 0.0, None,
                                ALU.max), wait=[tk])
                        else:
                            fg = fb * 8 + fc
                            zev[zs] = P.emit("vector", lambda v, pmz=pmz, fc=fc,
                                             fg=fg, b=b: v.tensor_scalar(
                                hbuf[:, fc, :], pmz[:, 0:256], 0.0,
                                mask_ap[:, b, fg:fg + 1], ALU.max, ALU.mult),
                                wait=[tk])
                    for tt in range(2):
                        for ch in range(2):
                            pslot = tt * 2 + ch
                            pmx = pa[pslot]
                            first = (fb == 0)
                            last = (fb == nblocks - 1)
                            wl = [zev[0], zev[1]]
                            if first:
                                wl.append(acev[pslot])
                            for fc in range(DC):
                                tk = P.emit("tensor", lambda t, pmx=pmx, es=es,
                                            tt=tt, ch=ch, fc=fc, first=first,
                                            last=last: t.matmul(
                                    pmx[:, :], hbuf[:, fc, tt * 128:(tt + 1) * 128],
                                    wp[es][:, fc, ch * 512:(ch + 1) * 512],
                                    start=(first and fc == 0),
                                    stop=(last and fc == DC - 1)),
                                    pe_tok=(fc == DC - 1),
                                    wait=wl if fc == 0 else [])
                            if last:
                                acev[pslot] = tk
                    wuse[es] = tk
                for tt in range(2):
                    qslot = tt
                    row0 = tok0 + tt * 128
                    if step >= 2:
                        ldp = P.emit("sync", lambda s, x_=x2t[qslot], row0=row0:
                                     s.dma_start(x_[:, :],
                                                 xs[step - 2][row0:row0 + 128, :]),
                                     dma=f"dma_p{qslot}", wait=[ldpX[qslot]])
                    if step == 3:
                        ldx = P.emit("sync", lambda s, x_=xr[qslot], row0=row0:
                                     s.dma_start(x_[:, :],
                                                 xin[row0:row0 + 128, :]),
                                     dma=f"dma_q{qslot}", wait=[sttok[qslot]])
                    ats = []
                    for ch in range(2):
                        pslot = tt * 2 + ch
                        pmx = pa[pslot]
                        if step == 1:
                            at = P.emit("vector", lambda v, pmx=pmx, x_=x2t[qslot],
                                        ch=ch: v.tensor_copy(
                                x_[:, ch * 512:(ch + 1) * 512], pmx[:, :]),
                                wait=[acev[pslot], ldpX[qslot]])
                        elif step == 2:
                            P.emit("vector", lambda v, x_=x2t[qslot], ch=ch:
                                   v.tensor_scalar(
                                x_[:, ch * 512:(ch + 1) * 512],
                                x_[:, ch * 512:(ch + 1) * 512], dbc[:, 0:1],
                                None, ALU.mult), wait=[ldp, acev[pslot]])
                            at = P.emit("vector", lambda v, pmx=pmx,
                                        x_=x2t[qslot], ch=ch:
                                        v.scalar_tensor_tensor(
                                x_[:, ch * 512:(ch + 1) * 512], pmx[:, :],
                                dbc[:, 1:2], x_[:, ch * 512:(ch + 1) * 512],
                                ALU.mult, ALU.add))
                        else:
                            P.emit("vector", lambda v, x_=x2t[qslot], ch=ch:
                                   v.tensor_scalar(
                                x_[:, ch * 512:(ch + 1) * 512],
                                x_[:, ch * 512:(ch + 1) * 512], dbc[:, 2:3],
                                None, ALU.mult), wait=[ldp, acev[pslot]])
                            P.emit("vector", lambda v, pmx=pmx, x_=x2t[qslot],
                                   ch=ch: v.scalar_tensor_tensor(
                                x_[:, ch * 512:(ch + 1) * 512], pmx[:, :],
                                dbc[:, 3:4], x_[:, ch * 512:(ch + 1) * 512],
                                ALU.mult, ALU.add))
                            at = P.emit("vector", lambda v, x_=x2t[qslot],
                                        xo=xr[qslot], ch=ch: v.tensor_add(
                                x_[:, ch * 512:(ch + 1) * 512],
                                x_[:, ch * 512:(ch + 1) * 512],
                                xo[:, ch * 512:(ch + 1) * 512]),
                                wait=[ldx])
                        acev[pslot] = at
                        ats.append(at)
                    dst = xout if step == 3 else xs_out
                    sttok[qslot] = P.emit("gpsimd", lambda g, o=x2t[qslot],
                                          row0=row0, dst=dst: g.dma_start(
                        dst[row0:row0 + 128, :], o[:, :]),
                        dma=f"dma_o{qslot}", wait=[at])
                    ldpX[qslot] = sttok[qslot]
            P.wait_only("gpsimd", [sttok[0], sttok[1]])
            P.barrier()

        def emit_load_fm(xs_in):
            trtok = {0: None, 1: None}
            cptok = {0: None, 1: None}
            prev = {0: None, 1: None}
            prevm = {0: None, 1: None}
            pmt = None
            pguard = None
            for i in range(32):
                b = i // 16
                slot = i % 2
                xt = xr[slot]
                row0 = i * 128
                ld = P.emit("sync", lambda s, xt=xt, row0=row0: s.dma_start(
                    xt[:, :], xs_in[row0:row0 + 128, :]), dma=f"dma_x{slot}",
                    wait=[prev[slot], prevm[slot]])
                txn = P.emit("vector", lambda v, xt=xt: v.tensor_copy(
                    sb["xntm"][:, :], xt[:, :]),
                    wait=[ld, trtok[0], trtok[1]])
                prev[slot] = txn
                for hh in range(2):
                    pmx = pz[hh]
                    pmt = P.emit("tensor", lambda t, pmx=pmx, hh=hh, xt=xt, i=i:
                                 t.matmul(
                        pmx[0:1, :], onescol32[:, :],
                        xt[:, hh * 512:(hh + 1) * 512],
                        start=(i % 16 == 0), stop=(i % 16 == 15)),
                        pe_tok=(hh == 1),
                        wait=[txn, pguard] if i % 16 == 0 else [txn])
                prevm[slot] = pmt
                for c in range(DC):
                    pslot = c % 2
                    ptr = ps[f"tr{pslot}"]
                    ttr = P.emit("tensor", lambda t, c=c, ptr=ptr: t.transpose(
                        ptr[:, :], sb["xntm"][:, c * 128:(c + 1) * 128],
                        sb["ident"][:, :]), pe_tok=True,
                        wait=[txn, cptok[pslot]])
                    col0 = i * 128
                    eng = "scalar" if pslot == 0 else "vector"
                    cptok[pslot] = P.emit(
                        eng, lambda e, c=c, ptr=ptr, col0=col0, eng=eng: (
                            e.activation(xc[:, c, col0:col0 + 128], ptr[:, :],
                                         AF.Identity) if eng == "scalar" else
                            e.tensor_copy(xc[:, c, col0:col0 + 128],
                                          ptr[:, :])), wait=[ttr])
                    trtok[pslot] = cptok[pslot]
                if i % 16 == 15:
                    e1 = P.emit("scalar", lambda s, b=b: s.activation(
                        prow[0:1, 0:512], pz[0][0:1, :], AF.Identity,
                        scale=1.0 / T), wait=[pmt])
                    e2 = P.emit("scalar", lambda s, b=b: s.activation(
                        prow[0:1, 512:1024], pz[1][0:1, :], AF.Identity,
                        scale=1.0 / T))
                    pguard = e2
                    for c in range(DC):
                        pslot = c % 2
                        ptr = ps[f"tr{pslot}"]
                        ttr = P.emit("tensor", lambda t, c=c, ptr=ptr: t.transpose(
                            ptr[:, 0:1], prow[0:1, c * 128:(c + 1) * 128],
                            sb["ident"][0:1, 0:1]), pe_tok=True,
                            wait=[e1, e2, cptok[pslot]])
                        cptok[pslot] = P.emit(
                            "vector", lambda v, c=c, ptr=ptr, b=b: v.tensor_copy(
                                pooled[:, c, b:b + 1], ptr[:, 0:1]), wait=[ttr])
                        trtok[pslot] = cptok[pslot]
            P.barrier()

        def emit_ck():
            for j in range(3):
                P.emit("gpsimd", lambda g, j=j: g.dma_start(
                    wc2v[j][:, :, :],
                    W["l0f_wc2t"][:, j * 4096:(j + 1) * 4096]
                    .rearrange("(rc p) f -> p rc f", p=128)), dma="dma_w")
            wc2tok = ("dma_w", P.count("dma_w"))
            uev = None
            for rc in range(2):
                pmz = pz[rc]
                for dc in range(DC):
                    tk = P.emit("tensor", lambda t, pmz=pmz, rc=rc, dc=dc:
                                t.matmul(
                        pmz[0:128, 0:2], wc1t[:, dc, rc * 128:(rc + 1) * 128],
                        pooled[:, dc, :], start=(dc == 0), stop=(dc == DC - 1)),
                        pe_tok=(dc == DC - 1), wait=[uev] if dc == 0 else [])
                uev = P.emit("vector", lambda v, pmz=pmz, rc=rc: v.tensor_scalar(
                    ubf[:, rc, :], pmz[0:128, 0:2], 0.0, None, ALU.max),
                    wait=[tk])
            ckev = {0: None, 1: None}
            for fc in range(FC):
                zs = fc % 2
                pmz = pz[zs]
                wcv = wc2v[fc // 32]
                fo = (fc % 32) * 128
                for rc in range(2):
                    tk = P.emit("tensor", lambda t, pmz=pmz, fo=fo, rc=rc,
                                wcv=wcv: t.matmul(
                        pmz[0:128, 0:2], wcv[:, rc, fo:fo + 128],
                        ubf[:, rc, :], start=(rc == 0), stop=(rc == 1)),
                        pe_tok=(rc == 1),
                        wait=[ckev[zs], uev, wc2tok] if rc == 0 else [])
                ckev[zs] = P.emit("vector", lambda v, pmz=pmz, fc=fc:
                                  v.tensor_copy(ck[:, :, fc], pmz[0:128, 0:2]),
                                  wait=[tk])
            wuse[0] = ("pe", P.count("pe"))
            wuse[1] = ("pe", P.count("pe"))
            return [ckev[0], ckev[1]]

        def bcast11(src_ap, dst_ap, waits):
            tk = P.emit("tensor", lambda t: t.matmul(
                pa[0][:, 0:1], ones32[:, :], src_ap, start=True, stop=True),
                pe_tok=True, wait=list(waits) + [("dve", P.count("dve"))])
            return P.emit("vector", lambda v: v.tensor_copy(dst_ap, pa[0][:, 0:1]),
                          wait=[tk])

        def count_ge(ck_b, thbc_tok, out_cell, op=ALU.is_ge):
            t1 = P.emit("vector", lambda v, ck_b=ck_b, op=op: v.tensor_scalar(
                junk[:, :], ck_b, candbc[:, :], None, op, ALU.add,
                accum_out=pcnt[:, :]),
                wait=[thbc_tok, ("pe", P.count("pe"))])
            t2 = P.emit("tensor", lambda t: t.matmul(
                pa[1][0:1, 0:1], pcnt[:, :], onescol32[:, :], start=True,
                stop=True), pe_tok=True, wait=[t1])
            return P.emit("vector", lambda v, out_cell=out_cell: v.tensor_copy(
                out_cell, pa[1][0:1, 0:1]), wait=[t2])

        def emit_topk(b, out_mask):
            ck_b = ck[:, b, :]
            z8 = P.emit("vector", lambda v: v.memset(icells[:, :], 0))
            tb = bcast11(fcell(TMP1), candbc[:, :], [P.emit(
                "vector", lambda v: v.memset(cells[0:1, TMP1:TMP1 + 1], 0.0))])
            cq = count_ge(ck_b, tb, fcell(QPM))
            P.emit("vector", lambda v: v.tensor_scalar(
                fcell(QPM), fcell(QPM), float(IS), None, ALU.is_ge))
            for bit in range(30, -1, -1):
                hb = float(1 << (bit - 16)) if bit >= 16 else 0.0
                lb = float(1 << bit) if bit < 16 else 0.0
                P.emit("vector", lambda v, hb=hb: v.tensor_scalar(
                    icell(T2H), icells[0:1, TH:TH + 1], hb, None, ALU.add))
                P.emit("vector", lambda v, lb=lb: v.tensor_scalar(
                    icell(T2L), icells[0:1, TL:TL + 1], lb, None, ALU.add))
                tc_ = P.emit("vector", lambda v: v.tensor_scalar(
                    icell(CAND), icell(T2H), 16, icell(T2L), ALU.logical_shift_left,
                    ALU.bitwise_or))
                tbb = bcast11(icell(CAND).bitcast(F32), candbc[:, :], [tc_])
                count_ge(ck_b, tbb, fcell(CNT))
                tq = P.emit("vector", lambda v: v.tensor_scalar(
                    fcell(Q), fcell(CNT), float(IS), None, ALU.is_ge))
                P.emit("vector", lambda v: v.tensor_copy(icell(Q), fcell(Q)))
                if bit >= 16:
                    P.emit("vector", lambda v, s=bit - 16: v.tensor_scalar(
                        icell(TMP1), icell(Q), s, None, ALU.logical_shift_left))
                    P.emit("vector", lambda v: v.tensor_tensor(
                        icell(TH), icell(TH), icell(TMP1), ALU.add))
                else:
                    P.emit("vector", lambda v, s=bit: v.tensor_scalar(
                        icell(TMP1), icell(Q), s, None, ALU.logical_shift_left))
                    P.emit("vector", lambda v: v.tensor_tensor(
                        icell(TL), icell(TL), icell(TMP1), ALU.add))
            P.emit("vector", lambda v: v.tensor_scalar(
                icell(TPOS), icell(TH), 16, icell(TL), ALU.logical_shift_left,
                ALU.bitwise_or))
            P.emit("vector", lambda v: v.memset(icell(T2H), 0))
            P.emit("vector", lambda v: v.memset(icell(T2L), 0))
            for bit in range(30, -1, -1):
                fill = (1 << bit) - 1
                fh, fl = float(fill >> 16), float(fill & 0xFFFF)
                P.emit("vector", lambda v, fh=fh: v.tensor_scalar(
                    icell(TH), icell(T2H), fh, None, ALU.add))
                P.emit("vector", lambda v, fl=fl: v.tensor_scalar(
                    icell(TL), icell(T2L), fl, None, ALU.add))
                tc_ = P.emit("vector", lambda v: v.tensor_scalar(
                    icell(CAND), icell(TH), 16, icell(TL), ALU.logical_shift_left,
                    ALU.bitwise_or))
                tc2 = P.emit("vector", lambda v: v.tensor_scalar(
                    icell(CAND), icell(CAND), -2147483648, None, ALU.bitwise_or))
                tbb = bcast11(icell(CAND).bitcast(F32), candbc[:, :], [tc2])
                count_ge(ck_b, tbb, fcell(CNT))
                P.emit("vector", lambda v: v.tensor_scalar(
                    fcell(Q), fcell(CNT), float(IS), None, ALU.is_lt))
                P.emit("vector", lambda v: v.tensor_copy(icell(Q), fcell(Q)))
                if bit >= 16:
                    P.emit("vector", lambda v, s=bit - 16: v.tensor_scalar(
                        icell(TMP1), icell(Q), s, None, ALU.logical_shift_left))
                    P.emit("vector", lambda v: v.tensor_tensor(
                        icell(T2H), icell(T2H), icell(TMP1), ALU.add))
                else:
                    P.emit("vector", lambda v, s=bit: v.tensor_scalar(
                        icell(TMP1), icell(Q), s, None, ALU.logical_shift_left))
                    P.emit("vector", lambda v: v.tensor_tensor(
                        icell(T2L), icell(T2L), icell(TMP1), ALU.add))
            P.emit("vector", lambda v: v.tensor_scalar(
                icell(TNEG), icell(T2H), 16, icell(T2L), ALU.logical_shift_left,
                ALU.bitwise_or))
            P.emit("vector", lambda v: v.tensor_scalar(
                icell(TNEG), icell(TNEG), -2147483648, None, ALU.bitwise_or))
            P.emit("vector", lambda v: v.tensor_copy(icell(TMP1), fcell(QPM)))
            P.emit("vector", lambda v: v.tensor_scalar(
                icell(TMP1), icell(TMP1), -1.0, None, ALU.mult))
            P.emit("vector", lambda v: v.tensor_scalar(
                icell(TMP2), icell(TMP1), 1.0, -1.0, ALU.add, ALU.mult))
            P.emit("vector", lambda v: v.tensor_scalar(
                icell(TPOS), icell(TPOS), icells[0:1, TMP1:TMP1 + 1], None,
                ALU.bitwise_and))
            P.emit("vector", lambda v: v.tensor_scalar(
                icell(TNEG), icell(TNEG), icells[0:1, TMP2:TMP2 + 1], None,
                ALU.bitwise_and))
            tsel = P.emit("vector", lambda v: v.tensor_scalar(
                icell(TSTar), icell(TPOS), icells[0:1, TNEG:TNEG + 1], None,
                ALU.bitwise_or))
            tbb = bcast11(icell(TSTar).bitcast(F32), candbc[:, :], [tsel])
            count_ge(ck_b, tbb, fcell(GT_), op=ALU.is_gt)
            P.emit("vector", lambda v: v.tensor_scalar(
                fcell(R_), fcell(GT_), -1.0, float(IS), ALU.mult, ALU.add))
            teq = P.emit("vector", lambda v, ck_b=ck_b: v.tensor_scalar(
                eqb[:, :], ck_b, candbc[:, :], None, ALU.is_equal))
            P.emit("vector", lambda v: v.memset(cells[0:1, J:J + 1], 0.0))
            for bit in range(13, -1, -1):
                bv = float(1 << bit)
                P.emit("vector", lambda v, bv=bv: v.tensor_scalar(
                    fcell(TMP1), fcell(J), bv - 1.0, None, ALU.add))
                tb2 = bcast11(fcell(TMP1), candbc[:, :],
                              [("dve", P.count("dve"))])
                t1 = P.emit("vector", lambda v: v.tensor_scalar(
                    junk[:, :], idxg[:, :], candbc[:, :], None, ALU.is_le),
                    wait=[tb2])
                P.emit("vector", lambda v: v.tensor_mul(junk2[:, :], junk[:, :],
                                                        eqb[:, :]))
                t3 = P.emit("vector", lambda v: v.tensor_scalar(
                    junk2[:, :], junk2[:, :], 0.0, None, ALU.add, ALU.add,
                    accum_out=pcnt[:, :]), wait=[("pe", P.count("pe"))])
                t4 = P.emit("tensor", lambda t: t.matmul(
                    pa[1][0:1, 0:1], pcnt[:, :], onescol32[:, :], start=True,
                    stop=True), pe_tok=True, wait=[t3])
                P.emit("vector", lambda v: v.tensor_copy(fcell(CNT),
                                                         pa[1][0:1, 0:1]),
                       wait=[t4])
                P.emit("vector", lambda v: v.tensor_tensor(
                    fcell(TF), fcell(CNT), fcell(R_), ALU.is_lt))
                P.emit("vector", lambda v, bv=bv: v.tensor_scalar(
                    fcell(TMP1), fcell(TF), bv, None, ALU.mult))
                P.emit("vector", lambda v: v.tensor_tensor(
                    fcell(J), fcell(J), fcell(TMP1), ALU.add))
            tbj = bcast11(fcell(J), candbc[:, :], [("dve", P.count("dve"))])
            tbt = bcast11(icell(TSTar).bitcast(F32), pcnt[:, :], [tbj])
            tgt = P.emit("vector", lambda v, ck_b=ck_b: v.tensor_scalar(
                junk[:, :], ck_b, pcnt[:, :], None, ALU.is_gt), wait=[tbt])
            P.emit("vector", lambda v: v.tensor_scalar(
                junk2[:, :], idxg[:, :], candbc[:, :], None, ALU.is_le),
                wait=[tbj])
            P.emit("vector", lambda v: v.tensor_mul(junk2[:, :], junk2[:, :],
                                                    eqb[:, :]))
            tm = P.emit("vector", lambda v, out_mask=out_mask: v.tensor_add(
                out_mask, junk[:, :], junk2[:, :]))
            return tm

        def emit_novelty_flags(nm, dcell):
            sums = []
            tmm = None
            for j, (src, sgn) in enumerate(((0, 1.0), (1, 1.0),
                                            (0, -1.0), (1, -1.0))):
                if sgn > 0:
                    tacc = P.emit("vector", lambda v, src=src: v.tensor_tensor(
                        junk[:, :], cmb[:, src, :], nm[:, src, :], ALU.max),
                        wait=[tmm])
                    tacc = P.emit("vector", lambda v: v.tensor_scalar(
                        junk2[:, :], junk[:, :], 0.0, None, ALU.add, ALU.add,
                        accum_out=pcnt[:, :]))
                else:
                    tacc = P.emit("vector", lambda v, src=src: v.tensor_scalar(
                        junk2[:, :], cmb[:, src, :], -1.0, None, ALU.mult,
                        ALU.add, accum_out=pcnt[:, :]), wait=[tmm])
                tmm = P.emit("tensor", lambda t, j=j: t.matmul(
                    pa[1][0:1, 0:1], pcnt[:, :], onescol32[:, :],
                    start=(j == 0), stop=(j == 3)), pe_tok=True, wait=[tacc])
            P.emit("vector", lambda v, nm=nm: v.tensor_tensor(
                cmb[:, 0, :], cmb[:, 0, :], nm[:, 0, :], ALU.max), wait=[tmm])
            P.emit("vector", lambda v, nm=nm: v.tensor_tensor(
                cmb[:, 1, :], cmb[:, 1, :], nm[:, 1, :], ALU.max))
            tz = P.emit("vector", lambda v: v.tensor_copy(
                zztile[0:1, 0:1], pa[1][0:1, 0:1]), wait=[tmm])
            st = P.emit("gpsimd", lambda g: g.dma_start(cci.ap(), zztile[:, :]),
                        dma="dma_c", wait=[tz])
            ar = P.emit("gpsimd", lambda g: g.collective_compute(
                "AllReduce", ALU.add, replica_groups=L0FFN_GROUPS,
                ins=[cci.ap().opt()], outs=[cco.ap().opt()]), wait=[st])
            ld = P.emit("sync", lambda s: s.dma_start(zztile[:, :], cco.ap()),
                        dma="dma_cl", wait=[ar])
            tk = P.emit("vector", lambda v, dcell=dcell: v.tensor_scalar(
                dcell, zztile[0:1, 0:1], L0FFN_THRESH, None, ALU.is_lt),
                wait=[ld])
            return tk

        emit_step(1, 4, None, xs[0])
        emit_load_fm(xs[0])
        ckt = emit_ck()
        P.wait_only("vector", ckt)
        emit_topk(0, nm1[:, 0, :])
        emit_topk(1, nm1[:, 1, :])
        t2 = emit_novelty_flags(nm1, fcell(D2))
        P.emit("vector", lambda v: v.tensor_scalar(
            fcell(TMP1), fcell(D2), -1.0, 1.0, ALU.mult, ALU.add), wait=[t2])
        bcast11(fcell(D2), dbc[:, 0:1], [("dve", P.count("dve"))])
        bcast11(fcell(TMP1), dbc[:, 1:2], [("dve", P.count("dve"))])
        P.barrier()
        emit_step(2, 12, nm1, xs[1])
        emit_load_fm(xs[1])
        ckt = emit_ck()
        P.wait_only("vector", ckt)
        emit_topk(0, nm2[:, 0, :])
        emit_topk(1, nm2[:, 1, :])
        t3 = emit_novelty_flags(nm2, fcell(D3))
        P.emit("vector", lambda v: v.tensor_tensor(
            fcell(D3), fcell(D3), fcell(D2), ALU.max), wait=[t3])
        P.emit("vector", lambda v: v.tensor_scalar(
            fcell(TMP1), fcell(D3), -1.0, 1.0, ALU.mult, ALU.add))
        bcast11(fcell(D3), dbc[:, 2:3], [("dve", P.count("dve"))])
        bcast11(fcell(TMP1), dbc[:, 3:4], [("dve", P.count("dve"))])
        for bb in range(2):
            P.emit("vector", lambda v, bb=bb: v.tensor_scalar(
                a3m[:, bb, :], nm1[:, bb, :], dbc[:, 0:1], None, ALU.mult))
            P.emit("vector", lambda v, bb=bb: v.scalar_tensor_tensor(
                a3m[:, bb, :], nm2[:, bb, :], dbc[:, 1:2], a3m[:, bb, :],
                ALU.mult, ALU.add))
        P.barrier()
        emit_step(3, 12, a3m, None)


def emit_l2_ffn(P, xin, xout, W, name="l2f"):
    nc = P.nc
    midd = nc.dram_tensor(f"{name}_mid", [NTOK, D], F32, kind="Internal").ap()
    xnd = nc.dram_tensor(f"{name}_xn", [NTOK, D], F32, kind="Internal").ap()
    with contextlib.ExitStack() as pstk:
        sb_ = lambda n, s, d: pstk.enter_context(nc.sbuf_tensor(f"{name}b_{n}", s, d))
        ps_ = lambda n, s, d: pstk.enter_context(nc.psum_tensor(f"{name}b_{n}", s, d))
        r3 = lambda t, c: t.ap().rearrange("p (c t) -> p c t", c=c)
        xr = [sb_("xr0", [128, D], F32), sb_("xr1", [128, D], F32)]
        xnt = [sb_("xnt0", [128, D], F32), sb_("xnt1", [128, D], F32)]
        sb = {
            "xt0": xr[0], "xt1": xr[1],
            "stats": sb_("stats", [128, 12], F32), "mv": sb_("mv", [128, 2], F32),
            "rstd": sb_("rstd", [128, 1], F32), "xntm": sb_("xntm", [128, D], BF16),
            "ident": sb_("ident", [128, 128], BF16),
            "lntmp": sb_("lntmp", [128, D], F32),
        }
        idraw = sb_("idraw", [128, 128], I32)
        ps = {"tr0": ps_("tr0", [128, 1024], BF16).ap()[:, 0:128],
              "tr1": ps_("tr1", [128, 1024], BF16).ap()[:, 0:128]}
        pmm = [ps_("pmm0", [128, 512], F32), ps_("pmm1", [128, 512], F32)]
        pss4 = ps_("pss4", [128, 2048], F32)
        pvv = [pss4.ap()[:, 0:512], pss4.ap()[:, 512:1024]]
        pmb = [pss4.ap()[:, 1024:1536], pss4.ap()[:, 1536:2048]]
        nf = r3(sb_("nf", [128, DC * T], BF16), DC)
        g1b = sb_("g1b", [128, D], F32)
        b1b = sb_("b1b", [128, D], F32)
        g2b, b2b = g1b, b1b
        qkvb = sb_("qkvb", [128, 2 * 2 * DC], F32).ap().rearrange(
            "p (w m c) -> p w m c", w=2, m=2)
        qbsc = sb_("qbsc", [128, 2 * DC], F32).ap().rearrange(
            "p (m c) -> p m c", m=2)
        vbrow = sb_("vbrow", [1, D], BF16)
        borow = sb_("borow", [1, D], BF16)
        bosc = sb_("bosc", [1, D], BF16)
        onescol = sb_("onescol", [128, 1], BF16)
        ones32 = sb_("ones32", [1, 128], F32)
        onesr = sb_("onesr", [1, 128], BF16)
        mrow = sb_("mrow", [1, D], BF16)
        mnfm = sb_("mnfm", [128, DC], BF16)
        rwm = sb_("rwm", [128, DC * 2], BF16).ap().rearrange(
            "p (c e) -> p c e", c=DC)
        rwf = sb_("rwf", [128, DC * 8], BF16).ap().rearrange(
            "p (c e) -> p c e", c=DC)
        rbm = sb_("rbm", [1, 2], F32)
        rbf = sb_("rbf", [1, 8], F32)
        lrow = sb_("lrow", [1, 8], F32)
        prow = sb_("prow", [1, 8], F32)
        jrow = sb_("jrow", [1, 8], F32)
        ei8 = sb_("ei8", [1, 8], F32)
        eim8 = sb_("eim8", [1, 8], F32)
        sc = [sb_("sc" + str(i), [1, 1], F32) for i in range(8)]
        pb = sb_("pb", [128, 4], F32)
        iotac = sb_("iotac", [128, 16], F32)
        idxf = sb_("idxf", [128, 16], F32)
        idxu = sb_("idxu", [128, 16], U32)
        rmax = sb_("rmax", [128, 1], F32)
        nbias = sb_("nbias", [128, 1], F32)
        rsum = sb_("rsum", [128, 1], F32)
        rrec = sb_("rrec", [128, 1], F32)

        emit_identity(P, sb["ident"], idraw)
        P.emit("gpsimd", lambda g: g.memset(onescol[:, :], 1.0))
        P.emit("gpsimd", lambda g: g.memset(ones32[:, :], 1.0))
        P.emit("gpsimd", lambda g: g.memset(onesr[:, :], 1.0))
        P.emit("gpsimd", lambda g: g.iota(ei8[:, :], [[1, 8]], base=0,
                                          channel_multiplier=0,
                                          allow_small_or_imprecise_dtypes=True))
        P.emit("gpsimd", lambda g: g.iota(eim8[:, :], [[1, 8]], base=-4096,
                                          channel_multiplier=0,
                                          allow_small_or_imprecise_dtypes=True))
        P.emit("gpsimd", lambda g: g.iota(iotac[:, :], [[128, 16]], base=0,
                                          channel_multiplier=1,
                                          allow_small_or_imprecise_dtypes=True))
        def load_lnp(which):
            for nm, dst in ((f"l2f_g{which}", g1b), (f"l2f_b{which}", b1b)):
                P.emit("gpsimd", lambda g, nm=nm, dst=dst: g.dma_start(
                    dst[:, :], bass.AP(W[nm].tensor, 0, [[0, 128], [1, D]])),
                    dma="dma_w")
            return ("dma_w", P.count("dma_w"))
        load_lnp(1)
        P.emit("gpsimd", lambda g: g.dma_start(
            qkvb[:, :, :, :],
            W["l2f_qkb"].rearrange("p (w m c) -> p w m c", w=2, m=2)),
            dma="dma_w")

        P.emit("gpsimd", lambda g: g.dma_start(
            rwm[:, :, :], W["l2f_rwm"].rearrange("(c p) e -> p c e", p=128)),
            dma="dma_w")
        P.emit("gpsimd", lambda g: g.dma_start(
            rwf[:, :, :], W["l2f_rwf"].rearrange("(c p) e -> p c e", p=128)),
            dma="dma_w")
        P.emit("gpsimd", lambda g: g.dma_start(rbm[:, :], W["l2f_rbm"][:, :]),
               dma="dma_w")
        P.emit("gpsimd", lambda g: g.dma_start(rbf[:, :], W["l2f_rbf"][:, :]),
               dma="dma_w")
        wt = ("dma_w", P.count("dma_w"))
        P.emit("vector", lambda v: v.tensor_scalar(
            qbsc[:, :, :], qkvb[:, 0, :, :], 0.125, None, ALU.mult), wait=[wt])
        P.wait_only("gpsimd", [wt])
        P.barrier()

        def meanfm(scale, war=(None, None)):
            e1 = P.emit("scalar", lambda s: s.activation(
                mrow[0:1, 0:512], pmb[0][0:1, :], AF.Identity, scale=scale))
            e2 = P.emit("scalar", lambda s: s.activation(
                mrow[0:1, 512:1024], pmb[1][0:1, :], AF.Identity, scale=scale))
            cpt = {0: war[0], 1: war[1]}
            for c in range(DC):
                pslot = c % 2
                ptr = ps[f"tr{pslot}"]
                ttr = P.emit("tensor", lambda t, c=c, ptr=ptr: t.transpose(
                    ptr[:, 0:1], mrow[0:1, c * 128:(c + 1) * 128],
                    sb["ident"][0:1, 0:1]), pe_tok=True,
                    wait=[e1, e2, cpt[pslot]])
                cpt[pslot] = P.emit("vector", lambda v, c=c, ptr=ptr: v.tensor_copy(
                    mnfm[:, c:c + 1], ptr[:, 0:1]), wait=[ttr])
            return cpt

        def bcast_scalar(src11, dstcol, waits):
            tk = P.emit("tensor", lambda t: t.matmul(
                pvv[0][:, 0:1], ones32[:, :], src11[0:1, :], start=True,
                stop=True), pe_tok=True,
                wait=list(waits) + [("dve", P.count("dve"))])
            return P.emit("vector", lambda v: v.tensor_copy(dstcol, pvv[0][:, 0:1]),
                          wait=[tk])

        for b in range(B_CORE):
            if b > 0:
                lnp_tok = load_lnp(1)
                P.wait_only("gpsimd", [lnp_tok])
                P.barrier()
            trtok = {0: None, 1: None}
            cptok = {0: None, 1: None}
            prev_inner = {0: None, 1: None}
            sxtok = {0: None, 1: None}
            last_ttr = None
            last_mm = None
            prev_n1out = None
            for i in range(16):
                slot = i % 2
                xt = sb[f"xt{slot}"]
                row0 = b * T + i * 128
                ld = P.emit("sync", lambda s, xt=xt, row0=row0: s.dma_start(
                    xt[:, :], xin[row0:row0 + 128, :]), dma=f"dma_x{slot}",
                    wait=[prev_inner[slot]])
                t1 = emit_ln_tile(P, xin, row0, xt, ld, sb, xnt[slot][:, :],
                                  eps=1e-5, out_wait=[sxtok[slot],
                                                      prev_inner[slot]])
                sxtok[slot] = P.emit("gpsimd", lambda g, row0=row0, slot=slot:
                                     g.dma_start(xnd[row0:row0 + 128, :],
                                                 xnt[slot][:, :]),
                                     dma=f"dma_sx{slot}", wait=[t1])
                txn = emit_ln_tile(P, xin, row0, xnt[slot], t1, sb,
                                   sb["xntm"][:, :], eps=1e-6, gain=g1b[:, :],
                                   bias_v=b1b[:, :],
                                   out_wait=[last_ttr, last_mm])
                prev_inner[slot] = txn
                for hh in range(2):
                    pmx = pmb[hh]
                    last_mm = P.emit("tensor", lambda t, pmx=pmx, hh=hh, i=i:
                                     t.matmul(
                        pmx[0:1, :], onescol[:, :],
                        sb["xntm"][:, hh * 512:(hh + 1) * 512],
                        start=(i == 0), stop=(i == 15)),
                        pe_tok=(i == 15 and hh == 1), wait=[txn])
                for c in range(DC):
                    pslot = c % 2
                    ptr = ps[f"tr{pslot}"]
                    ttr = P.emit("tensor", lambda t, c=c, ptr=ptr: t.transpose(
                        ptr[:, :], sb["xntm"][:, c * 128:(c + 1) * 128],
                        sb["ident"][:, :]), pe_tok=True, wait=[txn, cptok[pslot]])
                    col0 = i * 128
                    eng = "scalar" if pslot == 0 else "vector"
                    cptok[pslot] = P.emit(
                        eng, lambda e, c=c, ptr=ptr, col0=col0, eng=eng: (
                            e.activation(nf[:, c, col0:col0 + 128], ptr[:, :],
                                         AF.Identity) if eng == "scalar" else
                            e.tensor_copy(nf[:, c, col0:col0 + 128], ptr[:, :])),
                        wait=[ttr])
                    last_ttr = ttr
            cpt = meanfm(1.0 / T, war=(cptok[0], cptok[1]))
            for dc in range(DC):
                tk = P.emit("tensor", lambda t, dc=dc: t.matmul(
                    pmm[0][0:1, 0:2],
                    mnfm[:, dc:dc + 1], rwm[:, dc, :],
                    start=(dc == 0), stop=(dc == DC - 1)),
                    pe_tok=(dc == DC - 1),
                    wait=[cpt[0], cpt[1]] if dc == 0 else [])
            P.emit("vector", lambda v: v.tensor_add(
                lrow[0:1, 0:2], pmm[0][0:1, 0:2], rbm[0:1, :]), wait=[tk])
            P.emit("vector", lambda v: v.tensor_sub(
                sc[0][0:1, :], lrow[0:1, 0:1], lrow[0:1, 1:2]))
            tp0 = P.emit("scalar", lambda s: s.activation(
                sc[1][0:1, :], sc[0][0:1, :], AF.Sigmoid),
                wait=[("dve", P.count("dve"))])
            tp1 = P.emit("scalar", lambda s: s.activation(
                sc[2][0:1, :], sc[0][0:1, :], AF.Sigmoid, scale=-1.0))
            bcast_scalar(sc[1], pb[:, 0:1], [tp0])
            bcast_scalar(sc[2], pb[:, 1:2], [tp1])
            P.wait_only("gpsimd", [sxtok[0], sxtok[1]])
            P.barrier()

            with contextlib.ExitStack() as astk:
                sa_ = lambda n, s, d: astk.enter_context(
                    nc.sbuf_tensor(f"{name}A{b}_{n}", s, d))
                qf = r3(sa_("qf", [128, DC * T], BF16), DC)
                kf = r3(sa_("kf", [128, DC * T], BF16), DC)
                vt = r3(sa_("vt", [128, 16 * 1040], BF16), 16)
                wbuf = r3(sa_("wbuf", [128, DC * D], BF16), DC)
                est = r3(sa_("est", [128, 16 * 256], BF16), 16)
                ofm = r3(sa_("ofm", [128, DC * 256], BF16), DC)
                rdb = sa_("rdb", [64, 256], F32)
                drow = sa_("drow", [1, 256], BF16)
                mtile = xr
                for m in range(2):
                    P.emit("gpsimd", lambda g, m=m: g.dma_start(
                        vbrow[:, :], W["l2f_vb"][:, m * D:(m + 1) * D]),
                        dma="dma_vb", wait=[("pe", P.count("pe"))])
                    P.emit("gpsimd", lambda g, m=m: g.dma_start(
                        borow[:, :], W["l2f_bo"][:, m * D:(m + 1) * D]),
                        dma="dma_vb")
                    vbt = ("dma_vb", P.count("dma_vb"))
                    evtok = {0: None, 1: None}
                    wo_tok = None
                    for wname, kind, wi in (("l2f_wq", "q", 0), ("l2f_wk", "k", 1),
                                            ("l2f_wv", "v", 2),
                                            ("l2f_wo", "o", 3)):
                        wtok = load_weight_chunks(
                            P, wbuf, W[wname][m * D:(m + 1) * D, :],
                            f"dma_wm")
                        P.wait_only("gpsimd", [wtok])
                        if kind == "o":
                            wo_tok = wtok
                            break
                        if kind == "v":
                            tvo = P.emit("gpsimd", lambda g: g.memset(
                                bass.AP(vt.tensor, vt.offset + 64,
                                        [[vt.ap[0][0], 128], [1040, 16],
                                         [65, 16]]), 1.0),
                                wait=[("pe", P.count("pe"))])
                            for tc in range(16):
                                for half in range(2):
                                    slot = (tc * 2 + half) % 2
                                    pm = pmm[slot]
                                    for dc in range(DC):
                                        P.emit("tensor", lambda t, pm=pm, tc=tc,
                                               half=half, dc=dc: t.matmul(
                                            pm[:, :],
                                            nf[:, dc, tc * 128:(tc + 1) * 128],
                                            wbuf[:, dc,
                                                 half * 512:(half + 1) * 512],
                                            start=(dc == 0), stop=False),
                                            wait=[evtok[slot], wtok]
                                            if dc == 0 else [])
                                    tk = P.emit("tensor", lambda t, pm=pm,
                                                half=half: t.matmul(
                                        pm[:, :], onesr[0:1, :],
                                        vbrow[0:1, half * 512:(half + 1) * 512],
                                        start=False, stop=True), pe_tok=True,
                                        wait=[vbt])
                                    dst = bass.AP(
                                        vt.tensor,
                                        vt.offset + tc * 1040 + half * 8 * 65,
                                        [[vt.ap[0][0], 128], [65, 8], [1, 64]])
                                    eng = "scalar" if slot == 0 else "vector"
                                    evtok[slot] = P.emit(
                                        eng, lambda e, pm=pm, dst=dst, eng=eng: (
                                            e.activation(dst, pm[:, :],
                                                         AF.Identity)
                                            if eng == "scalar" else
                                            e.tensor_copy(dst, pm[:, :])),
                                        wait=[tk, tvo])
                        else:
                            dst = qf if kind == "q" else kf
                            for oc in range(DC):
                                for tg4 in range(4):
                                    slot = (oc * 4 + tg4) % 2
                                    pm = pmm[slot]
                                    for dc in range(DC):
                                        tk = P.emit("tensor", lambda t, pm=pm,
                                                    oc=oc, tg4=tg4, dc=dc:
                                                    t.matmul(
                                            pm[:, :],
                                            wbuf[:, dc, oc * 128:(oc + 1) * 128],
                                            nf[:, dc, tg4 * 512:(tg4 + 1) * 512],
                                            start=(dc == 0), stop=(dc == DC - 1)),
                                            pe_tok=(dc == DC - 1),
                                            wait=[evtok[slot], wtok]
                                            if dc == 0 else [])
                                    if kind == "q":
                                        evtok[slot] = P.emit(
                                            "scalar", lambda s, pm=pm, dst=dst,
                                            oc=oc, tg4=tg4, m=m: s.activation(
                                                dst[:, oc,
                                                    tg4 * 512:(tg4 + 1) * 512],
                                                pm[:, :], AF.Identity,
                                                scale=0.125,
                                                bias=qbsc[:, m, oc:oc + 1]),
                                            wait=[tk])
                                    else:
                                        evtok[slot] = P.emit(
                                            "vector", lambda v, pm=pm, dst=dst,
                                            oc=oc, tg4=tg4, m=m: v.tensor_scalar(
                                                dst[:, oc,
                                                    tg4 * 512:(tg4 + 1) * 512],
                                                pm[:, :],
                                                qkvb[:, 1, m, oc:oc + 1], None,
                                                ALU.add), wait=[tk])
                        P.barrier()
                    tbo = P.emit("vector", lambda v, m=m: v.tensor_scalar(
                        bosc[0:1, :], borow[0:1, :],
                        sc[1 + m][0:1, 0:1], None, ALU.mult),
                        wait=[("dma_vb", P.count("dma_vb"))])
                    sttok = {0: None, 1: None}
                    ldm = {0: None, 1: None}
                    oftok = {0: None, 1: None}
                    pxtok = {0: None, 1: None}
                    prev_pv = None
                    prev_op = {0: None, 1: None}
                    pse = [pss4.ap()[:, 0:256], pss4.ap()[:, 512:768]]
                    pvo = [pss4.ap()[:, 1024:1280], pss4.ap()[:, 1536:1792]]
                    bankread = {0: None, 1: None}
                    ovtok = {0: None, 1: None}
                    bctok = None
                    for qg in range(8):
                        for h in range(16):
                            rb_ = (64 * h) % 128
                            hc_ = h // 2
                            par = h % 2
                            etok = [None] * 16
                            for kt in range(16):
                                pb_ = kt % 2
                                tk = P.emit(
                                    "tensor", lambda t, rb_=rb_, hc_=hc_,
                                    qg=qg, kt=kt, pb_=pb_: t.matmul(
                                        pse[pb_][:, :],
                                        kf[rb_:rb_ + 64, hc_,
                                           kt * 128:(kt + 1) * 128],
                                        qf[rb_:rb_ + 64, hc_,
                                           qg * 256:(qg + 1) * 256],
                                        start=True, stop=True), pe_tok=True,
                                    wait=[bankread[pb_]])
                                etok[kt] = P.emit(
                                    "scalar", lambda s, kt=kt, pb_=pb_:
                                    s.activation(est[:, kt, :],
                                                 pse[pb_][:, :], AF.Exp),
                                    wait=[tk])
                                bankread[pb_] = etok[kt]
                            for kt in range(16):
                                tk = P.emit("tensor", lambda t, kt=kt, h=h,
                                            par=par: t.matmul(
                                    pvo[par][0:65, :],
                                    vt[:, kt, h * 65:h * 65 + 65],
                                    est[:, kt, :], start=(kt == 0),
                                    stop=(kt == 15)), pe_tok=(kt == 15),
                                    wait=([ovtok[par]] if kt == 0 else [])
                                    + [etok[kt]])
                            prev_pv = tk
                            def _recip(v, par=par):
                                with nc.allow_low_precision(
                                        reason="bf16 store of softmax 1/den"):
                                    return v.reciprocal(drow[0:1, :],
                                                        pvo[par][64:65, :])
                            trc = P.emit("vector", _recip, wait=[tk, bctok])
                            bctok = P.emit("tensor", lambda t, par=par:
                                           t.matmul(
                                pse[par][:, :], onesr[0:1, :], drow[0:1, :],
                                start=True, stop=True), pe_tok=True,
                                wait=[trc, bankread[par]])
                            rbt = P.emit("vector", lambda v, par=par:
                                         v.tensor_copy(rdb[:, :],
                                                       pse[par][0:64, :]),
                                wait=[bctok])
                            bankread[par] = rbt
                            def _norm(v, par=par, rb_=rb_, hc_=hc_):
                                with nc.allow_low_precision(
                                        reason="bf16 store of normalized O"):
                                    return v.tensor_mul(
                                        ofm[rb_:rb_ + 64, hc_, :],
                                        pvo[par][0:64, :], rdb[:, :])
                            ntk = P.emit("vector", _norm, wait=[rbt])
                            ovtok[par] = ntk
                            oftok[par] = ntk
                        for jj in range(2):
                            qi = qg * 2 + jj
                            qslot = qi % 2
                            row0 = b * T + qi * 128
                            if m == 0:
                                ldm[qslot] = P.emit(
                                    "sync", lambda s, x_=mtile[qslot], row0=row0:
                                    s.dma_start(x_[:, :], xnd[row0:row0 + 128, :]),
                                    dma=f"dma_mt{qslot}", wait=[sttok[qslot]])
                            at = {}
                            for half in range(2):
                                pm = pmm[half]
                                wl = [oftok[0], oftok[1], prev_op[half],
                                      pxtok[half], tbo, wo_tok, prev_pv]
                                for dc in range(DC):
                                    P.emit("tensor", lambda t, pm=pm, jj=jj,
                                           half=half, dc=dc: t.matmul(
                                        pm[:, :],
                                        ofm[:, dc, jj * 128:(jj + 1) * 128],
                                        wbuf[:, dc, half * 512:(half + 1) * 512],
                                        start=(dc == 0), stop=False),
                                        wait=wl if dc == 0 else [])
                                tk = P.emit("tensor", lambda t, pm=pm, half=half:
                                            t.matmul(
                                    pm[:, :], onesr[0:1, :],
                                    bosc[0:1, half * 512:(half + 1) * 512],
                                    start=False, stop=True), pe_tok=True)
                                if m == 0:
                                    at[half] = P.emit(
                                        "vector", lambda v, pm=pm, half=half,
                                        x_=mtile[qslot]: v.scalar_tensor_tensor(
                                            x_[:, half * 512:(half + 1) * 512],
                                            pm[:, :], pb[:, 0:1],
                                            x_[:, half * 512:(half + 1) * 512],
                                            ALU.mult, ALU.add),
                                        wait=[tk, ldm[qslot]])
                                else:
                                    at[half] = P.emit(
                                        "vector", lambda v, pm=pm, half=half,
                                        x_=mtile[qslot]: v.tensor_scalar(
                                            x_[:, half * 512:(half + 1) * 512],
                                            pm[:, :], pb[:, 1:2], None,
                                            ALU.mult), wait=[tk, sttok[qslot]])
                                prev_op[half] = at[half]
                                pxtok[half] = at[half]
                            oftok[0] = at[0]
                            oftok[1] = at[1]
                            acc = None if m == 0 else ALU.add
                            sttok[qslot] = P.emit(
                                "gpsimd", lambda g, o=mtile[qslot], row0=row0,
                                acc=acc: g.dma_start(
                                    midd[row0:row0 + 128, :], o[:, :],
                                    accum_op=acc) if acc else g.dma_start(
                                    midd[row0:row0 + 128, :], o[:, :]),
                                dma=f"dma_mo{qslot}", wait=[at[0], at[1]])
                    P.wait_only("gpsimd", [sttok[0], sttok[1]])
                    P.barrier()

            with contextlib.ExitStack() as cstk:
                sc_ = lambda n, s, d: cstk.enter_context(
                    nc.sbuf_tensor(f"{name}C{b}_{n}", s, d))
                xb = [sc_("xb0", [128, D], F32), sc_("xb1", [128, D], F32)]
                lnp_tok = load_lnp(2)
                P.wait_only("gpsimd", [lnp_tok])
                P.barrier()
                trtok = {0: None, 1: None}
                cptok = {0: None, 1: None}
                prev_inner = {0: None, 1: None}
                sttok = {0: None, 1: None}
                last_ttr = None
                last_mm = None
                for i in range(16):
                    slot = i % 2
                    xt = sb[f"xt{slot}"]
                    row0 = b * T + i * 128
                    ld = P.emit("sync", lambda s, xt=xt, row0=row0: s.dma_start(
                        xt[:, :], midd[row0:row0 + 128, :]), dma=f"dma_x{slot}",
                        wait=[prev_inner[slot]])
                    ldx = P.emit("sync", lambda s, x_=xb[slot], row0=row0:
                                 s.dma_start(x_[:, :], xin[row0:row0 + 128, :]),
                                 dma=f"dma_xb{slot}", wait=[sttok[slot]])
                    txn = emit_ln_tile(P, xin, row0, xt, ld, sb,
                                       sb["xntm"][:, :], eps=1e-6,
                                       gain=g2b[:, :], bias_v=b2b[:, :],
                                       out_wait=[last_ttr, last_mm])
                    ta = P.emit("vector", lambda v, xt=xt, x_=xb[slot]:
                                v.tensor_add(x_[:, :], xt[:, :], x_[:, :]),
                                wait=[ld, ldx])
                    prev_inner[slot] = ta
                    sttok[slot] = P.emit("gpsimd", lambda g, o=xb[slot],
                                         row0=row0: g.dma_start(
                        xout[row0:row0 + 128, :], o[:, :]),
                        dma=f"dma_o{slot}", wait=[ta])
                    for hh in range(2):
                        pmx = pmb[hh]
                        last_mm = P.emit("tensor", lambda t, pmx=pmx, hh=hh, i=i:
                                         t.matmul(
                            pmx[0:1, :], onescol[:, :],
                            sb["xntm"][:, hh * 512:(hh + 1) * 512],
                            start=(i == 0), stop=(i == 15)),
                            pe_tok=(i == 15 and hh == 1), wait=[txn])
                    for c in range(DC):
                        pslot = c % 2
                        ptr = ps[f"tr{pslot}"]
                        ttr = P.emit("tensor", lambda t, c=c, ptr=ptr: t.transpose(
                            ptr[:, :], sb["xntm"][:, c * 128:(c + 1) * 128],
                            sb["ident"][:, :]), pe_tok=True,
                            wait=[txn, cptok[pslot]])
                        col0 = i * 128
                        eng = "scalar" if pslot == 0 else "vector"
                        cptok[pslot] = P.emit(
                            eng, lambda e, c=c, ptr=ptr, col0=col0, eng=eng: (
                                e.activation(nf[:, c, col0:col0 + 128],
                                             ptr[:, :], AF.Identity)
                                if eng == "scalar" else
                                e.tensor_copy(nf[:, c, col0:col0 + 128],
                                              ptr[:, :])), wait=[ttr])
                        last_ttr = ttr
                P.wait_only("gpsimd", [sttok[0], sttok[1]])
                cpt = meanfm(1.0 / T, war=(cptok[0], cptok[1]))
                for dc in range(DC):
                    tk = P.emit("tensor", lambda t, dc=dc: t.matmul(
                        pmm[0][0:1, 0:8], mnfm[:, dc:dc + 1], rwf[:, dc, :],
                        start=(dc == 0), stop=(dc == DC - 1)),
                        pe_tok=(dc == DC - 1),
                        wait=[cpt[0], cpt[1]] if dc == 0 else [])
                P.emit("vector", lambda v: v.tensor_add(
                    lrow[0:1, :], pmm[0][0:1, 0:8], rbf[0:1, :]), wait=[tk])
                P.emit("vector", lambda v: v.reduce_max(sc[0][0:1, :],
                                                        lrow[0:1, :], axis=AX.X))
                P.emit("vector", lambda v: v.tensor_scalar(
                    sc[1][0:1, :], sc[0][0:1, :], -1.0, None, ALU.mult))
                tex = P.emit("scalar", lambda s: s.activation(
                    prow[0:1, :], lrow[0:1, :], AF.Exp, bias=sc[1][0:1, :],
                    accum_out=sc[2][0:1, :]), wait=[("dve", P.count("dve"))])
                P.emit("vector", lambda v: v.reciprocal(sc[3][0:1, :],
                                                        sc[2][0:1, :]),
                       wait=[tex])
                P.emit("vector", lambda v: v.tensor_scalar(
                    prow[0:1, :], prow[0:1, :], sc[3][0:1, :], None, ALU.mult))
                for k in range(2):
                    P.emit("vector", lambda v, k=k: v.reduce_max(
                        sc[4 + k][0:1, :], prow[0:1, :], axis=AX.X))
                    P.emit("vector", lambda v, k=k: v.tensor_scalar(
                        jrow[0:1, :], prow[0:1, :], sc[4 + k][0:1, :], None,
                        ALU.is_equal))
                    P.emit("vector", lambda v: v.tensor_mul(
                        jrow[0:1, :], jrow[0:1, :], eim8[0:1, :]))
                    P.emit("vector", lambda v, k=k: v.tensor_scalar(
                        jrow[0:1, :], jrow[0:1, :], 4096.0, None, ALU.add,
                        ALU.min, accum_out=sc[6 + k][0:1, :]))
                    if k == 0:
                        P.emit("vector", lambda v: v.tensor_scalar(
                            jrow[0:1, :], ei8[0:1, :], sc[6][0:1, :], None,
                            ALU.is_equal))
                        P.emit("vector", lambda v: v.scalar_tensor_tensor(
                            prow[0:1, :], jrow[0:1, :], NEGBIG, prow[0:1, :],
                            ALU.mult, ALU.add))
                P.emit("vector", lambda v: v.tensor_add(
                    sc[0][0:1, :], sc[4][0:1, :], sc[5][0:1, :]))
                P.emit("vector", lambda v: v.reciprocal(sc[1][0:1, :],
                                                        sc[0][0:1, :]))
                P.emit("vector", lambda v: v.tensor_mul(
                    sc[2][0:1, :], sc[4][0:1, :], sc[1][0:1, :]))
                tt3 = P.emit("vector", lambda v: v.tensor_mul(
                    sc[3][0:1, :], sc[5][0:1, :], sc[1][0:1, :]))
                bcast_scalar(sc[2], pb[:, 2:3], [tt3])
                bcast_scalar(sc[3], pb[:, 3:4], [tt3])
                P.barrier()

            with contextlib.ExitStack() as dstk:
                sd_ = lambda n, s, d: dstk.enter_context(
                    nc.sbuf_tensor(f"{name}D{b}_{n}", s, d))
                fw1s = r3(sd_("fw1s", [128, DC * 2048], BF16), DC)
                fw2s = r3(sd_("fw2s", [128, 16 * D], BF16), 16)
                hfm = r3(sd_("hfm", [128, 16 * 512], BF16), 16)
                yt = [sd_("yt0", [128, D], F32), sd_("yt1", [128, D], F32)]
                prevacc = {0: None, 1: None}
                pvbank = {0: None, 1: None}
                for k in range(2):
                    bt = bcast_scalar(sc[6 + k], pb[:, 0:1], [])
                    t1 = P.emit("vector", lambda v: v.tensor_scalar(
                        idxf[:, :], iotac[:, :], pb[:, 0:1], None, ALU.bypass)
                        if False else v.tensor_scalar(
                        pb[:, 0:1], pb[:, 0:1], 1024.0, None, ALU.mult),
                        wait=[bt])
                    t2 = P.emit("vector", lambda v: v.tensor_scalar(
                        idxf[:, :], iotac[:, :], pb[:, 0:1], None, ALU.add))
                    t3 = P.emit("vector", lambda v: v.tensor_copy(idxu[:, :],
                                                                  idxf[:, :]))
                    gts = []
                    for c in range(DC):
                        gts.append(P.emit("gpsimd", lambda g, c=c: g.indirect_dma_start(
                            fw1s[:, c, :], None, W["l2f_fw1"],
                            bass.IndirectOffsetOnAxis(ap=idxu[:, c:c + 1],
                                                      axis=0)),
                            dma="dma_g", wait=[t3]))
                    t4 = P.emit("vector", lambda v: v.tensor_scalar(
                        idxf[:, :], idxf[:, :], pb[:, 0:1], None, ALU.add),
                        wait=[("dma_g", P.count("dma_g"))])
                    t5 = P.emit("vector", lambda v: v.tensor_copy(idxu[:, :],
                                                                  idxf[:, :]))
                    for c in range(16):
                        gts.append(P.emit("gpsimd", lambda g, c=c: g.indirect_dma_start(
                            fw2s[:, c, :], None, W["l2f_fw2"],
                            bass.IndirectOffsetOnAxis(ap=idxu[:, c:c + 1],
                                                      axis=0)),
                            dma="dma_g", wait=[t5]))
                    gtok = ("dma_g", P.count("dma_g"))
                    hev = {0: None, 1: None}
                    for tg4 in range(4):
                        for hc in range(16):
                            slot = (tg4 * 16 + hc) % 2
                            pm = pmm[slot]
                            for dc in range(DC):
                                tk = P.emit("tensor", lambda t, pm=pm, hc=hc,
                                            tg4=tg4, dc=dc: t.matmul(
                                    pm[:, :], fw1s[:, dc, hc * 128:(hc + 1) * 128],
                                    nf[:, dc, tg4 * 512:(tg4 + 1) * 512],
                                    start=(dc == 0), stop=(dc == DC - 1)),
                                    pe_tok=(dc == DC - 1),
                                    wait=[hev[slot], gtok] if dc == 0 else [])
                            hev[slot] = P.emit("scalar", lambda s, pm=pm, hc=hc:
                                               s.activation(
                                hfm[:, hc, :], pm[:, :], AF.Gelu), wait=[tk])
                        for j in range(4):
                            tt = tg4 * 4 + j
                            qslot = tt % 2
                            row0 = b * T + tt * 128
                            at = {}
                            for ch in range(2):
                                slot = ch
                                pm = pvv[slot]
                                wl = [hev[0], hev[1], prevacc[qslot],
                                      pvbank[ch]]
                                for fc in range(16):
                                    tk = P.emit("tensor", lambda t, pm=pm, j=j,
                                                ch=ch, fc=fc: t.matmul(
                                        pm[:, :], hfm[:, fc, j * 128:(j + 1) * 128],
                                        fw2s[:, fc, ch * 512:(ch + 1) * 512],
                                        start=(fc == 0), stop=(fc == 15)),
                                        pe_tok=(fc == 15),
                                        wait=wl if fc == 0 else [])
                                at[ch] = P.emit("vector", lambda v, pm=pm, ch=ch,
                                                k=k, y=yt[qslot]: v.tensor_scalar(
                                    y[:, ch * 512:(ch + 1) * 512], pm[:, :],
                                    pb[:, 2 + k:3 + k], None, ALU.mult),
                                    wait=[tk])
                                pvbank[ch] = at[ch]
                            prevacc[qslot] = P.emit(
                                "gpsimd", lambda g, y=yt[qslot], row0=row0:
                                g.dma_start(xout[row0:row0 + 128, :], y[:, :],
                                            accum_op=ALU.add),
                                dma=f"dma_a{qslot}", wait=[at[0], at[1]])
                    P.wait_only("gpsimd", [prevacc[0], prevacc[1]])
                    P.barrier()
            P.barrier()
def emit_l1_ffn(P, xin, xout, W, name="l1f"):
    nc = P.nc
    NE, EH = 16, 512
    with contextlib.ExitStack() as pstk:
        sb_ = lambda n, s, d: pstk.enter_context(nc.sbuf_tensor(f"{name}_{n}", s, d))
        ps_ = lambda n, s, d: pstk.enter_context(nc.psum_tensor(f"{name}_{n}", s, d))
        r3 = lambda t, c: t.ap().rearrange("p (c t) -> p c t", c=c)
        xr = [sb_("xr0", [128, D], F32), sb_("xr1", [128, D], F32)]
        sb = {
            "xt0": xr[0], "xt1": xr[1],
            "stats": sb_("stats", [128, 12], F32), "mv": sb_("mv", [128, 2], F32),
            "rstd": sb_("rstd", [128, 1], F32), "xntm": sb_("xntm", [128, D], BF16),
            "ident": sb_("ident", [128, 128], BF16),
        }
        idraw = sb_("idraw", [128, 128], I32)
        ps = {"tr0": ps_("tr0", [128, 1024], BF16).ap()[:, 0:128],
              "tr1": ps_("tr1", [128, 1024], BF16).ap()[:, 0:128]}
        pmm = [ps_("pmm0", [128, 512], F32), ps_("pmm1", [128, 512], F32)]
        GT = 1024
        NG = NTOK // GT
        NT8 = GT // 128
        xn_g = r3(sb_("xng", [128, DC * GT], BF16), DC)
        hs_g = r3(sb_("hsg", [128, DC * GT], BF16), DC)
        he_g = r3(sb_("heg", [128, 4 * GT], BF16), 4)
        acc = sb_("acc", [128, DC * GT], F32).ap().rearrange(
            "p (tt c) -> p tt c", tt=NT8)
        _wsg = r3(sb_("wsga", [128, DC * 1024], BF16), DC)
        _wsu = r3(sb_("wsua", [128, DC * 1024], BF16), DC)
        _wsd = r3(sb_("wsda", [128, DC * D], BF16), DC)
        wsg_ = [_wsg, _wsg]
        wsu_ = [_wsu, _wsu]
        wsd_ = [_wsd, _wsd]
        weg = [r3(sb_("wega", [128, DC * EH], BF16), DC),
               r3(sb_("wegb", [128, DC * EH], BF16), DC)]
        weu = [r3(sb_("weua", [128, DC * EH], BF16), DC),
               r3(sb_("weub", [128, DC * EH], BF16), DC)]
        wed = [r3(sb_("weda", [128, 4 * D], BF16), 4),
               r3(sb_("wedb", [128, 4 * D], BF16), 4)]
        wg = sb_("wg", [128, DC * NE], BF16).ap().rearrange(
            "p (c e) -> p c e", c=DC)
        wsgc = sb_("wsgc", [128, DC], BF16)
        ei = sb_("ei", [128, NE], F32)
        eim = sb_("eim", [128, NE], F32)
        cwg = sb_("cwg", [128, NT8 * NE], F32).ap().rearrange(
            "p (tt e) -> p tt e", tt=NT8)
        sgc = sb_("sgc", [128, NT8], F32)
        probs = sb_("probs", [128, NE], F32)
        junk = sb_("junk", [128, NE], F32)
        cwraw = sb_("cwraw", [128, NE], F32)
        tmpg = sb_("tmpg", [128, 512], BF16)
        rmax = sb_("rmax", [128, 1], F32)
        nbias = sb_("nbias", [128, 1], F32)
        rsum = sb_("rsum", [128, 1], F32)
        rrec = sb_("rrec", [128, 1], F32)
        idxk = sb_("idxk", [128, 1], F32)
        mk = sb_("mk", [128, 1], F32)

        emit_identity(P, sb["ident"], idraw)
        P.emit("gpsimd", lambda g: g.iota(ei[:, :], [[1, NE]], base=0,
                                          channel_multiplier=0,
                                          allow_small_or_imprecise_dtypes=True))
        P.emit("gpsimd", lambda g: g.iota(eim[:, :], [[1, NE]], base=-4096,
                                          channel_multiplier=0,
                                          allow_small_or_imprecise_dtypes=True))
        P.emit("gpsimd", lambda g: g.dma_start(
            wg[:, :, :], W["l1f_WgT"].rearrange("(c p) e -> p c e", p=128)),
            dma="dma_w")
        P.emit("gpsimd", lambda g: g.dma_start(
            wsgc[:, :], W["l1f_Wsgc"][:, :]), dma="dma_w")
        P.wait_only("gpsimd", [("dma_w", P.count("dma_w"))])
        P.barrier()

        shld = {0: None, 1: None}
        shuse = {0: None, 1: None}
        wld = {0: None, 1: None}
        wuse = {0: None, 1: None}
        for tg in range(NG):
            tok0 = tg * GT
            emit_ln_to_fm(P, xin, xn_g, 0, sb, ps, tiles=NT8, tok0=tok0)
            P.barrier()

            evtok = {0: None, 1: None}
            for tt in range(NT8):
                slot = tt % 2
                pm = pmm[slot]
                for dc in range(DC):
                    tk = P.emit("tensor", lambda t, pm=pm, tt=tt, dc=dc: t.matmul(
                        pm[0:128, 0:NE], xn_g[:, dc, tt * 128:(tt + 1) * 128],
                        wg[:, dc, :], start=(dc == 0), stop=(dc == DC - 1)),
                        pe_tok=(dc == DC - 1), wait=[evtok[slot]] if dc == 0 else [])
                for dc in range(DC):
                    tk2 = P.emit("tensor", lambda t, pm=pm, tt=tt, dc=dc: t.matmul(
                        pm[0:128, 16:17], xn_g[:, dc, tt * 128:(tt + 1) * 128],
                        wsgc[:, dc:dc + 1], start=(dc == 0), stop=(dc == DC - 1)),
                        pe_tok=(dc == DC - 1))
                tsg = P.emit("scalar", lambda s, pm=pm, tt=tt: s.activation(
                    sgc[:, tt:tt + 1], pm[0:128, 16:17], AF.Sigmoid),
                    wait=[tk2])
                P.emit("vector", lambda v, pm=pm: v.reduce_max(
                    rmax[:, :], pm[0:128, 0:NE], axis=AX.X), wait=[tk])
                P.emit("vector", lambda v: v.tensor_scalar(
                    nbias[:, :], rmax[:, :], -1.0, None, ALU.mult))
                tex = P.emit("scalar", lambda s, pm=pm: s.activation(
                    probs[:, :], pm[0:128, 0:NE], AF.Exp, bias=nbias[:, :],
                    accum_out=rsum[:, :]), wait=[("dve", P.count("dve"))])
                evtok[slot] = tex
                P.emit("vector", lambda v: v.reciprocal(rrec[:, :], rsum[:, :]),
                       wait=[tex])
                P.emit("vector", lambda v: v.tensor_scalar(
                    probs[:, :], probs[:, :], rrec[:, :], None, ALU.mult))
                for k in range(2):
                    P.emit("vector", lambda v: v.reduce_max(mk[:, :], probs[:, :],
                                                            axis=AX.X))
                    P.emit("vector", lambda v: v.tensor_scalar(
                        junk[:, :], probs[:, :], mk[:, :], None, ALU.is_equal))
                    P.emit("vector", lambda v: v.tensor_mul(junk[:, :], junk[:, :],
                                                            eim[:, :]))
                    P.emit("vector", lambda v: v.tensor_scalar(
                        junk[:, :], junk[:, :], 4096.0, None, ALU.add, ALU.min,
                        accum_out=idxk[:, :]))
                    P.emit("vector", lambda v: v.tensor_scalar(
                        junk[:, :], ei[:, :], idxk[:, :], None, ALU.is_equal))
                    if k == 0:
                        P.emit("vector", lambda v: v.tensor_scalar(
                            cwraw[:, :], junk[:, :], mk[:, :], None, ALU.mult))
                        P.emit("vector", lambda v: v.scalar_tensor_tensor(
                            probs[:, :], junk[:, :], NEGBIG, probs[:, :],
                            ALU.mult, ALU.add))
                    else:
                        P.emit("vector", lambda v: v.scalar_tensor_tensor(
                            cwraw[:, :], junk[:, :], mk[:, :], cwraw[:, :],
                            ALU.mult, ALU.add))
                P.emit("vector", lambda v: v.tensor_scalar(
                    junk[:, 0:1], cwraw[:, 0:1], 0.0, None, ALU.mult, ALU.add,
                    accum_out=rsum[:, :]) if False else v.reduce_sum(
                    rsum[:, :], cwraw[:, :], axis=AX.X))
                P.emit("vector", lambda v: v.reciprocal(rrec[:, :], rsum[:, :]))
                P.emit("vector", lambda v, tt=tt: v.tensor_scalar(
                    cwg[:, tt, :], cwraw[:, :], rrec[:, :], None, ALU.mult))

            hevtok = {0: evtok[0], 1: evtok[1]}
            acctok = {0: None, 1: None}
            first_half = True
            for sh in range(2):
                ss = sh % 2
                h0 = sh * 1024
                l1 = P.emit("gpsimd", lambda g, ss=ss, h0=h0: g.dma_start(
                    wsg_[ss][:, :, :], W["l1f_WsgateT"][:, h0:h0 + 1024]
                    .rearrange("(c p) h -> p c h", p=128)), dma="dma_s",
                    wait=[shuse[0], shuse[1], hevtok[0], hevtok[1]])
                l2 = P.emit("gpsimd", lambda g, ss=ss, h0=h0: g.dma_start(
                    wsu_[ss][:, :, :], W["l1f_WsupT"][:, h0:h0 + 1024]
                    .rearrange("(c p) h -> p c h", p=128)), dma="dma_s")
                l3 = P.emit("gpsimd", lambda g, ss=ss, h0=h0: g.dma_start(
                    wsd_[ss][:, :, :], W["l1f_WsdownT"][h0:h0 + 1024, :]
                    .rearrange("(c p) h -> p c h", p=128)), dma="dma_s")
                sldt = ("dma_s", P.count("dma_s"))
                for gh in range(GT // 512):
                    for hc in range(DC):
                        pmg, pmu = pmm[0], pmm[1]
                        for dc in range(DC):
                            tkg = P.emit("tensor", lambda t, pmg=pmg, hc=hc, gh=gh,
                                         dc=dc, ss=ss: t.matmul(
                                pmg[:, :], wsg_[ss][:, dc, hc * 128:(hc + 1) * 128],
                                xn_g[:, dc, gh * 512:(gh + 1) * 512],
                                start=(dc == 0), stop=(dc == DC - 1)),
                                pe_tok=(dc == DC - 1),
                                wait=[hevtok[0], sldt] if dc == 0 else [])
                        tg_e = P.emit("scalar", lambda s, pmg=pmg: s.activation(
                            tmpg[:, :], pmg[:, :], AF.Sigmoid), wait=[tkg])
                        tg_e = P.emit("vector", lambda v, pmg=pmg: v.tensor_tensor(
                            tmpg[:, :], tmpg[:, :], pmg[:, :], ALU.mult),
                            wait=[tg_e])
                        for dc in range(DC):
                            tku = P.emit("tensor", lambda t, pmu=pmu, hc=hc, gh=gh,
                                         dc=dc, ss=ss: t.matmul(
                                pmu[:, :], wsu_[ss][:, dc, hc * 128:(hc + 1) * 128],
                                xn_g[:, dc, gh * 512:(gh + 1) * 512],
                                start=(dc == 0), stop=(dc == DC - 1)),
                                pe_tok=(dc == DC - 1),
                                wait=[hevtok[1]] if dc == 0 else [])
                        hevtok[0] = P.emit("vector", lambda v, pmu=pmu, hc=hc,
                                           gh=gh: v.tensor_mul(
                            hs_g[:, hc, gh * 512:(gh + 1) * 512], pmu[:, :],
                            tmpg[:, :]), wait=[tku, tg_e])
                        hevtok[1] = hevtok[0]
                for tt in range(NT8):
                    for ch in range(2):
                        slot = (tt * 2 + ch) % 2
                        pm = pmm[slot]
                        wl = [hevtok[0], hevtok[1], acctok[slot]]
                        for hc in range(DC):
                            tk = P.emit("tensor", lambda t, pm=pm, tt=tt, ch=ch,
                                        hc=hc, ss=ss: t.matmul(
                                pm[:, :], hs_g[:, hc, tt * 128:(tt + 1) * 128],
                                wsd_[ss][:, hc, ch * 512:(ch + 1) * 512],
                                start=(hc == 0), stop=(hc == DC - 1)),
                                pe_tok=(hc == DC - 1), wait=wl if hc == 0 else [])
                        shuse[ss] = tk
                        if first_half:
                            acctok[slot] = P.emit(
                                "vector", lambda v, pm=pm, tt=tt, ch=ch:
                                v.tensor_scalar(
                                    acc[:, tt, ch * 512:(ch + 1) * 512], pm[:, :],
                                    sgc[:, tt:tt + 1], None, ALU.mult),
                                wait=[tk])
                        else:
                            acctok[slot] = P.emit(
                                "vector", lambda v, pm=pm, tt=tt, ch=ch:
                                v.scalar_tensor_tensor(
                                    acc[:, tt, ch * 512:(ch + 1) * 512], pm[:, :],
                                    sgc[:, tt:tt + 1],
                                    acc[:, tt, ch * 512:(ch + 1) * 512],
                                    ALU.mult, ALU.add), wait=[tk])
                first_half = False
                hevtok[0] = acctok[0]
                hevtok[1] = acctok[1]

            for e in range(NE):
                es = e % 2
                w1t = P.emit("gpsimd", lambda g, es=es, e=e: g.dma_start(
                    weg[es][:, :, :], W["l1f_WegT"][e * D:(e + 1) * D, :]
                    .rearrange("(c p) h -> p c h", p=128)), dma=f"dma_e{es}",
                    wait=[wuse[es]])
                P.emit("gpsimd", lambda g, es=es, e=e: g.dma_start(
                    weu[es][:, :, :], W["l1f_WeuT"][e * D:(e + 1) * D, :]
                    .rearrange("(c p) h -> p c h", p=128)), dma=f"dma_e{es}")
                P.emit("gpsimd", lambda g, es=es, e=e: g.dma_start(
                    wed[es][:, :, :], W["l1f_WedT"][e * EH:(e + 1) * EH, :]
                    .rearrange("(c p) h -> p c h", p=128)), dma=f"dma_e{es}")
                wldt = (f"dma_e{es}", P.count(f"dma_e{es}"))
                for gh in range(GT // 512):
                    for hc in range(4):
                        pmg, pmu = pmm[0], pmm[1]
                        for dc in range(DC):
                            tkg = P.emit("tensor", lambda t, pmg=pmg, hc=hc, gh=gh,
                                         dc=dc, es=es: t.matmul(
                                pmg[:, :], weg[es][:, dc, hc * 128:(hc + 1) * 128],
                                xn_g[:, dc, gh * 512:(gh + 1) * 512],
                                start=(dc == 0), stop=(dc == DC - 1)),
                                pe_tok=(dc == DC - 1),
                                wait=[hevtok[0], wldt] if dc == 0 else [])
                        tg_e = P.emit("scalar", lambda s, pmg=pmg: s.activation(
                            tmpg[:, :], pmg[:, :], AF.Sigmoid), wait=[tkg])
                        tg_e = P.emit("vector", lambda v, pmg=pmg: v.tensor_tensor(
                            tmpg[:, :], tmpg[:, :], pmg[:, :], ALU.mult),
                            wait=[tg_e])
                        for dc in range(DC):
                            tku = P.emit("tensor", lambda t, pmu=pmu, hc=hc, gh=gh,
                                         dc=dc, es=es: t.matmul(
                                pmu[:, :], weu[es][:, dc, hc * 128:(hc + 1) * 128],
                                xn_g[:, dc, gh * 512:(gh + 1) * 512],
                                start=(dc == 0), stop=(dc == DC - 1)),
                                pe_tok=(dc == DC - 1),
                                wait=[hevtok[1]] if dc == 0 else [])
                        hevtok[0] = P.emit("vector", lambda v, pmu=pmu, hc=hc,
                                           gh=gh: v.tensor_mul(
                            he_g[:, hc, gh * 512:(gh + 1) * 512], pmu[:, :],
                            tmpg[:, :]), wait=[tku, tg_e])
                        hevtok[1] = hevtok[0]
                for tt in range(NT8):
                    for ch in range(2):
                        slot = (tt * 2 + ch) % 2
                        pm = pmm[slot]
                        wl = [hevtok[0], hevtok[1], acctok[slot]]
                        for hc in range(4):
                            tk = P.emit("tensor", lambda t, pm=pm, tt=tt, ch=ch,
                                        hc=hc, es=es: t.matmul(
                                pm[:, :], he_g[:, hc, tt * 128:(tt + 1) * 128],
                                wed[es][:, hc, ch * 512:(ch + 1) * 512],
                                start=(hc == 0), stop=(hc == 3)),
                                pe_tok=(hc == 3), wait=wl if hc == 0 else [])
                        wuse[es] = tk
                        acctok[slot] = P.emit(
                            "vector", lambda v, pm=pm, tt=tt, ch=ch, e=e:
                            v.scalar_tensor_tensor(
                                acc[:, tt, ch * 512:(ch + 1) * 512], pm[:, :],
                                cwg[:, tt, e:e + 1],
                                acc[:, tt, ch * 512:(ch + 1) * 512],
                                ALU.mult, ALU.add), wait=[tk])
                hevtok[0] = acctok[0]
                hevtok[1] = acctok[1]

            sttok = {0: None, 1: None}
            for tt in range(NT8):
                qslot = tt % 2
                row0 = tok0 + tt * 128
                ldx = P.emit("sync", lambda s, x_=xr[qslot], row0=row0: s.dma_start(
                    x_[:, :], xin[row0:row0 + 128, :]), dma=f"dma_xr{qslot}",
                    wait=[sttok[qslot]])
                at = P.emit("vector", lambda v, x_=xr[qslot], tt=tt: v.tensor_add(
                    x_[:, :], acc[:, tt, :], x_[:, :]),
                    wait=[ldx, acctok[0], acctok[1]])
                sttok[qslot] = P.emit("gpsimd", lambda g, o=xr[qslot], row0=row0:
                                      g.dma_start(xout[row0:row0 + 128, :],
                                                  o[:, :]),
                                      dma=f"dma_o{qslot}", wait=[at])
            P.wait_only("gpsimd", [sttok[0], sttok[1]])
            P.barrier()


def emit_l3_ffn(P, xin, xout, W, name="l3f"):
    nc = P.nc
    NE, HH = 15, 1024
    with contextlib.ExitStack() as pstk:
        sb_ = lambda n, s, d: pstk.enter_context(nc.sbuf_tensor(f"{name}_{n}", s, d))
        ps_ = lambda n, s, d: pstk.enter_context(nc.psum_tensor(f"{name}_{n}", s, d))
        r3 = lambda t, c: t.ap().rearrange("p (c t) -> p c t", c=c)
        xr = [sb_("xr0", [128, D], F32), sb_("xr1", [128, D], F32)]
        sb = {
            "xt0": xr[0], "xt1": xr[1],
            "stats": sb_("stats", [128, 12], F32), "mv": sb_("mv", [128, 2], F32),
            "rstd": sb_("rstd", [128, 1], F32), "xntm": sb_("xntm", [128, D], BF16),
            "ident": sb_("ident", [128, 128], BF16),
        }
        idraw = sb_("idraw", [128, 128], I32)
        ps = {"tr0": ps_("tr0", [128, 1024], BF16).ap()[:, 0:128],
              "tr1": ps_("tr1", [128, 1024], BF16).ap()[:, 0:128]}
        pmm = [ps_("pmm0", [128, 512], F32), ps_("pmm1", [128, 512], F32)]
        GT = 1024
        NG = NTOK // GT
        xn_g = r3(sb_("xng", [128, DC * GT], BF16), DC)
        hs_g = r3(sb_("hsg", [128, DC * GT], BF16), DC)
        he_g = r3(sb_("heg", [128, DC * GT], BF16), DC)
        acc = sb_("acc", [128, DC * GT], F32).ap().rearrange(
            "p (tt c) -> p tt c", tt=GT // 128)
        sw1 = r3(sb_("sw1", [128, DC * HH], BF16), DC)
        sw2 = r3(sb_("sw2", [128, DC * D], BF16), DC)
        we1 = [r3(sb_("we1a", [128, DC * HH], BF16), DC),
               r3(sb_("we1b", [128, DC * HH], BF16), DC)]
        we2 = [r3(sb_("we2a", [128, DC * D], BF16), DC),
               r3(sb_("we2b", [128, DC * D], BF16), DC)]
        rw = sb_("rw", [128, DC * NE], BF16).ap().rearrange(
            "p (c e) -> p c e", c=DC)
        rbb = sb_("rbb", [128, NE], F32)
        eb1s = sb_("eb1s", [128, NE * DC], F32).ap().rearrange(
            "p (e c) -> p e c", e=NE)
        eb2c = [sb_("eb2c0", [1, D], BF16), sb_("eb2c1", [1, D], BF16)]
        sb1s = sb_("sb1s", [128, DC], F32)
        sb2r = sb_("sb2r", [1, D], BF16)
        onesr = sb_("onesr", [1, 512], BF16)
        ei = sb_("ei", [128, NE], F32)
        eim = sb_("eim", [128, NE], F32)
        cwg = sb_("cwg", [128, (GT // 128) * NE], F32).ap().rearrange(
            "p (tt e) -> p tt e", tt=GT // 128)
        logit = sb_("logit", [128, NE], F32)
        probs = sb_("probs", [128, NE], F32)
        junk = sb_("junk", [128, NE], F32)
        rmax = sb_("rmax", [128, 1], F32)
        nbias = sb_("nbias", [128, 1], F32)
        rsum = sb_("rsum", [128, 1], F32)
        rrec = sb_("rrec", [128, 1], F32)
        idxk = sb_("idxk", [128, 1], F32)
        mk = sb_("mk", [128, 1], F32)

        emit_identity(P, sb["ident"], idraw)
        P.emit("gpsimd", lambda g: g.iota(ei[:, :], [[1, NE]], base=0,
                                          channel_multiplier=0,
                                          allow_small_or_imprecise_dtypes=True))
        P.emit("gpsimd", lambda g: g.iota(eim[:, :], [[1, NE]], base=-4096,
                                          channel_multiplier=0,
                                          allow_small_or_imprecise_dtypes=True))
        P.emit("gpsimd", lambda g: g.memset(onesr[:, :], 1.0))
        load_weight_chunks(P, sw1, W["l3f_sW1"], "dma_w")
        load_weight_chunks(P, sw2, W["l3f_sW2"], "dma_w")
        P.emit("gpsimd", lambda g: g.dma_start(
            rw[:, :, :], W["l3f_rW"].rearrange("(c p) e -> p c e", p=128)),
            dma="dma_w")
        P.emit("gpsimd", lambda g: g.dma_start(
            rbb[:, :], bass.AP(W["l3f_rb"].tensor, 0, [[0, 128], [1, NE]])),
            dma="dma_w")
        P.emit("gpsimd", lambda g: g.dma_start(
            eb1s[:, :, :], W["l3f_eb1"].rearrange("p (e c) -> p e c", e=NE)),
            dma="dma_w")
        P.emit("gpsimd", lambda g: g.dma_start(
            sb1s[:, :], W["l3f_sb1"][:, :]), dma="dma_w")
        P.emit("gpsimd", lambda g: g.dma_start(sb2r[:, :], W["l3f_sb2"][None, :]),
            dma="dma_w")
        wtok = ("dma_w", P.count("dma_w"))
        P.wait_only("gpsimd", [wtok])
        P.barrier()

        wld = {0: None, 1: None}
        wuse = {0: None, 1: None}
        for tg in range(NG):
            tok0 = tg * GT
            emit_ln_to_fm(P, xin, xn_g, 0, sb, ps, tiles=GT // 128, tok0=tok0)
            P.barrier()

            evtok = {0: None, 1: None}
            for tt in range(GT // 128):
                slot = tt % 2
                pm = pmm[slot]
                for dc in range(DC):
                    tk = P.emit("tensor", lambda t, pm=pm, tt=tt, dc=dc: t.matmul(
                        pm[0:128, 0:NE], xn_g[:, dc, tt * 128:(tt + 1) * 128],
                        rw[:, dc, :], start=(dc == 0), stop=(dc == DC - 1)),
                        pe_tok=(dc == DC - 1), wait=[evtok[slot]] if dc == 0 else [])
                tl = P.emit("vector", lambda v, pm=pm: v.tensor_add(
                    logit[:, :], pm[0:128, 0:NE], rbb[:, :]), wait=[tk])
                evtok[slot] = tl
                P.emit("vector", lambda v: v.reduce_max(rmax[:, :], logit[:, :],
                                                        axis=AX.X))
                P.emit("vector", lambda v: v.tensor_scalar(
                    nbias[:, :], rmax[:, :], -1.0, None, ALU.mult))
                tex = P.emit("scalar", lambda s: s.activation(
                    probs[:, :], logit[:, :], AF.Exp, bias=nbias[:, :],
                    accum_out=rsum[:, :]), wait=[("dve", P.count("dve"))])
                P.emit("vector", lambda v: v.reciprocal(rrec[:, :], rsum[:, :]),
                       wait=[tex])
                P.emit("vector", lambda v: v.tensor_scalar(
                    probs[:, :], probs[:, :], rrec[:, :], None, ALU.mult))
                for k in range(3):
                    P.emit("vector", lambda v: v.reduce_max(mk[:, :], probs[:, :],
                                                            axis=AX.X))
                    P.emit("vector", lambda v: v.tensor_scalar(
                        junk[:, :], probs[:, :], mk[:, :], None, ALU.is_equal))
                    P.emit("vector", lambda v: v.tensor_mul(junk[:, :], junk[:, :],
                                                            eim[:, :]))
                    P.emit("vector", lambda v: v.tensor_scalar(
                        junk[:, :], junk[:, :], 4096.0, None, ALU.add, ALU.min,
                        accum_out=idxk[:, :]))
                    P.emit("vector", lambda v: v.tensor_scalar(
                        junk[:, :], ei[:, :], idxk[:, :], None, ALU.is_equal))
                    if k == 0:
                        P.emit("vector", lambda v, tt=tt: v.tensor_scalar(
                            cwg[:, tt, :], junk[:, :], mk[:, :], None, ALU.mult))
                    else:
                        P.emit("vector", lambda v, tt=tt: v.scalar_tensor_tensor(
                            cwg[:, tt, :], junk[:, :], mk[:, :], cwg[:, tt, :],
                            ALU.mult, ALU.add))
                    if k < 2:
                        P.emit("vector", lambda v: v.scalar_tensor_tensor(
                            probs[:, :], junk[:, :], NEGBIG, probs[:, :],
                            ALU.mult, ALU.add))

            hevtok = {0: evtok[0], 1: evtok[1]}
            for gh in range(GT // 512):
                for hc in range(DC):
                    slot = (gh * DC + hc) % 2
                    pm = pmm[slot]
                    for dc in range(DC):
                        tk = P.emit("tensor", lambda t, pm=pm, hc=hc, gh=gh, dc=dc:
                                    t.matmul(
                            pm[:, :], sw1[:, dc, hc * 128:(hc + 1) * 128],
                            xn_g[:, dc, gh * 512:(gh + 1) * 512],
                            start=(dc == 0), stop=(dc == DC - 1)),
                            pe_tok=(dc == DC - 1),
                            wait=[hevtok[slot]] if dc == 0 else [])
                    hevtok[slot] = P.emit("vector", lambda v, pm=pm, hc=hc, gh=gh:
                                          v.tensor_scalar(
                        hs_g[:, hc, gh * 512:(gh + 1) * 512], pm[:, :],
                        sb1s[:, hc:hc + 1], 0.0, ALU.add, ALU.max), wait=[tk])
            for tt in range(GT // 128):
                for ch in range(2):
                    slot = (tt * 2 + ch) % 2
                    pm = pmm[slot]
                    wl = [hevtok[0], hevtok[1], evtok[0], evtok[1]]
                    for hc in range(DC):
                        tk = P.emit("tensor", lambda t, pm=pm, tt=tt, ch=ch, hc=hc:
                                    t.matmul(
                            pm[:, :], hs_g[:, hc, tt * 128:(tt + 1) * 128],
                            sw2[:, hc, ch * 512:(ch + 1) * 512],
                            start=(hc == 0), stop=False),
                            wait=wl if hc == 0 else [])
                    tk = P.emit("tensor", lambda t, pm=pm, ch=ch: t.matmul(
                        pm[:, :], onesr[0:1, 0:128],
                        sb2r[0:1, ch * 512:(ch + 1) * 512],
                        start=False, stop=True), pe_tok=True)
                    hevtok[slot] = P.emit("vector", lambda v, pm=pm, tt=tt, ch=ch:
                                          v.tensor_copy(
                        acc[:, tt, ch * 512:(ch + 1) * 512], pm[:, :]), wait=[tk])

            for e in range(NE):
                es = e % 2
                w1t = P.emit("gpsimd", lambda g, es=es, e=e: g.dma_start(
                    we1[es][:, :, :],
                    W["l3f_eW1"][e * D:(e + 1) * D, :].rearrange(
                        "(c p) h -> p c h", p=128)), dma=f"dma_e{es}",
                    wait=[wuse[es]])
                w2t = P.emit("gpsimd", lambda g, es=es, e=e: g.dma_start(
                    we2[es][:, :, :],
                    W["l3f_eW2"][e * HH:(e + 1) * HH, :].rearrange(
                        "(c p) h -> p c h", p=128)), dma=f"dma_e{es}")
                P.emit("gpsimd", lambda g, es=es, e=e: g.dma_start(
                    eb2c[es][:, :], W["l3f_eb2"][:, e * D:(e + 1) * D]),
                    dma=f"dma_e{es}")
                wldt = ("dma_e" + str(es), P.count(f"dma_e{es}"))
                for gh in range(GT // 512):
                    for hc in range(DC):
                        slot = (gh * DC + hc) % 2
                        pm = pmm[slot]
                        for dc in range(DC):
                            tk = P.emit("tensor", lambda t, pm=pm, hc=hc, gh=gh,
                                        dc=dc, es=es: t.matmul(
                                pm[:, :], we1[es][:, dc, hc * 128:(hc + 1) * 128],
                                xn_g[:, dc, gh * 512:(gh + 1) * 512],
                                start=(dc == 0), stop=(dc == DC - 1)),
                                pe_tok=(dc == DC - 1),
                                wait=[hevtok[slot], wldt] if dc == 0 else [])
                        hevtok[slot] = P.emit(
                            "vector", lambda v, pm=pm, hc=hc, gh=gh, e=e:
                            v.tensor_scalar(
                                he_g[:, hc, gh * 512:(gh + 1) * 512], pm[:, :],
                                eb1s[:, e, hc:hc + 1], 0.0, ALU.add, ALU.max),
                            wait=[tk])
                for tt in range(GT // 128):
                    for ch in range(2):
                        slot = (tt * 2 + ch) % 2
                        pm = pmm[slot]
                        wl = [hevtok[0], hevtok[1]]
                        for hc in range(DC):
                            tk = P.emit("tensor", lambda t, pm=pm, tt=tt, ch=ch,
                                        hc=hc, es=es: t.matmul(
                                pm[:, :], he_g[:, hc, tt * 128:(tt + 1) * 128],
                                we2[es][:, hc, ch * 512:(ch + 1) * 512],
                                start=(hc == 0), stop=False),
                                wait=wl if hc == 0 else [])
                        tk = P.emit("tensor", lambda t, pm=pm, ch=ch, es=es: t.matmul(
                            pm[:, :], onesr[0:1, 0:128],
                            eb2c[es][0:1, ch * 512:(ch + 1) * 512],
                            start=False, stop=True), pe_tok=True)
                        wuse[es] = tk
                        hevtok[slot] = P.emit(
                            "vector", lambda v, pm=pm, tt=tt, ch=ch, e=e:
                            v.scalar_tensor_tensor(
                                acc[:, tt, ch * 512:(ch + 1) * 512], pm[:, :],
                                cwg[:, tt, e:e + 1],
                                acc[:, tt, ch * 512:(ch + 1) * 512],
                                ALU.mult, ALU.add), wait=[tk])

            sttok = {0: None, 1: None}
            for tt in range(GT // 128):
                qslot = tt % 2
                row0 = tok0 + tt * 128
                ldx = P.emit("sync", lambda s, x_=xr[qslot], row0=row0: s.dma_start(
                    x_[:, :], xin[row0:row0 + 128, :]), dma=f"dma_xr{qslot}",
                    wait=[sttok[qslot]])
                at = P.emit("vector", lambda v, x_=xr[qslot], tt=tt: v.tensor_add(
                    x_[:, :], acc[:, tt, :], x_[:, :]), wait=[ldx])
                sttok[qslot] = P.emit("gpsimd", lambda g, o=xr[qslot], row0=row0:
                                      g.dma_start(xout[row0:row0 + 128, :],
                                                  o[:, :]),
                                      dma=f"dma_o{qslot}", wait=[at])
            P.wait_only("gpsimd", [sttok[0], sttok[1]])
            P.barrier()


def _ct(a):
    return np.ascontiguousarray(np.asarray(a))


def prep_l2_attn(ins):
    return {"l2_WqT": _ct(np.asarray(ins["L2_attn_Wq"]).T),
            "l2_WkT": _ct(np.asarray(ins["L2_attn_Wk"]).T),
            "l2_WvT": _ct(np.asarray(ins["L2_attn_Wv"]).T)}


DECL_L2_ATTN = [("l2_WqT", [D, D]), ("l2_WkT", [D, D]), ("l2_WvT", [D, D])]


def prep_l1_attn(ins):
    perm = np.concatenate([np.arange(0, D, 2), np.arange(1, D, 2)])
    inv = 1.0 / (10000.0 ** (np.arange(0, D, 2, dtype=np.float64) / D))
    ang = np.outer(np.arange(T, dtype=np.float64), inv)
    ropecs = np.concatenate([np.cos(ang), np.sin(ang)], axis=1).astype(np.float32)
    return {"l1_wqT": _ct(np.asarray(ins["L1_attn_wq"]).T[:, perm]),
            "l1_wkT": _ct(np.asarray(ins["L1_attn_wk"]).T[:, perm]),
            "l1_wvT": _ct(np.asarray(ins["L1_attn_wv"]).T),
            "l1_woT": _ct(np.asarray(ins["L1_attn_wo"]).T),
            "l1_ropecs": ropecs}


DECL_L1_ATTN = [("l1_wqT", [D, D]), ("l1_wkT", [D, D]), ("l1_wvT", [D, D]),
                ("l1_woT", [D, D]), ("l1_ropecs", [T, D])]


def prep_l0_attn(ins):
    i = np.arange(128)[:, None]
    j = np.arange(128)[None, :]
    causal = np.where(j <= i, 0.0, NEGBIG).astype(np.float32)
    edge = np.where(i < j, 0.0, NEGBIG).astype(np.float32)
    return {"l0_wqkv": _ct(ins["L0_attn_w_qkv_full"]),
            "l0_wqkr": _ct(ins["L0_attn_w_qk_red"]),
            "l0_wvr": _ct(ins["L0_attn_w_v_red"]),
            "l0_wproj": _ct(ins["L0_attn_w_proj"]),
            "l0_masks": np.concatenate([causal, edge], axis=0)}


DECL_L0_ATTN = [("l0_wqkv", [D, 1536]), ("l0_wqkr", [D, 512]),
                ("l0_wvr", [D, 512]), ("l0_wproj", [D, D]),
                ("l0_masks", [256, 128])]

def prep_l3_ffn(ins):
    return {"l3f_sW1": _ct(ins["L3_ffn_sW1"]),
            "l3f_sW2": _ct(ins["L3_ffn_sW2"]),
            "l3f_rW": _ct(ins["L3_ffn_router_W"]),
            "l3f_rb": _ct(np.asarray(ins["L3_ffn_router_b"]).reshape(1, 15)),
            "l3f_eW1": _ct(np.asarray(ins["L3_ffn_eW1"]).reshape(15 * D, D)),
            "l3f_eW2": _ct(np.asarray(ins["L3_ffn_eW2"]).reshape(15 * D, D)),
            "l3f_eb1": _ct(np.asarray(ins["L3_ffn_eb1"]).reshape(15, 8, 128)
                           .transpose(2, 0, 1).reshape(128, 15 * 8)),
            "l3f_eb2": _ct(np.asarray(ins["L3_ffn_eb2"]).reshape(1, 15 * D)),
            "l3f_sb1": _ct(np.asarray(ins["L3_ffn_sb1"]).reshape(8, 128).T),
            "l3f_sb2": _ct(np.asarray(ins["L3_ffn_sb2"]).reshape(1, D))}


DECL_L3_FFN = [("l3f_sW1", [D, D]), ("l3f_sW2", [D, D]), ("l3f_rW", [D, 15]),
               ("l3f_rb", [1, 15]), ("l3f_eW1", [15 * D, D]),
               ("l3f_eW2", [15 * D, D]), ("l3f_eb1", [128, 15 * 8]),
               ("l3f_eb2", [1, 15 * D]), ("l3f_sb1", [128, 8]), ("l3f_sb2", [1, D])]

def prep_l1_ffn(ins):
    weg = np.asarray(ins["L1_ffn_We_gate"]).transpose(0, 2, 1)
    weu = np.asarray(ins["L1_ffn_We_up"]).transpose(0, 2, 1)
    wed = np.asarray(ins["L1_ffn_We_down"]).transpose(0, 2, 1)
    return {"l1f_WgT": _ct(np.asarray(ins["L1_ffn_Wg"]).T),
            "l1f_Wsgc": _ct(np.asarray(ins["L1_ffn_Wsg"]).reshape(8, 128).T),
            "l1f_WsgateT": _ct(np.asarray(ins["L1_ffn_Ws_gate"]).T),
            "l1f_WsupT": _ct(np.asarray(ins["L1_ffn_Ws_up"]).T),
            "l1f_WsdownT": _ct(np.asarray(ins["L1_ffn_Ws_down"]).T),
            "l1f_WegT": _ct(weg.reshape(16 * D, 512)),
            "l1f_WeuT": _ct(weu.reshape(16 * D, 512)),
            "l1f_WedT": _ct(wed.reshape(16 * 512, D))}


DECL_L1_FFN = [("l1f_WgT", [D, 16]), ("l1f_Wsgc", [128, 8]),
               ("l1f_WsgateT", [D, 2048]), ("l1f_WsupT", [D, 2048]),
               ("l1f_WsdownT", [2048, D]), ("l1f_WegT", [16 * D, 512]),
               ("l1f_WeuT", [16 * D, 512]), ("l1f_WedT", [16 * 512, D])]

def prep_l2_ffn(ins):
    bq = np.asarray(ins["L2_ffn_mha_bq"]).reshape(2, D)
    bk = np.asarray(ins["L2_ffn_mha_bk"]).reshape(2, D)
    qkb = np.stack([bq, bk]).reshape(2, 2, 8, 128).transpose(3, 0, 1, 2)
    return {"l2f_g1": _ct(np.asarray(ins["L2_ffn_ln1_g"]).reshape(1, D)),
            "l2f_b1": _ct(np.asarray(ins["L2_ffn_ln1_b"]).reshape(1, D)),
            "l2f_g2": _ct(np.asarray(ins["L2_ffn_ln2_g"]).reshape(1, D)),
            "l2f_b2": _ct(np.asarray(ins["L2_ffn_ln2_b"]).reshape(1, D)),
            "l2f_qkb": _ct(qkb.reshape(128, 32)),
            "l2f_vb": _ct(np.asarray(ins["L2_ffn_mha_bv"]).reshape(1, 2 * D)),
            "l2f_bo": _ct(np.asarray(ins["L2_ffn_mha_bo"]).reshape(1, 2 * D)),
            "l2f_wq": _ct(np.asarray(ins["L2_ffn_mha_wq"]).reshape(2 * D, D)),
            "l2f_wk": _ct(np.asarray(ins["L2_ffn_mha_wk"]).reshape(2 * D, D)),
            "l2f_wv": _ct(np.asarray(ins["L2_ffn_mha_wv"]).reshape(2 * D, D)),
            "l2f_wo": _ct(np.asarray(ins["L2_ffn_mha_wo"]).reshape(2 * D, D)),
            "l2f_rwm": _ct(ins["L2_ffn_mha_rw"]),
            "l2f_rbm": _ct(np.asarray(ins["L2_ffn_mha_rb"]).reshape(1, 2)),
            "l2f_rwf": _ct(ins["L2_ffn_ffn_rw"]),
            "l2f_rbf": _ct(np.asarray(ins["L2_ffn_ffn_rb"]).reshape(1, 8)),
            "l2f_fw1": _ct(np.asarray(ins["L2_ffn_fw1"]).reshape(8 * D, 2048)),
            "l2f_fw2": _ct(np.asarray(ins["L2_ffn_fw2"]).reshape(8 * 2048, D))}


DECL_L2_FFN = [("l2f_g1", [1, D]), ("l2f_b1", [1, D]), ("l2f_g2", [1, D]),
               ("l2f_b2", [1, D]), ("l2f_qkb", [128, 32]), ("l2f_vb", [1, 2 * D]),
               ("l2f_bo", [1, 2 * D]), ("l2f_wq", [2 * D, D]),
               ("l2f_wk", [2 * D, D]), ("l2f_wv", [2 * D, D]),
               ("l2f_wo", [2 * D, D]), ("l2f_rwm", [D, 2]), ("l2f_rbm", [1, 2]),
               ("l2f_rwf", [D, 8]), ("l2f_rbf", [1, 8]),
               ("l2f_fw1", [8 * D, 2048]), ("l2f_fw2", [8 * 2048, D])]

def prep_l0_ffn(ins):
    return {"l0f_wmt": _ct(np.asarray(ins["L0_ffn_Wm"]).T),
            "l0f_wpt": _ct(np.asarray(ins["L0_ffn_Wp"]).T),
            "l0f_wc1t": _ct(np.asarray(ins["L0_ffn_Wc1"]).T),
            "l0f_wc2t": _ct(np.asarray(ins["L0_ffn_Wc2"]).T)}


DECL_L0_FFN = [("l0f_wmt", [D, 12288]), ("l0f_wpt", [12288, D]),
               ("l0f_wc1t", [D, 256]), ("l0f_wc2t", [256, 12288])]

def prep_l3_attn(ins):
    m2 = np.zeros((2, 128), np.float32)
    m2[0, 64:] = NEGBIG
    m2[1, :64] = NEGBIG
    return {"l3a_wqw": _ct(ins["L3_attn_wq_w"]),
            "l3a_wqb": _ct(np.asarray(ins["L3_attn_wq_b"]).reshape(1, D)),
            "l3a_w2w": _ct(np.asarray(ins["L3_attn_w2_w"]).reshape(1, D)),
            "l3a_w2b": _ct(np.asarray(ins["L3_attn_w2_b"]).reshape(1, 1)),
            "l3a_lng": _ct(np.asarray(ins["L3_attn_ln_g"]).reshape(1, D)),
            "l3a_lnb": _ct(np.asarray(ins["L3_attn_ln_b"]).reshape(1, D)),
            "l3a_mask2": m2}


DECL_L3_ATTN = [("l3a_wqw", [D, D]), ("l3a_wqb", [1, D]), ("l3a_w2w", [1, D]),
                ("l3a_w2b", [1, 1]), ("l3a_lng", [1, D]), ("l3a_lnb", [1, D]),
                ("l3a_mask2", [2, 128])]

PHASES = {
    "l3_attn": (emit_l3_attn, DECL_L3_ATTN, prep_l3_attn),
    "l0_ffn": (emit_l0_ffn, DECL_L0_FFN, prep_l0_ffn),
    "l2_ffn": (emit_l2_ffn, DECL_L2_FFN, prep_l2_ffn),
    "l2_attn": (emit_l2_attn, DECL_L2_ATTN, prep_l2_attn),
    "l1_attn": (emit_l1_attn, DECL_L1_ATTN, prep_l1_attn),
    "l0_attn": (emit_l0_attn, DECL_L0_ATTN, prep_l0_attn),
    "l3_ffn": (emit_l3_ffn, DECL_L3_FFN, prep_l3_ffn),
    "l1_ffn": (emit_l1_ffn, DECL_L1_FFN, prep_l1_ffn),
}

ALL_PHASES = ["l0_attn", "l0_ffn", "l1_attn", "l1_ffn",
              "l2_attn", "l2_ffn", "l3_attn", "l3_ffn"]


def build_program(phase_names):
    nc = bass.Bass("TRN2", target_bir_lowering=False)
    xin = nc.dram_tensor("xin", [NTOK, D], F32, kind="ExternalInput").ap()
    xout = nc.dram_tensor("xout", [NTOK, D], F32, kind="ExternalOutput").ap()
    n = len(phase_names)
    bufs = []
    for i in range(n - 1):
        bufs.append(nc.dram_tensor(f"xmid{i}", [NTOK, D], F32, kind="Internal").ap())
    W = {}
    for pn in phase_names:
        for key, shape in PHASES[pn][1]:
            W[key] = nc.dram_tensor(key, shape, F32, kind="ExternalInput").ap()
    with contextlib.ExitStack() as stk:
        P = Prog(nc, stk)
        P.setup_consts([1e-5, 1e-6])
        for i, pn in enumerate(phase_names):
            src = xin if i == 0 else bufs[i - 1]
            dst = xout if i == n - 1 else bufs[i]
            PHASES[pn][0](P, src, dst, W)
            P.barrier()
    return nc


def run_phases(inputs, phase_names):
    nc = build_program(phase_names)
    wshared = {}
    for pn in phase_names:
        wshared.update(PHASES[pn][2](inputs))
    in_maps = []
    for c in range(NCORES):
        b0 = c * B_CORE
        d = {"xin": _ct(np.asarray(inputs["L0_attn_x"])[b0:b0 + B_CORE]
                        .reshape(NTOK, D))}
        d.update(wshared)
        in_maps.append(d)
    res = run_bass_kernel_spmd(nc, in_maps, core_ids=list(range(NCORES)))
    out = np.empty((16, T, D), np.float32)
    for c in range(NCORES):
        out[c * B_CORE:(c + 1) * B_CORE] = res.results[c]["xout"].reshape(B_CORE, T, D)
    return out


KERNEL_PHASES = ["l0_attn", "l0_ffn", "l1_attn", "l1_ffn",
                 "l2_attn", "l2_ffn", "l3_attn", "l3_ffn"]


def kernel(**inputs):
    return run_phases(inputs, KERNEL_PHASES)
